# version 24
# baseline (speedup 1.0000x reference)
import numpy as np
import ml_dtypes
from contextlib import ExitStack

import concourse.bass as bass
import concourse.mybir as mybir
from concourse.bass_utils import run_bass_kernel_spmd

F32 = mybir.dt.float32
BF16 = mybir.dt.bfloat16
FP8 = mybir.dt.float8e4
AX = mybir.AxisListType.X
OP = mybir.AluOpType
AF = mybir.ActivationFunctionType
EPS = 1e-5
C = 1024
CONST_VALS = (EPS, EPS * EPS, 1.0 + EPS, 1.0, -1.0, 0.5, -0.5, 1.0 / 1024.0,
              448.0, 4.0, 1e-12, 0.0, 2.0, -2.0, 128.0, 1024.0)
NC8 = 8
H = 4096
NHC = 32


class PB:

    def __init__(self, nc):
        self.nc = nc
        self.plan = []
        self.cnt = {}
        self.lastw = {}
        self.sems = {}
        self.dmatok = {}

    def mksem(self, ctx, *names):
        for n in names:
            self.sems[n] = ctx.enter_context(self.nc.semaphore(n))
            self.cnt[n] = 0

    def w(self, eng, sem, val):
        if not val or val <= 0:
            return
        key = (eng, sem)
        if self.lastw.get(key, 0) >= val:
            return
        self.lastw[key] = val
        h = self.sems[sem]
        self.plan.append((eng, lambda E, h=h, v=val: E.wait_ge(h, v)))

    def op(self, eng, fn, inc=None, n=1):
        if inc is None:
            self.plan.append((eng, fn))
            return None
        self.cnt[inc] += n
        h = self.sems[inc]
        self.plan.append((eng, lambda E, f=fn, h=h, m=n: f(E).then_inc(h, m)))
        val = self.cnt[inc]
        if inc in ("yd", "gd"):
            toks = self.dmatok.setdefault((eng, inc), [])
            toks.append(val)
            tgt = toks[-1]
            self.plan.append(
                (eng, lambda E, h=h, v=tgt: E.wait_ge(h, v)))
            key = (eng, inc)
            if self.lastw.get(key, 0) < tgt:
                self.lastw[key] = tgt
        return val

    def emit_block(self):
        plan = self.plan
        self.plan = []
        with self.nc.Block() as blk:
            blk.tensor(lambda E: [f(E) for e, f in plan if e == 't'])
            blk.vector(lambda E: [f(E) for e, f in plan if e == 'v'])
            blk.scalar(lambda E: [f(E) for e, f in plan if e == 's'])
            blk.gpsimd(lambda E: [f(E) for e, f in plan if e == 'g'])
            blk.sync(lambda E: [f(E) for e, f in plan if e == 'y'])


def build(nt=16, dbg=False):
    nc = bass.Bass(target_bir_lowering=False, detect_race_conditions=False)
    T = nt * 128
    SL1 = 2
    SL2 = min(4, nt)
    NS1 = nt // SL1
    NS2 = nt // SL2

    X = nc.dram_tensor("X", [T, C], F32, kind="ExternalInput")
    WKT = nc.dram_tensor("WKT", [C, C], F32, kind="ExternalInput")
    WVT = nc.dram_tensor("WVT", [C, C], F32, kind="ExternalInput")
    WRT = nc.dram_tensor("WRT", [C, C], F32, kind="ExternalInput")
    WCKR = nc.dram_tensor("WCKR", [NHC, 128, NC8 * 128], F32, kind="ExternalInput")
    WCVT = nc.dram_tensor("WCVT", [H, C], F32, kind="ExternalInput")
    WCRT = nc.dram_tensor("WCRT", [C, C], F32, kind="ExternalInput")
    WFT = nc.dram_tensor("WFT", [C, C], F32, kind="ExternalInput")
    TSV = nc.dram_tensor("TSV", [C], F32, kind="ExternalInput")
    TFV = nc.dram_tensor("TFV", [C], F32, kind="ExternalInput")
    L1W = nc.dram_tensor("L1W", [C], F32, kind="ExternalInput")
    L1B = nc.dram_tensor("L1B", [C], F32, kind="ExternalInput")
    L2W = nc.dram_tensor("L2W", [C], F32, kind="ExternalInput")
    L2B = nc.dram_tensor("L2B", [C], F32, kind="ExternalInput")
    FBV = nc.dram_tensor("FBV", [C], F32, kind="ExternalInput")
    CONSTD = nc.dram_tensor("CONSTD", [16], F32, kind="ExternalInput")
    TRID = nc.dram_tensor("TRID", [128, 128], F32, kind="ExternalInput")
    IDD = nc.dram_tensor("IDD", [128, 128], F32, kind="ExternalInput")
    ONED = nc.dram_tensor("ONED", [1, 128], F32, kind="ExternalInput")
    OUT = nc.dram_tensor("OUT", [T, C], F32, kind="ExternalOutput")
    eud = nc.dram_tensor("eud", [1, C], F32)
    WCK8D = nc.dram_tensor("WCK8D", [NHC, 128, NC8 * 128], BF16)
    WCV8D = nc.dram_tensor("WCV8D", [H, C], BF16)
    scald = nc.dram_tensor("scald", [1, 24], F32)
    cc_in = nc.dram_tensor("cc_in", [1, 8], F32)
    cc_out = nc.dram_tensor("cc_out", [1, 8], F32)
    if dbg:
        DXI = nc.dram_tensor("DXI", [T, C], F32, kind="ExternalOutput")
        DX1 = nc.dram_tensor("DX1", [T, C], F32, kind="ExternalOutput")
        DHF = nc.dram_tensor("DHF", [T, C], F32, kind="ExternalOutput")

    b = PB(nc)
    with ExitStack() as outer:
        e = outer.enter_context
        b.mksem(outer, "tk", "vk", "sk", "gk", "yd", "gd", "cc", "yo", "pp",
                "wk0", "wk1", "wk2", "wk3", "wk4", "wk5", "wv0", "wv1", "wv2", "wv3", "wv4", "wv5", "xb0", "xb1", "xg0", "xg1",
                "ld0", "ld1", "ld2")

        S2 = e(nc.sbuf_tensor("S2", [128, nt * C], F32))
        XB = e(nc.sbuf_tensor("XB", [128, 2, C], F32))
        SCRA = e(nc.sbuf_tensor("SCRA", [128, 3072], BF16))
        SUM = e(nc.sbuf_tensor("SUM", [128, nt], F32))
        SSQ = e(nc.sbuf_tensor("SSQ", [128, nt], F32))
        MXT = SUM
        MNT = SSQ
        XS2 = e(nc.sbuf_tensor("XS2", [128, 2 * nt], F32))
        XSQ2 = e(nc.sbuf_tensor("XSQ2", [128, 2 * nt], F32))
        LSUM = e(nc.sbuf_tensor("LSUM", [128, nt], F32))
        LSQ = e(nc.sbuf_tensor("LSQ", [128, nt], F32))
        STA = e(nc.sbuf_tensor("STA", [128, 6, nt], F32))
        RS0 = e(nc.sbuf_tensor("RS0", [128, nt], F32))
        B0N = e(nc.sbuf_tensor("B0N", [128, nt], F32))
        RSC = e(nc.sbuf_tensor("RSC", [128, nt], F32))
        BCN = e(nc.sbuf_tensor("BCN", [128, nt], F32))
        RS2 = e(nc.sbuf_tensor("RS2", [128, nt], F32))
        B2N = e(nc.sbuf_tensor("B2N", [128, nt], F32))
        RSL = e(nc.sbuf_tensor("RSL", [128, nt], F32))
        BLN = e(nc.sbuf_tensor("BLN", [128, nt], F32))
        TRIB = e(nc.sbuf_tensor("TRIB", [128, 128], BF16))
        IDB = e(nc.sbuf_tensor("IDB", [128, 128], BF16))
        ONEB = e(nc.sbuf_tensor("ONEB", [1, 128], BF16))
        ONEF = e(nc.sbuf_tensor("ONEF", [1, 128], F32))
        TSC = e(nc.sbuf_tensor("TSC", [128, 8], F32))
        OPTS = e(nc.sbuf_tensor("OPTS", [128, 8], F32))
        NTS = e(nc.sbuf_tensor("NTS", [128, 8], F32))
        OMTS = e(nc.sbuf_tensor("OMTS", [128, 8], F32))
        TSB = e(nc.sbuf_tensor("TSB", [128, 8], BF16))
        W1C = e(nc.sbuf_tensor("W1C", [128, 8], F32))
        B1C = e(nc.sbuf_tensor("B1C", [128, 8], F32))
        W2C = e(nc.sbuf_tensor("W2C", [128, 8], F32))
        B2C = e(nc.sbuf_tensor("B2C", [128, 8], F32))
        CK = e(nc.sbuf_tensor("CK", [1, C], F32))
        CA = e(nc.sbuf_tensor("CA", [1, C], F32))
        CZ = e(nc.sbuf_tensor("CZ", [1, C], F32))
        PBB = e(nc.sbuf_tensor("PBB", [128, 8], BF16))
        PB2 = e(nc.sbuf_tensor("PB2", [128, 8], BF16))
        ROLL = e(nc.sbuf_tensor("ROLL", [128, 8], BF16))
        AMX = e(nc.sbuf_tensor("AMX", [128, nt], F32))
        AROW = e(nc.sbuf_tensor("AROW", [1, 128], F32))
        SCL = e(nc.sbuf_tensor("SCL", [1, 24], F32))
        AMSB = e(nc.sbuf_tensor("AMSB", [1, 8], F32))
        GMX = e(nc.sbuf_tensor("GMX", [1, 8], F32))
        CB = e(nc.sbuf_tensor("CB", [128, 16], F32))
        for _i, _v in enumerate(CONST_VALS):
            nc.const_aps.register(F32, float(_v), CB[:, _i:_i + 1])

        def s2t(j):
            return S2[:, j * C:(j + 1) * C]

        def s2th(j, h):
            return S2[:, j * C + h * 512: j * C + (h + 1) * 512]

        with ExitStack() as ph1:
            e1 = ph1.enter_context
            W3 = e1(nc.sbuf_tensor("W3", [128, 3, NC8, C], BF16))
            HT = e1(nc.sbuf_tensor("HT", [128, NC8, SL1 * 128], BF16))
            XXT = e1(nc.sbuf_tensor("XXT", [128, 2, NC8, SL1 * 128], BF16))
            EKB = e1(nc.sbuf_tensor("EKB", [128, C], BF16))
            EKV = e1(nc.sbuf_tensor("EKV", [128, C], BF16))
            E1B = e1(nc.sbuf_tensor("E1B", [128, C], BF16))
            EUM1 = e1(nc.sbuf_tensor("EUM1", [128, C], BF16))
            T1F = e1(nc.sbuf_tensor("T1F", [128, NC8, SL1 * 128], F32))
            FP2 = e1(nc.sbuf_tensor("FP2", [128, 2, 2, 512], F32))
            BP2 = e1(nc.sbuf_tensor("BP2", [128, 2, 2, 512], BF16))
            BDEN = e1(nc.sbuf_tensor("BDEN", [128, 512], BF16))
            BIG = e1(nc.sbuf_tensor("BIG", [128, 512], BF16))
            F0P = e1(nc.sbuf_tensor("F0P", [128, 3, 512], F32))
            F3P = e1(nc.sbuf_tensor("F3P", [128, 3, 512], F32))
            B3P = e1(nc.sbuf_tensor("B3P", [128, 3, 4, 512], BF16))
            PKV = e1(nc.psum_tensor("PKV", [128, 2, 512], F32))
            PCS = e1(nc.psum_tensor("PCS", [65, 512], F32))
            PS = e1(nc.psum_tensor("PS", [128, 512], F32))
            PA = e1(nc.psum_tensor("PA", [128, 512], F32))
            PZ = e1(nc.psum_tensor("PZ", [128, 512], F32))
            PT = e1(nc.psum_tensor("PT", [128, C], F32))

            def pools(it):
                par = it % 2
                t3 = it % 3
                F0 = F0P[:, t3, :]
                F1 = FP2[:, par, 0, :]
                F2 = FP2[:, par, 1, :]
                F3 = F3P[:, t3, :]
                B_te = B3P[:, t3, 0, :]
                B_a = B3P[:, t3, 1, :]
                B_dent = B3P[:, t3, 2, :]
                B_t2 = B3P[:, t3, 3, :]
                B_den = BDEN[:, :]
                B_ig = BIG[:, :]
                B_z = BP2[:, par, 0, :]
                B_g = BP2[:, par, 1, :]
                return (F0, F1, F2, F3), (B_te, B_den, B_a, B_t2,
                                          B_dent, B_ig, B_z, B_g)
            UROW = FP2[0:1, 0, 0:2, :].rearrange("p a b -> p (a b)")
            EROW = FP2[0:1, 1, 0:2, :].rearrange("p a b -> p (a b)")

            b.op('g', lambda E: E.dma_start(
                CB[:, :], bass.AP(CONSTD, 0, [[0, 128], [1, 16]])), 'gd', 16)
            wap = lambda t: bass.AP(t, 0, [[C, 128], [128 * C, NC8], [1, C]])
            LW0 = b.op('g', lambda E: E.dma_start(W3[:, 0, :, :], wap(WKT)), 'ld0', 16)
            LW1 = b.op('g', lambda E: E.dma_start(W3[:, 1, :, :], wap(WVT)), 'ld1', 16)
            LW2 = b.op('g', lambda E: E.dma_start(W3[:, 2, :, :], wap(WRT)), 'ld2', 16)
            b.op('g', lambda E: E.dma_start(TRIB[:, :], TRID.ap()), 'gd', 16)
            b.op('g', lambda E: E.dma_start(IDB[:, :], IDD.ap()), 'gd', 16)
            b.op('g', lambda E: E.dma_start(ONEB[:, :], ONED.ap()), 'gd', 16)
            b.op('g', lambda E: E.dma_start(ONEF[:, :], ONED.ap()), 'gd', 16)

            def veccol(dst, src):
                return lambda E, d=dst, s=src: E.dma_start(
                    d[:, :], bass.AP(s, 0, [[1, 128], [128, 8]]),
                    allow_slow_non_contiguous=True)
            b.op('g', veccol(TSC, TSV), 'gd', 16)
            b.op('g', veccol(W1C, L1W), 'gd', 16)
            b.op('g', veccol(B1C, L1B), 'gd', 16)
            b.op('g', veccol(W2C, L2W), 'gd', 16)
            b.op('g', veccol(B2C, L2B), 'gd', 16)
            b.op('g', lambda E: E.dma_start(
                UROW, bass.AP(TFV, 0, [[C, 1], [1, C]])), 'gd', 16)
            LD_ALL = b.cnt['gd']
            b.op('g', lambda E: E.memset(CK[:, :], 1.0), 'gk')
            b.op('g', lambda E: E.memset(CA[:, :], 0.0), 'gk')
            b.op('g', lambda E: E.memset(SCL[:, :], 0.0), 'gk')
            G_init = b.op('g', lambda E: E.memset(CZ[:, :], 0.0), 'gk')

            b.w('s', 'gd', LD_ALL)
            b.op('s', lambda E: E.activation(OPTS[:, :], TSC[:, :], AF.Copy, bias=1.0), 'sk')
            b.op('s', lambda E: E.activation(NTS[:, :], TSC[:, :], AF.Copy, scale=-1.0), 'sk')
            b.op('s', lambda E: E.activation(OMTS[:, :], TSC[:, :], AF.Copy, scale=-1.0, bias=1.0), 'sk')
            b.op('s', lambda E: E.activation(TSB[:, :], TSC[:, :], AF.Copy), 'sk')
            S_eu = b.op('s', lambda E: E.activation(EROW, UROW, AF.Exp), 'sk')
            b.w('v', 'sk', S_eu)
            V_er = b.op('v', lambda E: E.tensor_scalar(
                EROW, EROW, -1.0, None, OP.add), 'vk')
            b.w('y', 'vk', V_er)
            Y_eu = b.op('y', lambda E: E.dma_start(eud.ap(), EROW), 'yd', 16)
            b.w('g', 'yd', Y_eu)
            G_eum = b.op('g', lambda E: E.dma_start(
                EUM1[:, :], bass.AP(eud, 0, [[0, 128], [1, C]])), 'gd', 16)
            pp_chunks = []
            for src, dst, nel in ((WCKR, WCK8D, NHC * 128 * NC8 * 128),
                                  (WCVT, WCV8D, H * C)):
                for q in range(8):
                    ck = nel // 8
                    pp_chunks.append(lambda src=src, dst=dst, off=q * ck,
                                     w=ck // 128: b.op(
                        'g', lambda E, src=src, dst=dst, off=off, w=w:
                        E.dma_start(
                            bass.AP(dst, off, [[w, 128], [1, w]]),
                            bass.AP(src, off, [[w, 128], [1, w]])),
                        'pp', 16))
            PP_K = 8 * 16
            PP_ALL = 16 * 16

            V_st = {}
            S_st = {}
            for j in range(nt):
                xs_ = 'xb%d' % (j % 2)
                Yx0 = b.op('y', lambda E, j=j: E.dma_start(
                    s2t(j), X[j * 128:(j + 1) * 128, :]), xs_, 16)
                b.w('v', xs_, Yx0)
                V_st[j] = b.op('v', lambda E, j=j: E.reduce_sum(
                    SUM[:, j:j + 1], s2t(j), AX), 'vk')
                b.w('s', xs_, Yx0)
                S_st[j] = b.op('s', lambda E, j=j: E.activation(
                    SCRA[:, 2048:3072],
                    s2t(j), AF.Square,
                    accum_out=SSQ[:, j:j + 1]), 'sk')

            def rs_chain(sumt, sqt, rs_outs, bias_outs, double_ln):
                M0c, T1c, M2c, V0c, L0c, LCc = (STA[:, i, :] for i in range(6))
                b.op('s', lambda E: E.activation(M0c, sumt[:, :], AF.Copy, scale=1.0 / C), 'sk')
                b.op('s', lambda E: E.activation(T1c, sqt[:, :], AF.Copy, scale=1.0 / C), 'sk')
                S_m = b.op('s', lambda E: E.activation(M2c, M0c, AF.Square), 'sk')
                b.w('v', 'sk', S_m)
                V_v0 = b.op('v', lambda E: E.tensor_sub(V0c, T1c, M2c), 'vk')
                b.w('s', 'vk', V_v0)
                b.op('s', lambda E: E.activation(L0c, V0c, AF.Ln, bias=EPS), 'sk')
                S_l = b.op('s', lambda E: E.activation(
                    rs_outs[0][:, :], L0c, AF.Exp, scale=-0.5), 'sk')
                if double_ln:
                    b.op('s', lambda E: E.activation(
                        LCc, V0c, AF.Ln, scale=1.0 + EPS, bias=EPS * EPS), 'sk')
                    S_l = b.op('s', lambda E: E.activation(
                        rs_outs[1][:, :], LCc, AF.Exp, scale=-0.5), 'sk')
                b.w('v', 'sk', S_l)
                tok = None
                for rs_t, bi_t in zip(rs_outs, bias_outs):
                    tok = b.op('v', lambda E, r=rs_t, o=bi_t: E.scalar_tensor_tensor(
                        o[:, :], M0c, -1.0, r[:, :], OP.mult, OP.mult), 'vk')
                return tok

            b.w('s', 'vk', V_st[nt - 1])
            b.w('s', 'sk', S_st[nt - 1])
            V_chain = rs_chain(SUM, SSQ, [RS0, RSC], [B0N, BCN], True)

            T_ttr = {}
            S_hev = {}
            V_xxx = {}
            V_done = {}
            S_done = {}
            hist = []
            T_r_lastslice = {}

            pend1 = [None]
            pend2 = [None]

            W = SL1 * 128

            def emit_xxt(s1, chunk):
                par = s1 % 2
                tiles = [s1 * SL1 + i for i in range(SL1)]
                if chunk in (0, 1):
                    j = tiles[chunk]
                    b.w('s', 'vk', V_chain)
                    b.w('s', 'tk', T_ttr.get(j - 2, 0))
                    S_done[j] = b.op('s', lambda E, j=j: E.activation(
                        SCRA[:, (j % 2) * 1024:(j % 2 + 1) * 1024],
                        s2t(j), AF.Identity,
                        scale=RSC[:, j:j + 1], bias=BCN[:, j:j + 1]), 'sk')
                if chunk in (1, 2):
                    j = tiles[chunk - 1]
                    b.w('t', 'sk', S_done[j])
                    b.w('t', 'sk', S_hev.get(j - 1, 0))
                    for i in range(NC8):
                        T_ttr[j] = b.op('t', lambda E, j=j, i=i: E.matmul(
                            PT[:, i * 128:(i + 1) * 128],
                            SCRA[:, (j % 2) * 1024 + i * 128:(j % 2) * 1024 + (i + 1) * 128],
                            IDB[:, :], start=True, stop=True),
                            'tk' if i == NC8 - 1 else None)
                    b.w('s', 'tk', T_ttr[j])
                    b.w('s', 'vk', V_xxx.get(s1 - 1, 0))
                    for i in range(NC8):
                        S_hev[j] = b.op('s', lambda E, j=j, i=i: E.activation(
                            HT[:, i, (j % SL1) * 128:(j % SL1 + 1) * 128],
                            PT[:, i * 128:(i + 1) * 128], AF.Identity,
                            scale=W1C[:, i:i + 1], bias=B1C[:, i:i + 1]),
                            'sk' if i == NC8 - 1 else None)
                if chunk != 3:
                    return
                S_t1 = None
                for i in range(NC8):
                    S_t1 = b.op('s', lambda E, i=i: E.activation(
                        T1F[:, i, :],
                        HT[:, i, :], AF.Copy, scale=OPTS[:, i:i + 1]),
                        'sk' if i == NC8 - 1 else None)
                b.w('v', 'sk', S_t1)
                b.w('v', 'tk', T_r_lastslice.get(s1 - 2, 0))
                pbsrc = TSB if s1 == 0 else PBB
                for i in range(NC8):
                    b.op('v', lambda E, i=i, par=par: E.scalar_tensor_tensor(
                        XXT[:, par, i, 1:W], HT[:, i, 0:W - 1], NTS[:, i:i + 1],
                        T1F[:, i, 1:W], OP.mult, OP.add), None)
                    V_xxx[s1] = b.op('v', lambda E, i=i, par=par, pbsrc=pbsrc:
                        E.scalar_tensor_tensor(
                            XXT[:, par, i, 0:1], pbsrc[:, i:i + 1], NTS[:, i:i + 1],
                            T1F[:, i, 0:1], OP.mult, OP.add),
                        'vk' if i == NC8 - 1 else None)
                b.w('s', 'vk', V_xxx[s1])
                for i in range(NC8):
                    b.op('s', lambda E, i=i: E.activation(
                        PBB[:, i:i + 1], HT[:, i, W - 1:W], AF.Copy),
                        'sk' if i == NC8 - 1 else None)

            for ch in range(4):
                emit_xxt(0, ch)
            for s1 in range(NS1):
                par = s1 % 2
                tiles = [s1 * SL1 + i for i in range(SL1)]
                for j in tiles:
                    for h in (0, 1):
                        hs = slice(h * 512, (h + 1) * 512)
                        xoff = (j % SL1) * 128
                        it = 2 * j + h
                        (F0, F1, F2, F3), (B_te, B_den, B_a, B_t2, B_dent,
                                           B_ig, B_z, B_g) = pools(it)
                        p1 = hist[-1] if len(hist) >= 1 else {}
                        p2 = hist[-2] if len(hist) >= 2 else {}
                        p3 = hist[-3] if len(hist) >= 3 else {}
                        cur = {}
                        b.w('t', 'sk', p1.get('e1', 0))
                        b.w('t', 'vk', V_xxx[s1])
                        b.w('t', 'gd', LD_ALL)
                        b.w('t', 'ld0', LW0)
                        b.w('t', 'ld1', LW1)
                        b.w('t', 'ld2', LW2)
                        b.w('t', 'gk', G_init)
                        for ci in range(NC8):
                            T_k = b.op('t', lambda E, ci=ci, par=par, xoff=xoff, h=h:
                                E.matmul(PKV[:, 0, :], XXT[:, par, ci, xoff:xoff + 128],
                                         W3[:, 0, ci, h * 512:(h + 1) * 512],
                                         start=(ci == 0), stop=(ci == NC8 - 1)),
                                'tk' if ci == NC8 - 1 else None)
                        b.w('t', 'vk', p1.get('ekv', 0))
                        for ci in range(NC8):
                            T_v = b.op('t', lambda E, ci=ci, par=par, xoff=xoff, h=h:
                                E.matmul(PKV[:, 1, :], XXT[:, par, ci, xoff:xoff + 128],
                                         W3[:, 1, ci, h * 512:(h + 1) * 512],
                                         start=(ci == 0), stop=(ci == NC8 - 1)),
                                'tk' if ci == NC8 - 1 else None)
                        b.w('s', 'tk', T_k)
                        b.w('s', 'gd', G_eum)
                        b.w('s', 'vk', p2.get('dent', 0))
                        S_ek = b.op('s', lambda E, hs=hs: E.activation(
                            EKB[:, hs], PKV[:, 0, :], AF.Exp), 'sk')
                        cur['ek'] = S_ek
                        b.w('t', 'sk', S_ek)
                        for ci in range(NC8):
                            T_r = b.op('t', lambda E, ci=ci, par=par, xoff=xoff, h=h:
                                E.matmul(PKV[:, 0, :], XXT[:, par, ci, xoff:xoff + 128],
                                         W3[:, 2, ci, h * 512:(h + 1) * 512],
                                         start=(ci == 0), stop=(ci == NC8 - 1)),
                                'tk' if ci == NC8 - 1 else None)
                        b.w('s', 'tk', T_r)
                        S_e1 = b.op('s', lambda E, hs=hs: E.activation(
                            E1B[:, hs], PKV[:, 0, :], AF.Exp, scale=-1.0), 'sk')
                        cur['e1'] = S_e1
                        b.w('t', 'vk', p1.get('dd', 0))
                        b.w('t', 'vk', p2.get('ck', 0))
                        b.op('t', lambda E, hs=hs: E.matmul(
                            PS[:, :], TRIB[:, :], EKB[:, hs],
                            start=True, stop=False), None)
                        T_s = b.op('t', lambda E, h=h: E.matmul(
                            PS[:, :], ONEF[:, :], CK[0:1, h * 512:(h + 1) * 512],
                            start=False, stop=True), 'tk')
                        b.w('t', 'vk', p1.get('ck', 0))
                        T_bk = b.op('t', lambda E, hs=hs: E.matmul(
                            PCS[0:1, :], TRIB[:, 127:128], EKB[:, hs],
                            start=True, stop=True), 'tk')
                        b.w('v', 'tk', T_s)
                        b.w('v', 'gk', p3.get('xin', 0))
                        S_dd = b.op('v', lambda E, F0=F0: E.tensor_scalar(
                            F0, PS[:, :], 1.0, None, OP.mult), 'vk')
                        cur['dd'] = S_dd
                        b.w('v', 'tk', T_v)
                        b.w('v', 'sk', S_ek)
                        b.w('v', 'gk', p3.get('xin', 0))
                        V_ekv = b.op('v', lambda E, hs=hs: E.tensor_mul(
                            EKV[:, hs], EKB[:, hs], PKV[:, 1, :]), 'vk')
                        cur['ekv'] = V_ekv
                        b.op('v', lambda E, hs=hs, F0=F0, F1=F1: E.scalar_tensor_tensor(
                            F1, EKB[:, hs], -1.0, F0, OP.mult, OP.add), None)
                        b.op('v', lambda E, hs=hs, B_te=B_te: E.tensor_mul(
                            B_te, EKB[:, hs], EUM1[:, hs]), None)
                        b.w('v', 'tk', T_bk)
                        V_ck = b.op('v', lambda E, h=h: E.tensor_add(
                            CK[0:1, h * 512:(h + 1) * 512],
                            CK[0:1, h * 512:(h + 1) * 512], PCS[0:1, :]), 'vk')
                        cur['ck'] = V_ck
                        b.op('v', lambda E, hs=hs, F1=F1, F2=F2: E.tensor_max(
                            F2, F1, EKB[:, hs]), None)
                        V_den = b.op('v', lambda E, F0=F0, B_te=B_te, B_den=B_den:
                                     E.tensor_add(B_den, F0, B_te), 'vk')
                        b.w('s', 'vk', V_den)
                        b.op('s', lambda E, F2=F2, F3=F3: E.activation(F3, F2, AF.Ln), None)
                        S_ln = b.op('s', lambda E, F0=F0, F1=F1: E.activation(F1, F0, AF.Ln), 'sk')
                        b.w('v', 'sk', S_ln)
                        b.w('v', 'sk', S_e1)
                        V_a = b.op('v', lambda E, F1=F1, F3=F3, B_a=B_a:
                                   E.tensor_sub(B_a, F1, F3), 'vk')
                        cur['a'] = V_a
                        b.op('v', lambda E, hs=hs, B_den=B_den, B_dent=B_dent, E1B=E1B:
                             E.scalar_tensor_tensor(
                                 B_dent, E1B[:, hs], 1.0, B_den, OP.add, OP.mult), None)
                        V_pre = b.op('v', lambda E, j=j, h=h, hs=hs, F3=F3: E.tensor_scalar(
                            F3, s2th(j, h),
                            RS0[:, j:j + 1], B0N[:, j:j + 1], OP.mult, OP.add), 'vk')
                        cur['dent'] = V_pre
                        if h == 1:
                            V_done[j] = V_pre

                        def stage_b1(j=j, h=h, hs=hs, F0=F0, F1=F1, F2=F2, F3=F3,
                                     B_te=B_te, B_den=B_den, B_a=B_a, B_t2=B_t2,
                                     B_dent=B_dent, B_ig=B_ig, B_z=B_z, B_g=B_g,
                                     cur=cur, p1=p1, p2=p2):
                            b.w('t', 'vk', cur['a'])
                            b.w('t', 'vk', p1.get('lng', 0))
                            b.w('t', 'vk', p2.get('ca', 0))
                            b.op('t', lambda E: E.matmul(
                                PA[:, :], TRIB[:, :], B_a, start=True, stop=False), None)
                            T_a = b.op('t', lambda E, h=h: E.matmul(
                                PA[:, :], ONEF[:, :], CA[0:1, h * 512:(h + 1) * 512],
                                start=False, stop=True), 'tk')
                            b.w('t', 'vk', p1.get('ca', 0))
                            T_ba = b.op('t', lambda E: E.matmul(
                                PCS[32:33, :], TRIB[:, 127:128], B_a,
                                start=True, stop=True), 'tk')
                            b.w('v', 'tk', T_a)
                            V_lng = b.op('v', lambda E: E.scalar_tensor_tensor(
                                F2, B_a, -1.0, PA[:, :], OP.mult, OP.add), 'vk')
                            cur['lng'] = V_lng
                            b.w('v', 'tk', T_ba)
                            V_ca = b.op('v', lambda E, h=h: E.tensor_add(
                                CA[0:1, h * 512:(h + 1) * 512],
                                CA[0:1, h * 512:(h + 1) * 512], PCS[32:33, :]), 'vk')
                            cur['ca'] = V_ca
                            b.w('s', 'vk', V_lng)
                            b.op('s', lambda E: E.activation(B_g, F2, AF.Exp), None)
                            b.op('s', lambda E: E.activation(B_ig, F2, AF.Exp, scale=-1.0), None)
                            b.op('s', lambda E: E.activation(F1, B_dent, AF.Ln), None)
                            S_g = b.op('s', lambda E: E.activation(
                                B_t2, F1, AF.Exp, scale=-1.0), 'sk')
                            b.w('v', 'sk', S_g)
                            V_tn = b.op('v', lambda E, hs=hs: E.tensor_mul(
                                B_te, EKV[:, hs], EUM1[:, hs]), 'vk')
                            b.w('v', 'tk', p2.get('tz', 0))
                            V_z = b.op('v', lambda E, hs=hs: E.tensor_mul(
                                B_z, EKV[:, hs], B_ig), 'vk')
                            cur['z'] = V_z

                            def stage_b2():
                                b.w('t', 'vk', V_z)
                                b.w('t', 'vk', p1.get('tg', 0))
                                b.w('t', 'vk', p2.get('cz', 0))
                                b.op('t', lambda E: E.matmul(
                                    PZ[:, :], TRIB[:, :], B_z, start=True, stop=False), None)
                                T_z = b.op('t', lambda E, h=h: E.matmul(
                                    PZ[:, :], ONEF[:, :], CZ[0:1, h * 512:(h + 1) * 512],
                                    start=False, stop=True), 'tk')
                                b.w('t', 'vk', p1.get('cz', 0))
                                T_bz = b.op('t', lambda E: E.matmul(
                                    PCS[64:65, :], TRIB[:, 127:128], B_z,
                                    start=True, stop=True), 'tk')
                                cur['tz'] = T_bz
                                b.w('v', 'tk', T_z)
                                V_tg = b.op('v', lambda E: E.scalar_tensor_tensor(
                                    F0, B_g, 0.0, PZ[:, :], OP.bypass, OP.mult), 'vk')
                                cur['tg'] = V_tg
                                b.w('v', 'tk', T_bz)
                                V_cz = b.op('v', lambda E, h=h: E.tensor_add(
                                    CZ[0:1, h * 512:(h + 1) * 512],
                                    CZ[0:1, h * 512:(h + 1) * 512], PCS[64:65, :]), 'vk')
                                cur['cz'] = V_cz
                                b.w('g', 'vk', V_tg)
                                b.op('g', lambda E: E.tensor_add(B_a, F0, B_te), None)
                                b.op('g', lambda E: E.tensor_mul(B_dent, B_a, B_t2), None)
                                G_xin = b.op('g', lambda E, j=j, h=h: E.tensor_add(
                                    s2th(j, h), B_dent, F3), 'gk')
                                cur['xin'] = G_xin
                                cur['num'] = G_xin
                            return stage_b2

                        hist.append(cur)
                        dit = it - 4
                        if dit >= 0 and dit % 2 == 1:
                            jj = dit // 2
                            b.w('v', 'gk', hist[dit]['xin'])
                            b.op('v', lambda E, jj=jj: E.reduce_sum(
                                XS2[:, jj:jj + 1], s2t(jj), AX), 'vk')
                            b.w('s', 'gk', hist[dit]['xin'])
                            b.op('s', lambda E, jj=jj: E.activation(
                                SCRA[:, 2048:3072], s2t(jj), AF.Square,
                                accum_out=XSQ2[:, jj:jj + 1]), 'sk')
                        if s1 + 1 < NS1:
                            emit_xxt(s1 + 1, it % 4)
                        if pp_chunks and it % 2 == 0:
                            pp_chunks.pop(0)()
                        nb2 = pend1[0]() if pend1[0] is not None else None
                        if pend2[0] is not None:
                            pend2[0]()
                        pend1[0] = stage_b1
                        pend2[0] = nb2
                T_r_lastslice[s1] = b.cnt['tk']
            nb2 = pend1[0]()
            if pend2[0] is not None:
                pend2[0]()
            nb2()
            while pp_chunks:
                pp_chunks.pop(0)()
            XL = b.cnt['gk']
            b.w('v', 'gk', XL)
            b.w('s', 'gk', XL)
            for j in range(nt - 2, nt):
                b.op('v', lambda E, j=j: E.reduce_sum(
                    XS2[:, j:j + 1], s2t(j), AX), 'vk')
                b.op('s', lambda E, j=j: E.activation(
                    SCRA[:, 2048:3072], s2t(j), AF.Square,
                    accum_out=XSQ2[:, j:j + 1]), 'sk')
            V_ch2 = rs_chain(XS2[:, 0:nt], XSQ2[:, 0:nt], [RS2], [B2N], False)
            jr = nt - 1
            b.w('s', 'vk', V_ch2)
            b.w('s', 'tk', T_ttr[nt - 1])
            S_hr = b.op('s', lambda E, jr=jr: E.activation(
                SCRA[:, 0:1024], s2t(jr), AF.Identity,
                scale=RS2[:, jr:jr + 1], bias=B2N[:, jr:jr + 1]), 'sk')
            b.w('t', 'sk', S_hr)
            b.w('t', 'sk', S_hev[nt - 1])
            for i in range(NC8):
                T_tr = b.op('t', lambda E, i=i: E.matmul(
                    PT[:, i * 128:(i + 1) * 128],
                    SCRA[:, i * 128:(i + 1) * 128], IDB[:, :],
                    start=True, stop=True),
                    'tk' if i == NC8 - 1 else None)
            b.w('s', 'tk', T_tr)
            S_roll = None
            for i in range(NC8):
                S_roll = b.op('s', lambda E, i=i: E.activation(
                    ROLL[:, i:i + 1],
                    PT[:, i * 128 + 127:i * 128 + 128],
                    AF.Identity, scale=W2C[:, i:i + 1], bias=B2C[:, i:i + 1]),
                    'sk' if i == NC8 - 1 else None)
            iter_join = {'vk': b.cnt['vk'], 'sk': b.cnt['sk'],
                         'gk': b.cnt['gk'], 'tk': b.cnt['tk']}
            if dbg:
                for sem, val in iter_join.items():
                    b.w('y', sem, val)
                DBG1 = b.op('y', lambda E: E.dma_start(
                    DXI.ap().rearrange("(j p) c -> p j c", p=128),
                    S2[:, :].rearrange("p (j c) -> p j c", c=C)), 'yd', 16)
            b.emit_block()

        with ExitStack() as ph2:
            e2 = ph2.enter_context
            W2T = SL2 * 128
            H2T = e2(nc.sbuf_tensor("H2T", [128, NC8, W2T], BF16))
            XX2 = e2(nc.sbuf_tensor("XX2", [128, NC8, W2T], BF16))
            KC2 = e2(nc.sbuf_tensor("KC2", [128, NHC, W2T], BF16))
            WCR8 = e2(nc.sbuf_tensor("WCR8", [128, NC8, C], BF16))
            WCKB = e2(nc.sbuf_tensor("WCKB", [128, NC8, 5, 128], BF16))
            WCVB = e2(nc.sbuf_tensor("WCVB", [128, 5, C], BF16))
            RCB = e2(nc.sbuf_tensor("RCB", [128, SL2, C], BF16))
            RHOB = e2(nc.sbuf_tensor("RHOB", [128, W2T], BF16))
            T2F = e2(nc.sbuf_tensor("T2F", [128, NC8, W2T // 2], F32))
            T3B = e2(nc.sbuf_tensor("T3B", [128, 2, C], F32))
            SCR2 = e2(nc.sbuf_tensor("SCR2", [128, 2, C], BF16))
            PALL = e2(nc.psum_tensor("PALL", [128, 8, 512], F32))

            join2 = dict(iter_join)
            for eng in ('t', 'v', 's', 'g', 'y'):
                for sem, val in join2.items():
                    b.w(eng, sem, val)
            if dbg:
                b.w('v', 'yd', DBG1)

            G_wcr = b.op('g', lambda E: E.dma_start(
                WCR8[:, :, :],
                bass.AP(WCRT, 0, [[C, 128], [128 * C, NC8], [1, C]])), 'gd', 16)

            slice_join = {'tk': b.cnt['tk'], 'vk': b.cnt['vk'],
                          'sk': b.cnt['sk'], 'gk': b.cnt['gk']}
            V_x1 = {}
            V_rcvv = {}
            V_x1j = {}
            G_t3 = {}
            Yx2 = {}
            V_mm = 0
            pend_ev = [None]
            A_tok = {}
            HW2 = W2T // 2
            NPK = NS2 * NHC
            wck_tok = {}
            wck_rd = {}
            wcv_tok = {}
            wcv_rd = {}

            def wck_load(g):
                b.w('s', 'pp', PP_K)
                b.w('s', 'tk', wck_rd.get(g - 5, 0))
                wck_tok[g] = b.op('s', lambda E, hh=g % NHC, g=g: E.dma_start(
                    WCKB[:, :, g % 5, :],
                    bass.AP(WCK8D, hh * 128 * NC8 * 128,
                            [[NC8 * 128, 128], [128, NC8], [1, 128]])),
                    'wk%d' % (g % 5), 16)

            def wcv_load(g):
                b.w('y', 'pp', PP_ALL)
                b.w('y', 'tk', wcv_rd.get(g - 5, 0))
                wcv_tok[g] = b.op('y', lambda E, hh=g % NHC, g=g: E.dma_start(
                    WCVB[:, g % 5, :],
                    bass.AP(WCV8D, hh * 128 * C, [[C, 128], [1, C]])),
                    'wv%d' % (g % 5), 16)

            for g in range(5):
                wck_load(g)
            for g in range(5):
                wcv_load(g)
            for jn in (0, 1):
                Yx2[jn] = b.op('g', lambda E, jn=jn: E.dma_start(
                    XB[:, jn % 2, :], X[jn * 128:(jn + 1) * 128, :]),
                    'xg%d' % (jn % 2), 16)

            def emit_A(s2i, part):
                par = s2i % 2
                tiles = [s2i * SL2 + i for i in range(SL2)]
                prev = A_tok.get(s2i - 1)
                if part == 0:
                    hh2 = {}
                    for j in tiles[:2]:
                        b.w('s', 'tk', prev['T_tt'] if prev else slice_join['tk'])
                        b.w('s', 'vk', V_ch2)
                        hh2[j] = b.op('s', lambda E, j=j: E.activation(
                            SCRA[:, (j % 2) * 1024:(j % 2 + 1) * 1024], s2t(j),
                            AF.Identity,
                            scale=RS2[:, j:j + 1], bias=B2N[:, j:j + 1]), 'sk')
                    A_tok[s2i] = dict(hh2=hh2, Trc=0)
                    return
                if part == 1:
                    cur = A_tok[s2i]
                    if s2i >= 2:
                        b.w('t', 'vk', V_rcvv.get((s2i - 2) * SL2, 0))
                    S_hev2 = S_roll if s2i == 0 else prev['S_hev2']
                    T_tt = 0
                    for j in tiles:
                        if j - 2 in tiles:
                            b.w('s', 'tk', T_tt)
                            b.w('s', 'vk', V_ch2)
                            cur['hh2'][j] = b.op('s', lambda E, j=j: E.activation(
                                SCRA[:, (j % 2) * 1024:(j % 2 + 1) * 1024],
                                s2t(j), AF.Identity,
                                scale=RS2[:, j:j + 1], bias=B2N[:, j:j + 1]), 'sk')
                        b.w('t', 'sk', cur['hh2'][j])
                        b.w('t', 'sk', S_hev2)
                        for i in range(NC8):
                            T_tt = b.op('t', lambda E, j=j, i=i: E.matmul(
                                PALL[:, 0:2, :].rearrange("p a b -> p (a b)")[:, i * 128:(i + 1) * 128],
                                SCRA[:, (j % 2) * 1024 + i * 128:(j % 2) * 1024 + (i + 1) * 128],
                                IDB[:, :], start=True, stop=True),
                                'tk' if i == NC8 - 1 else None)
                        b.w('s', 'tk', T_tt)
                        b.w('s', 'vk', slice_join['vk'] if s2i == 0
                            else prev['V_xx2'])
                        for i in range(NC8):
                            S_hev2 = b.op('s', lambda E, j=j, i=i: E.activation(
                                H2T[:, i, (j % SL2) * 128:(j % SL2 + 1) * 128],
                                PALL[:, 0:2, :].rearrange("p a b -> p (a b)")[:, i * 128:(i + 1) * 128],
                                AF.Identity, scale=W2C[:, i:i + 1], bias=B2C[:, i:i + 1]),
                                'sk' if i == NC8 - 1 else None)
                    cur['S_hev2'] = S_hev2
                    cur['T_tt'] = T_tt
                    return
                cur = A_tok[s2i]
                pbsrc = ROLL if s2i == 0 else PB2
                V_xx2 = None
                for hv in range(2):
                    if hv == 1:
                        b.w('s', 'vk', V_xx2)
                    S_t1 = None
                    for i in range(NC8):
                        S_t1 = b.op('s', lambda E, i=i, hv=hv: E.activation(
                            T2F[:, i, :],
                            H2T[:, i, hv * HW2:(hv + 1) * HW2], AF.Copy,
                            scale=OMTS[:, i:i + 1]),
                            'sk' if i == NC8 - 1 else None)
                    b.w('v', 'sk', S_t1)
                    if hv == 0 and s2i >= 1:
                        b.w('v', 'tk', prev['Trc'])
                    for i in range(NC8):
                        lo = hv * HW2
                        b.op('v', lambda E, i=i, lo=lo: E.scalar_tensor_tensor(
                            XX2[:, i, lo + 1:lo + HW2],
                            H2T[:, i, lo:lo + HW2 - 1], TSC[:, i:i + 1],
                            T2F[:, i, 1:HW2], OP.mult, OP.add), None)
                        shsrc = (pbsrc[:, i:i + 1] if hv == 0
                                 else H2T[:, i, HW2 - 1:HW2])
                        V_xx2 = b.op('v', lambda E, i=i, lo=lo, shsrc=shsrc:
                            E.scalar_tensor_tensor(
                                XX2[:, i, lo:lo + 1], shsrc, TSC[:, i:i + 1],
                                T2F[:, i, 0:1], OP.mult, OP.add),
                            'vk' if i == NC8 - 1 else None)
                b.w('s', 'vk', V_xx2)
                for i in range(NC8):
                    b.op('s', lambda E, i=i: E.activation(
                        PB2[:, i:i + 1], H2T[:, i, W2T - 1:W2T], AF.Copy),
                        'sk' if i == NC8 - 1 else None)
                cur['V_xx2'] = V_xx2

            for pp_ in range(3):
                emit_A(0, pp_)
            for s2i in range(NS2):
                tiles = [s2i * SL2 + i for i in range(SL2)]
                V_xx2 = A_tok[s2i]['V_xx2']
                V_kc2 = {}
                S_rho = {}
                Tkc = {}
                b.w('t', 'vk', V_rcvv.get((s2i - 1) * SL2 + 1, 0))
                for hh in range(NHC):
                    g = s2i * NHC + hh
                    b.w('t', 'wk%d' % (g % 5), wck_tok[g])
                    b.w('t', 'vk', V_xx2)
                    if hh >= 2:
                        b.w('t', 'sk', S_rho[hh - 2])
                    for ci in range(NC8):
                        Tkc[hh] = b.op('t', lambda E, hh=hh, g=g, ci=ci: E.matmul(
                            PALL[:, 2 + hh % 2, :], WCKB[:, ci, g % 5, :],
                            XX2[:, ci, :], start=(ci == 0), stop=(ci == NC8 - 1)),
                            'tk' if ci == NC8 - 1 else None)
                    wck_rd[g] = Tkc[hh]
                    if hh == 4 and pend_ev[0] is not None:
                        pend_ev[0]()
                        pend_ev[0] = None
                    if hh == 16 and s2i + 1 < NS2:
                        emit_A(s2i + 1, 0)
                    b.w('s', 'tk', Tkc[hh])
                    if hh >= 1:
                        b.w('s', 'vk', V_kc2[hh - 1])
                    S_rho[hh] = b.op('s', lambda E, hh=hh: E.activation(
                        RHOB[:, :], PALL[:, 2 + hh % 2, :], AF.Relu), 'sk')
                    if g + 5 < NPK:
                        wck_load(g + 5)
                    b.w('v', 'sk', S_rho[hh])
                    V_kc2[hh] = b.op('v', lambda E, hh=hh: E.tensor_mul(
                        KC2[:, hh, :], RHOB[:, :], RHOB[:, :]), 'vk')
                if s2i + 1 < NS2:
                    emit_A(s2i + 1, 1)
                S_rc = None
                Trc = {}
                Trc_s = {}
                b.w('s', 'vk', V_rcvv.get(s2i * SL2 - 1, 0))
                b.w('t', 'vk', V_rcvv.get((s2i - 1) * SL2 + 2, 0))
                for tq in range(SL2):
                    for hf2 in range(2):
                        idx = tq * 2 + hf2
                        if idx >= 2:
                            b.w('t', 'sk', Trc_s[idx - 2])
                        for ci in range(NC8):
                            Trc[idx] = b.op('t', lambda E, tq=tq, hf2=hf2, ci=ci:
                                E.matmul(PALL[:, 4 + (tq * 2 + hf2) % 2, :],
                                         XX2[:, ci, tq * 128:(tq + 1) * 128],
                                         WCR8[:, ci, hf2 * 512:(hf2 + 1) * 512],
                                         start=(ci == 0), stop=(ci == NC8 - 1)),
                                'tk' if ci == NC8 - 1 else None)
                        b.w('s', 'tk', Trc[idx])
                        Trc_s[idx] = b.op('s', lambda E, tq=tq, hf2=hf2: E.activation(
                            RCB[:, tq, hf2 * 512:(hf2 + 1) * 512],
                            PALL[:, 4 + (tq * 2 + hf2) % 2, :], AF.Sigmoid), 'sk')
                        S_rc = Trc_s[idx]
                A_tok[s2i]['Trc'] = Trc[2 * SL2 - 1]
                if s2i + 1 < NS2:
                    emit_A(s2i + 1, 2)
                b.w('t', 'vk', V_kc2[NHC - 1])
                b.w('t', 'vk', V_rcvv.get((s2i - 1) * SL2 + 3, 0))
                b.w('t', 'sk', S_rc)
                if s2i + 1 < NS2:
                    b.w('t', 'sk', A_tok[s2i + 1]['S_hev2'])
                Tvv = {}
                for hh in range(NHC):
                    g = s2i * NHC + hh
                    b.w('t', 'wv%d' % (g % 5), wcv_tok[g])
                    for tq in range(SL2):
                        for hf2 in range(2):
                            Tvv[hh] = b.op('t', lambda E, hh=hh, g=g, tq=tq, hf2=hf2:
                                E.matmul(PALL[:, tq * 2 + hf2, :],
                                         KC2[:, hh, tq * 128:(tq + 1) * 128],
                                         WCVB[:, g % 5, hf2 * 512:(hf2 + 1) * 512],
                                         start=(hh == 0), stop=(hh == NHC - 1)),
                                'tk' if (tq == SL2 - 1 and hf2 == 1) else None)
                    wcv_rd[g] = Tvv[hh]
                    if g + 5 < NPK:
                        wcv_load(g + 5)
                T_vv = Tvv[NHC - 1]
                for ii, j in enumerate(tiles[:2]):
                    b.w('v', 'tk', T_vv)
                    V_rcvv[j] = b.op('v', lambda E, tq=ii, j=j: E.tensor_mul(
                        SCR2[:, j % 2, :], RCB[:, tq, :],
                        PALL[:, tq * 2:tq * 2 + 2, :].rearrange("p a b -> p (a b)")), 'vk')
                for j in tiles[:2]:
                    b.w('g', 'xg%d' % (j % 2), Yx2[j])
                    b.w('g', 'vk', V_x1j.get(j - 2, 0))
                    G_t3[j] = b.op('g', lambda E, j=j: E.tensor_add(
                        T3B[:, j % 2, :], XB[:, j % 2, :], s2t(j)), 'gk')
                for j in tiles[:2]:
                    b.w('v', 'gk', G_t3[j])
                    V_x1j[j] = b.op('v', lambda E, j=j: E.scalar_tensor_tensor(
                        s2t(j), SCR2[:, j % 2, :], 0.0, T3B[:, j % 2, :],
                        OP.bypass, OP.add,
                        accum_out=LSUM[:, j:j + 1]), 'vk')
                for j in tiles[:2]:
                    b.w('s', 'vk', V_x1j[j])
                    b.op('s', lambda E, j=j: E.activation(
                        SCRA[:, 2048:3072], s2t(j), AF.Square,
                        accum_out=LSQ[:, j:j + 1]), 'sk')
                V_x1[s2i] = V_x1j[tiles[1]]

                def evac2(s2i=s2i, tiles=tiles):
                    for ii, j in enumerate(tiles[2:]):
                        tq = 2 + ii
                        b.w('g', 'vk', V_x1j[j - 2])
                        Yx2[j] = b.op('g', lambda E, j=j: E.dma_start(
                            XB[:, j % 2, :], X[j * 128:(j + 1) * 128, :]),
                            'xg%d' % (j % 2), 16)
                        b.w('v', 'tk', T_vv)
                        V_rcvv[j] = b.op('v', lambda E, tq=tq, j=j: E.tensor_mul(
                            SCR2[:, j % 2, :], RCB[:, tq, :],
                            PALL[:, tq * 2:tq * 2 + 2, :].rearrange("p a b -> p (a b)")), 'vk')
                    for j in tiles[2:]:
                        b.w('g', 'xg%d' % (j % 2), Yx2[j])
                        G_t3[j] = b.op('g', lambda E, j=j: E.tensor_add(
                            T3B[:, j % 2, :], XB[:, j % 2, :], s2t(j)), 'gk')
                    for j in tiles[2:]:
                        b.w('v', 'gk', G_t3[j])
                        V_x1j[j] = b.op('v', lambda E, j=j: E.scalar_tensor_tensor(
                            s2t(j), SCR2[:, j % 2, :], 0.0, T3B[:, j % 2, :],
                            OP.bypass, OP.add,
                            accum_out=LSUM[:, j:j + 1]), 'vk')
                    for j in tiles[2:]:
                        b.w('s', 'vk', V_x1j[j])
                        b.op('s', lambda E, j=j: E.activation(
                            SCRA[:, 2048:3072], s2t(j), AF.Square,
                            accum_out=LSQ[:, j:j + 1]), 'sk')
                    nonlocal V_mm
                    for j in tiles:
                        b.op('v', lambda E, j=j: E.reduce_max(
                            MXT[:, j:j + 1], s2t(j), AX), None)
                        V_mm = b.op('v', lambda E, j=j: E.tensor_reduce(
                            MNT[:, j:j + 1], s2t(j), AX, OP.min), 'vk')
                    if s2i + 1 < NS2:
                        for jn in [(s2i + 1) * SL2, (s2i + 1) * SL2 + 1]:
                            Yx2[jn] = b.op('g', lambda E, jn=jn: E.dma_start(
                                XB[:, jn % 2, :],
                                X[jn * 128:(jn + 1) * 128, :]),
                                'xg%d' % (jn % 2), 16)
                pend_ev[0] = evac2
                slice_join = {'tk': b.cnt['tk'], 'vk': b.cnt['vk'],
                              'sk': b.cnt['sk'], 'gk': b.cnt['gk']}

            if pend_ev[0] is not None:
                pend_ev[0]()
                pend_ev[0] = None
            slice_join = {'tk': b.cnt['tk'], 'vk': b.cnt['vk'],
                          'sk': b.cnt['sk'], 'gk': b.cnt['gk']}

            if dbg:
                for sem, val in slice_join.items():
                    b.w('y', sem, val)
                DBG2 = b.op('y', lambda E: E.dma_start(
                    DX1.ap().rearrange("(j p) c -> p j c", p=128),
                    S2[:, :].rearrange("p (j c) -> p j c", c=C)), 'yd', 16)
            b.emit_block()

        with ExitStack() as ph3:
            e3 = ph3.enter_context
            HFP = e3(nc.sbuf_tensor("HFP", [128, 2, C], F32))
            WFS = e3(nc.sbuf_tensor("WFS", [128, NC8, C], F32))
            WQ8 = e3(nc.sbuf_tensor("WQ8", [128, NC8, C], FP8))
            QT = e3(nc.sbuf_tensor("QT", [128, 2, NC8, 128], FP8))
            QTK = e3(nc.sbuf_tensor("QTK", [128, 2, C], FP8))
            IDF8 = e3(nc.sbuf_tensor("IDF8", [128, 128], FP8))
            FBB = e3(nc.sbuf_tensor("FBB", [128, C], F32))
            OF3 = e3(nc.sbuf_tensor("OF3", [128, 2, C], F32))
            UBB = e3(nc.sbuf_tensor("UBB", [128, 2, C], BF16))
            OB = e3(nc.sbuf_tensor("OB", [128, 2, C], F32))
            SWH = e3(nc.sbuf_tensor("SWH", [128, 1], F32))
            SXH = e3(nc.sbuf_tensor("SXH", [128, 1], F32))
            SCC = e3(nc.sbuf_tensor("SCC", [128, 1], F32))
            WAM = e3(nc.sbuf_tensor("WAM", [128, 1], F32))
            PT3 = e3(nc.psum_tensor("PT3", [128, 2, C], F32))
            PAC = e3(nc.psum_tensor("PAC", [128, 2, C], F32))

            for eng in ('t', 'v', 's', 'g', 'y'):
                for sem, val in slice_join.items():
                    b.w(eng, sem, val)
            G_id8 = b.op('g', lambda E: E.dma_start(IDF8[:, :], IDD.ap()), 'gd', 16)
            if dbg:
                b.w('y', 'yd', DBG2)
                b.w('s', 'yd', DBG2)

            V_ch3 = rs_chain(LSUM, LSQ, [RSL], [BLN], False)
            Y_wf = b.op('y', lambda E: E.dma_start(
                WFS[:, :, :],
                bass.AP(WFT, 0, [[C, 128], [128 * C, NC8], [1, C]])), 'yd', 16)
            G_fb = b.op('g', lambda E: E.dma_start(
                FBB[:, :], bass.AP(FBV, 0, [[0, 128], [1, C]])), 'gd', 16)
            b.w('v', 'yd', Y_wf)
            V_wa = b.op('v', lambda E: E.reduce_max(
                WAM[:, :], WFS[:, :, :].rearrange("p a b -> p (a b)"), AX,
                apply_absolute_value=True), 'vk')
            b.w('y', 'vk', V_wa)
            Y_wr = b.op('y', lambda E: E.dma_start(AROW[:, :], WAM[:, 0:1]), 'yd', 16)
            b.w('v', 'yd', Y_wr)
            V_wm = b.op('v', lambda E: E.reduce_max(SCL[0:1, 0:1], AROW[:, :], AX), 'vk')
            S_hf = {}

            def sw_step(i):
                if i == 0:
                    b.w('s', 'vk', V_wm)
                    b.op('s', lambda E: E.activation(SCL[0:1, 1:2], SCL[0:1, 0:1], AF.Copy), 'sk')
                elif i == 1:
                    b.w('v', 'sk', b.cnt['sk'])
                    b.op('v', lambda E: E.tensor_scalar(
                        SCL[0:1, 2:3], SCL[0:1, 1:2], 1e-12, None, OP.max), 'vk')
                elif i == 2:
                    b.w('s', 'vk', b.cnt['vk'])
                    b.op('s', lambda E: E.activation(SCL[0:1, 3:4], SCL[0:1, 2:3], AF.Copy), 'sk')
                elif i == 3:
                    b.w('v', 'sk', b.cnt['sk'])
                    b.op('v', lambda E: E.reciprocal(SCL[0:1, 4:5], SCL[0:1, 3:4]), 'vk')
                elif i == 4:
                    b.w('s', 'vk', b.cnt['vk'])
                    b.op('s', lambda E: E.activation(SCL[0:1, 5:6], SCL[0:1, 4:5], AF.Copy), 'sk')
                elif i == 5:
                    b.w('v', 'sk', b.cnt['sk'])
                    b.op('v', lambda E: E.tensor_scalar(
                        SCL[0:1, 6:7], SCL[0:1, 5:6], 448.0, 448.0, OP.mult, OP.min), 'vk')
                elif i == 6:
                    b.w('s', 'vk', b.cnt['vk'])
                    b.op('s', lambda E: E.activation(
                        SCL[0:1, 7:8], SCL[0:1, 6:7], AF.Copy, scale=0.5), 'sk')

            sw_step(0)
            b.w('v', 'vk', V_ch3)
            b.w('v', 'vk', V_mm)
            b.op('v', lambda E: E.tensor_mul(AMX[:, :], MXT[:, :], RSL[:, :]), None)
            sw_step(1)
            b.op('v', lambda E: E.tensor_mul(MNT[:, :], MNT[:, :], RSL[:, :]), None)
            b.op('v', lambda E: E.tensor_add(AMX[:, :], AMX[:, :], BLN[:, :]), None)
            sw_step(2)
            b.op('v', lambda E: E.tensor_add(MNT[:, :], MNT[:, :], BLN[:, :]), None)
            b.op('v', lambda E: E.reduce_max(
                MXT[:, 0:1], AMX[:, :], AX, apply_absolute_value=True), None)
            sw_step(3)
            b.op('v', lambda E: E.reduce_max(
                MNT[:, 0:1], MNT[:, :], AX, apply_absolute_value=True), None)
            sw_step(4)
            V_a1 = b.op('v', lambda E: E.tensor_max(
                WAM[:, 0:1], MXT[:, 0:1], MNT[:, 0:1]), 'vk')
            b.w('y', 'vk', V_a1)
            Y_ar = b.op('y', lambda E: E.dma_start(AROW[:, :], WAM[:, 0:1]), 'yd', 16)
            b.w('v', 'yd', Y_ar)
            V_ams = b.op('v', lambda E: E.reduce_max(SCL[0:1, 16:17], AROW[:, :], AX), 'vk')
            sw_step(5)
            b.w('s', 'vk', V_ams)
            S_amc = b.op('s', lambda E: E.activation(
                AMSB[0:1, 0:1], SCL[0:1, 16:17], AF.Copy), 'sk')
            b.op('g', lambda E: E.memset(AMSB[0:1, 1:8], 0.0), 'gk')
            b.w('g', 'sk', S_amc)
            G_ci = b.op('g', lambda E: E.dma_start(cc_in.ap(), AMSB[:, :]), 'gd', 16)
            b.w('g', 'gd', G_ci)
            G_cc = b.op('g', lambda E: E.collective_compute(
                "AllReduce", OP.max, replica_groups=[list(range(8))],
                ins=[cc_in.ap().opt()], outs=[cc_out.ap().opt()]), 'cc')
            sw_step(6)
            b.w('y', 'sk', b.cnt['sk'])
            Y_scA = b.op('y', lambda E: E.dma_start(
                bass.AP(scald, 0, [[8, 1], [1, 8]]), SCL[0:1, 0:8]), 'yd', 16)
            b.w('g', 'yd', Y_scA)
            LW_swh = b.op('g', lambda E: E.dma_start(
                SWH[:, :], bass.AP(scald, 7, [[0, 128], [1, 1]])), 'ld0', 16)
            b.w('v', 'ld0', LW_swh)
            V_wq = b.op('v', lambda E: E.tensor_scalar(
                WQ8[:, :, :].rearrange("p a b -> p (a b)"),
                WFS[:, :, :].rearrange("p a b -> p (a b)"),
                SWH[:, 0:1], None, OP.mult), 'vk')
            b.w('g', 'cc', G_cc)
            G_gm = b.op('g', lambda E: E.dma_start(GMX[:, :], cc_out.ap()), 'gd', 16)
            b.w('v', 'gd', G_gm)
            b.op('v', lambda E: E.tensor_scalar(
                SCL[0:1, 8:9], GMX[0:1, 0:1], 1e-12, None, OP.max), 'vk')
            b.w('s', 'vk', b.cnt['vk'])
            b.op('s', lambda E: E.activation(SCL[0:1, 9:10], SCL[0:1, 8:9], AF.Copy), 'sk')
            b.w('v', 'sk', b.cnt['sk'])
            b.op('v', lambda E: E.reciprocal(SCL[0:1, 10:11], SCL[0:1, 9:10]), 'vk')
            b.w('s', 'vk', b.cnt['vk'])
            b.op('s', lambda E: E.activation(SCL[0:1, 11:12], SCL[0:1, 10:11], AF.Copy), 'sk')
            b.w('v', 'sk', b.cnt['sk'])
            b.op('v', lambda E: E.tensor_scalar(
                SCL[0:1, 12:13], SCL[0:1, 11:12], 448.0, 448.0, OP.mult, OP.min), 'vk')
            b.w('s', 'vk', b.cnt['vk'])
            b.op('s', lambda E: E.activation(
                SCL[0:1, 13:14], SCL[0:1, 12:13], AF.Copy, scale=0.5), 'sk')
            b.w('v', 'sk', b.cnt['sk'])
            b.op('v', lambda E: E.tensor_mul(
                SCL[0:1, 14:15], SCL[0:1, 12:13], SCL[0:1, 6:7]), 'vk')
            b.w('s', 'vk', b.cnt['vk'])
            b.op('s', lambda E: E.activation(SCL[0:1, 15:16], SCL[0:1, 14:15], AF.Copy), 'sk')
            b.w('v', 'sk', b.cnt['sk'])
            V_sc = b.op('v', lambda E: E.reciprocal(SCL[0:1, 17:18], SCL[0:1, 15:16]), 'vk')
            b.w('s', 'vk', V_sc)
            S_sc = b.op('s', lambda E: E.activation(
                SCL[0:1, 18:19], SCL[0:1, 17:18], AF.Copy, scale=4.0), 'sk')
            b.w('y', 'sk', S_sc)
            Y_sc = b.op('y', lambda E: E.dma_start(
                bass.AP(scald, 8, [[16, 1], [1, 16]]), SCL[0:1, 8:24]), 'yd', 16)
            b.w('g', 'yd', Y_sc)
            LW_sxh = b.op('g', lambda E: E.dma_start(
                SXH[:, :], bass.AP(scald, 13, [[0, 128], [1, 1]])), 'ld1', 16)
            LW_scc = b.op('g', lambda E: E.dma_start(
                SCC[:, :], bass.AP(scald, 18, [[0, 128], [1, 1]])), 'ld2', 16)

            T_t3 = {}
            V_q8 = {}
            T_ff = {}
            S_so = {}
            G_u = {}
            V_o = {}
            S_hf2 = {}
            V_qtk = {}
            T_t3last = {}
            Y_out = {}
            pend3 = [None]
            for j in range(nt):
                b.w('s', 'vk', V_wq)
                b.w('s', 'vk', V_qtk.get(j - 2, 0))
                S_hf2[j] = b.op('s', lambda E, j=j: E.activation(
                    HFP[:, j % 2, :], s2t(j), AF.Identity,
                    scale=RSL[:, j:j + 1], bias=BLN[:, j:j + 1]), 'sk')
                if dbg:
                    b.w('y', 'sk', S_hf2[j])
                    Ydh = b.op('y', lambda E, j=j: E.dma_start(
                        DHF.ap()[j * 128:(j + 1) * 128, :], HFP[:, j % 2, :]),
                        'yd', 16)
                    b.w('s', 'yd', Ydh)
                b.w('v', 'sk', S_hf2[j])
                b.w('v', 'ld1', LW_sxh)
                b.w('v', 'tk', T_t3last.get(j - 2, 0))
                V_qtk[j] = b.op('v', lambda E, j=j: E.tensor_scalar(
                    QTK[:, j % 2, :], HFP[:, j % 2, :],
                    SXH[:, 0:1], None, OP.mult), 'vk')
                b.w('t', 'vk', V_qtk[j])
                b.w('t', 'gd', G_id8)
                b.w('t', 'vk', V_q8.get(j - 2, 0))
                for i in range(NC8):
                    T_t3[j] = b.op('t', lambda E, j=j, i=i: E.matmul(
                        PT3[:, j % 2, i * 128:(i + 1) * 128],
                        QTK[:, j % 2, i * 128:(i + 1) * 128], IDF8[:, :],
                        start=True, stop=True),
                        'tk' if i == NC8 - 1 else None)
                T_t3last[j] = T_t3[j]
                b.w('v', 'tk', T_t3[j])
                b.w('v', 'tk', T_ff.get(j - 2, 0))
                V_q8[j] = b.op('v', lambda E, j=j: E.tensor_copy(
                    QT[:, j % 2, :, :].rearrange("p a b -> p (a b)"),
                    PT3[:, j % 2, :]), 'vk')
                b.w('t', 'vk', V_q8[j])
                b.w('t', 'sk', S_so.get(j - 2, 0))
                for hf2 in range(2):
                    for ci in range(NC8):
                        T_ff[j] = b.op('t', lambda E, j=j, hf2=hf2, ci=ci: E.matmul(
                            PAC[:, j % 2, hf2 * 512:(hf2 + 1) * 512],
                            QT[:, j % 2, ci, :],
                            WQ8[:, ci, hf2 * 512:(hf2 + 1) * 512],
                            start=(ci == 0), stop=(ci == NC8 - 1)),
                            'tk' if (hf2 == 1 and ci == NC8 - 1) else None)

                def back(j=j, T_ffj=T_ff[j]):
                    b.w('s', 'tk', T_ffj)
                    b.w('s', 'ld2', LW_scc)
                    b.w('s', 'gk', G_u.get(j - 2, 0))
                    S_so[j] = b.op('s', lambda E, j=j: E.activation(
                        OF3[:, j % 2, :], PAC[:, j % 2, :], AF.Copy,
                        scale=SCC[:, 0:1]), 'sk')
                    b.w('g', 'sk', S_so[j])
                    b.w('g', 'gd', G_fb)
                    b.w('g', 'vk', V_o.get(j - 2, 0))
                    G_u[j] = b.op('g', lambda E, j=j: E.tensor_add(
                        UBB[:, j % 2, :], OF3[:, j % 2, :], FBB[:, :]), 'gk')
                    b.w('v', 'gk', G_u[j])
                    b.w('v', 'yo', Y_out.get(j - 2, 0))
                    V_o[j] = b.op('v', lambda E, j=j: E.tensor_add(
                        OB[:, j % 2, :], s2t(j), UBB[:, j % 2, :]), 'vk')
                    b.w('y', 'vk', V_o[j])
                    Y_out[j] = b.op('y', lambda E, j=j: E.dma_start(
                        OUT[j * 128:(j + 1) * 128, :], OB[:, j % 2, :]), 'yo', 16)

                if pend3[0] is not None:
                    pend3[0]()
                pend3[0] = back
            pend3[0]()
            b.w('y', 'yd', b.cnt['yd'])
            b.w('y', 'yo', b.cnt['yo'])
            b.emit_block()

    return nc


_BF = ml_dtypes.bfloat16


def _consts():
    tri = np.triu(np.ones((128, 128), dtype=np.float32))
    ident = np.eye(128, dtype=np.float32)
    ones = np.ones((1, 128), dtype=np.float32)
    return tri, ident, ones


def kernel(L0_attn_x, L0_attn_time_decay, L0_attn_time_first, L0_attn_time_shift,
           L0_attn_Wk, L0_attn_Wv, L0_attn_Wr, L0_attn_Wck, L0_attn_Wcv,
           L0_attn_Wcr, L0_attn_ln1_w, L0_attn_ln1_b, L0_attn_ln2_w,
           L0_attn_ln2_b, L0_ffn_weight, L0_ffn_bias):
    x = np.ascontiguousarray(np.asarray(L0_attn_x, dtype=np.float32))
    cA = np.ascontiguousarray
    tri, ident, ones = _consts()
    common = {
        "WKT": cA(np.asarray(L0_attn_Wk, np.float32).T),
        "WVT": cA(np.asarray(L0_attn_Wv, np.float32).T),
        "WRT": cA(np.asarray(L0_attn_Wr, np.float32).T),
        "WCKR": cA(np.asarray(L0_attn_Wck, np.float32).T.reshape(
            NC8, 128, NHC, 128).transpose(2, 1, 0, 3).reshape(NHC, 128, NC8 * 128)),
        "WCVT": cA(np.asarray(L0_attn_Wcv, np.float32).T),
        "WCRT": cA(np.asarray(L0_attn_Wcr, np.float32).T),
        "WFT": cA(np.asarray(L0_ffn_weight, np.float32).T),
        "TSV": cA(np.asarray(L0_attn_time_shift, np.float32).reshape(C)),
        "TFV": cA(np.asarray(L0_attn_time_first, np.float32).reshape(C)),
        "L1W": cA(np.asarray(L0_attn_ln1_w, np.float32)),
        "L1B": cA(np.asarray(L0_attn_ln1_b, np.float32)),
        "L2W": cA(np.asarray(L0_attn_ln2_w, np.float32)),
        "L2B": cA(np.asarray(L0_attn_ln2_b, np.float32)),
        "FBV": cA(np.asarray(L0_ffn_bias, np.float32)),
        "TRID": tri, "IDD": ident, "ONED": ones,
        "CONSTD": np.array(CONST_VALS, dtype=np.float32),
    }
    nc = build(nt=16, dbg=False)
    in_maps = [dict(common, X=cA(x[bb])) for bb in range(8)]
    res = run_bass_kernel_spmd(nc, in_maps, core_ids=list(range(8)))
    out = np.stack([np.asarray(r["OUT"]) for r in res.results], axis=0)
    return out.astype(np.float32)


# version 26
# speedup vs baseline: 1.0295x; 1.0295x over previous
import numpy as np
import ml_dtypes
from contextlib import ExitStack

import concourse.bass as bass
import concourse.mybir as mybir
from concourse.bass_utils import run_bass_kernel_spmd

F32 = mybir.dt.float32
BF16 = mybir.dt.bfloat16
FP8 = mybir.dt.float8e4
AX = mybir.AxisListType.X
OP = mybir.AluOpType
AF = mybir.ActivationFunctionType
EPS = 1e-5
C = 1024
CONST_VALS = (EPS, EPS * EPS, 1.0 + EPS, 1.0, -1.0, 0.5, -0.5, 1.0 / 1024.0,
              448.0, 4.0, 1e-12, 0.0, 2.0, -2.0, 128.0, 1024.0)
NC8 = 8
H = 4096
NHC = 32


class PB:

    def __init__(self, nc):
        self.nc = nc
        self.plan = []
        self.cnt = {}
        self.lastw = {}
        self.sems = {}
        self.dmatok = {}

    def mksem(self, ctx, *names):
        for n in names:
            self.sems[n] = ctx.enter_context(self.nc.semaphore(n))
            self.cnt[n] = 0

    def w(self, eng, sem, val):
        if not val or val <= 0:
            return
        key = (eng, sem)
        if self.lastw.get(key, 0) >= val:
            return
        self.lastw[key] = val
        h = self.sems[sem]
        self.plan.append((eng, lambda E, h=h, v=val: E.wait_ge(h, v)))

    def op(self, eng, fn, inc=None, n=1):
        if inc is None:
            self.plan.append((eng, fn))
            return None
        self.cnt[inc] += n
        h = self.sems[inc]
        self.plan.append((eng, lambda E, f=fn, h=h, m=n: f(E).then_inc(h, m)))
        val = self.cnt[inc]
        if inc in ("yd", "gd"):
            toks = self.dmatok.setdefault((eng, inc), [])
            toks.append(val)
            tgt = toks[-1]
            self.plan.append(
                (eng, lambda E, h=h, v=tgt: E.wait_ge(h, v)))
            key = (eng, inc)
            if self.lastw.get(key, 0) < tgt:
                self.lastw[key] = tgt
        return val

    def emit_block(self):
        plan = self.plan
        self.plan = []
        with self.nc.Block() as blk:
            blk.tensor(lambda E: [f(E) for e, f in plan if e == 't'])
            blk.vector(lambda E: [f(E) for e, f in plan if e == 'v'])
            blk.scalar(lambda E: [f(E) for e, f in plan if e == 's'])
            blk.gpsimd(lambda E: [f(E) for e, f in plan if e == 'g'])
            blk.sync(lambda E: [f(E) for e, f in plan if e == 'y'])


def build(nt=16, dbg=False):
    nc = bass.Bass(target_bir_lowering=False, detect_race_conditions=False)
    T = nt * 128
    SL1 = 2
    SL2 = min(4, nt)
    NS1 = nt // SL1
    NS2 = nt // SL2

    X = nc.dram_tensor("X", [T, C], F32, kind="ExternalInput")
    WKT = nc.dram_tensor("WKT", [C, C], F32, kind="ExternalInput")
    WVT = nc.dram_tensor("WVT", [C, C], F32, kind="ExternalInput")
    WRT = nc.dram_tensor("WRT", [C, C], F32, kind="ExternalInput")
    WCKR = nc.dram_tensor("WCKR", [NHC, 128, NC8 * 128], F32, kind="ExternalInput")
    WCVT = nc.dram_tensor("WCVT", [H, C], F32, kind="ExternalInput")
    WCRT = nc.dram_tensor("WCRT", [C, C], F32, kind="ExternalInput")
    WFT = nc.dram_tensor("WFT", [C, C], F32, kind="ExternalInput")
    TSV = nc.dram_tensor("TSV", [C], F32, kind="ExternalInput")
    TFV = nc.dram_tensor("TFV", [C], F32, kind="ExternalInput")
    L1W = nc.dram_tensor("L1W", [C], F32, kind="ExternalInput")
    L1B = nc.dram_tensor("L1B", [C], F32, kind="ExternalInput")
    L2W = nc.dram_tensor("L2W", [C], F32, kind="ExternalInput")
    L2B = nc.dram_tensor("L2B", [C], F32, kind="ExternalInput")
    FBV = nc.dram_tensor("FBV", [C], F32, kind="ExternalInput")
    CONSTD = nc.dram_tensor("CONSTD", [16], F32, kind="ExternalInput")
    TRID = nc.dram_tensor("TRID", [128, 128], F32, kind="ExternalInput")
    IDD = nc.dram_tensor("IDD", [128, 128], F32, kind="ExternalInput")
    ONED = nc.dram_tensor("ONED", [1, 128], F32, kind="ExternalInput")
    OUT = nc.dram_tensor("OUT", [T, C], F32, kind="ExternalOutput")
    eud = nc.dram_tensor("eud", [1, C], F32)
    WCK8D = nc.dram_tensor("WCK8D", [NHC, 128, NC8 * 128], BF16)
    WCV8D = nc.dram_tensor("WCV8D", [H, C], BF16)
    scald = nc.dram_tensor("scald", [1, 24], F32)
    cc_in = nc.dram_tensor("cc_in", [1, 8], F32)
    cc_out = nc.dram_tensor("cc_out", [1, 8], F32)
    if dbg:
        DXI = nc.dram_tensor("DXI", [T, C], F32, kind="ExternalOutput")
        DX1 = nc.dram_tensor("DX1", [T, C], F32, kind="ExternalOutput")
        DHF = nc.dram_tensor("DHF", [T, C], F32, kind="ExternalOutput")

    b = PB(nc)
    with ExitStack() as outer:
        e = outer.enter_context
        b.mksem(outer, "tk", "vk", "sk", "gk", "yd", "gd", "cc", "yo", "pp",
                "wk0", "wk1", "wk2", "wk3", "wk4", "wk5", "wv0", "wv1", "wv2", "wv3", "wv4", "wv5", "xb0", "xb1", "xg0", "xg1",
                "ld0", "ld1", "ld2")

        S2 = e(nc.sbuf_tensor("S2", [128, nt * C], F32))
        XB = e(nc.sbuf_tensor("XB", [128, 2, C], F32))
        SCRA = e(nc.sbuf_tensor("SCRA", [128, 3072], BF16))
        SUM = e(nc.sbuf_tensor("SUM", [128, nt], F32))
        SSQ = e(nc.sbuf_tensor("SSQ", [128, nt], F32))
        MXT = SUM
        MNT = SSQ
        XS2 = e(nc.sbuf_tensor("XS2", [128, 2 * nt], F32))
        XSQ2 = e(nc.sbuf_tensor("XSQ2", [128, 2 * nt], F32))
        LSUM = e(nc.sbuf_tensor("LSUM", [128, nt], F32))
        LSQ = e(nc.sbuf_tensor("LSQ", [128, nt], F32))
        STA = e(nc.sbuf_tensor("STA", [128, 6, nt], F32))
        RS0 = e(nc.sbuf_tensor("RS0", [128, nt], F32))
        B0N = e(nc.sbuf_tensor("B0N", [128, nt], F32))
        RSC = e(nc.sbuf_tensor("RSC", [128, nt], F32))
        BCN = e(nc.sbuf_tensor("BCN", [128, nt], F32))
        RS2 = e(nc.sbuf_tensor("RS2", [128, nt], F32))
        B2N = e(nc.sbuf_tensor("B2N", [128, nt], F32))
        RSL = e(nc.sbuf_tensor("RSL", [128, nt], F32))
        BLN = e(nc.sbuf_tensor("BLN", [128, nt], F32))
        TRIB = e(nc.sbuf_tensor("TRIB", [128, 128], BF16))
        IDB = e(nc.sbuf_tensor("IDB", [128, 128], BF16))
        ONEB = e(nc.sbuf_tensor("ONEB", [1, 128], BF16))
        ONEF = e(nc.sbuf_tensor("ONEF", [1, 128], F32))
        TSC = e(nc.sbuf_tensor("TSC", [128, 8], F32))
        OPTS = e(nc.sbuf_tensor("OPTS", [128, 8], F32))
        NTS = e(nc.sbuf_tensor("NTS", [128, 8], F32))
        OMTS = e(nc.sbuf_tensor("OMTS", [128, 8], F32))
        TSB = e(nc.sbuf_tensor("TSB", [128, 8], BF16))
        W1C = e(nc.sbuf_tensor("W1C", [128, 8], F32))
        B1C = e(nc.sbuf_tensor("B1C", [128, 8], F32))
        W2C = e(nc.sbuf_tensor("W2C", [128, 8], F32))
        B2C = e(nc.sbuf_tensor("B2C", [128, 8], F32))
        CK = e(nc.sbuf_tensor("CK", [1, C], F32))
        CA = e(nc.sbuf_tensor("CA", [1, C], F32))
        CZ = e(nc.sbuf_tensor("CZ", [1, C], F32))
        PBB = e(nc.sbuf_tensor("PBB", [128, 8], BF16))
        PB2 = e(nc.sbuf_tensor("PB2", [128, 8], BF16))
        ROLL = e(nc.sbuf_tensor("ROLL", [128, 8], BF16))
        AMX = e(nc.sbuf_tensor("AMX", [128, nt], F32))
        AROW = e(nc.sbuf_tensor("AROW", [1, 128], F32))
        SCL = e(nc.sbuf_tensor("SCL", [1, 24], F32))
        AMSB = e(nc.sbuf_tensor("AMSB", [1, 8], F32))
        GMX = e(nc.sbuf_tensor("GMX", [1, 8], F32))
        CB = e(nc.sbuf_tensor("CB", [128, 16], F32))
        for _i, _v in enumerate(CONST_VALS):
            nc.const_aps.register(F32, float(_v), CB[:, _i:_i + 1])

        def s2t(j):
            return S2[:, j * C:(j + 1) * C]

        def s2th(j, h):
            return S2[:, j * C + h * 512: j * C + (h + 1) * 512]

        with ExitStack() as ph1:
            e1 = ph1.enter_context
            W3 = e1(nc.sbuf_tensor("W3", [128, 3, NC8, C], BF16))
            HT = e1(nc.sbuf_tensor("HT", [128, NC8, SL1 * 128], BF16))
            XXT = e1(nc.sbuf_tensor("XXT", [128, 2, NC8, SL1 * 128], BF16))
            EKB = e1(nc.sbuf_tensor("EKB", [128, C], BF16))
            EKV = e1(nc.sbuf_tensor("EKV", [128, C], BF16))
            E1B = e1(nc.sbuf_tensor("E1B", [128, C], BF16))
            EUM1 = e1(nc.sbuf_tensor("EUM1", [128, C], BF16))
            T1F = e1(nc.sbuf_tensor("T1F", [128, NC8, SL1 * 128], F32))
            FP2 = e1(nc.sbuf_tensor("FP2", [128, 2, 2, 512], F32))
            BP2 = e1(nc.sbuf_tensor("BP2", [128, 2, 2, 512], BF16))
            BDEN = e1(nc.sbuf_tensor("BDEN", [128, 512], BF16))
            BIG = e1(nc.sbuf_tensor("BIG", [128, 512], BF16))
            F0P = e1(nc.sbuf_tensor("F0P", [128, 3, 512], F32))
            F3P = e1(nc.sbuf_tensor("F3P", [128, 3, 512], F32))
            B3P = e1(nc.sbuf_tensor("B3P", [128, 3, 4, 512], BF16))
            PKV = e1(nc.psum_tensor("PKV", [128, 2, 512], F32))
            PCS = e1(nc.psum_tensor("PCS", [65, 512], F32))
            PS = e1(nc.psum_tensor("PS", [128, 512], F32))
            PA = e1(nc.psum_tensor("PA", [128, 512], F32))
            PZ = e1(nc.psum_tensor("PZ", [128, 512], F32))
            PT = e1(nc.psum_tensor("PT", [128, C], F32))

            def pools(it):
                par = it % 2
                t3 = it % 3
                F0 = F0P[:, t3, :]
                F1 = FP2[:, par, 0, :]
                F2 = FP2[:, par, 1, :]
                F3 = F3P[:, t3, :]
                B_te = B3P[:, t3, 0, :]
                B_a = B3P[:, t3, 1, :]
                B_dent = B3P[:, t3, 2, :]
                B_t2 = B3P[:, t3, 3, :]
                B_den = BDEN[:, :]
                B_ig = BIG[:, :]
                B_z = BP2[:, par, 0, :]
                B_g = BP2[:, par, 1, :]
                return (F0, F1, F2, F3), (B_te, B_den, B_a, B_t2,
                                          B_dent, B_ig, B_z, B_g)
            UROW = FP2[0:1, 0, 0:2, :].rearrange("p a b -> p (a b)")
            EROW = FP2[0:1, 1, 0:2, :].rearrange("p a b -> p (a b)")

            b.op('g', lambda E: E.dma_start(
                CB[:, :], bass.AP(CONSTD, 0, [[0, 128], [1, 16]])), 'gd', 16)
            wap = lambda t: bass.AP(t, 0, [[C, 128], [128 * C, NC8], [1, C]])
            LW0 = b.op('g', lambda E: E.dma_start(W3[:, 0, :, :], wap(WKT)), 'ld0', 16)
            LW1 = b.op('g', lambda E: E.dma_start(W3[:, 1, :, :], wap(WVT)), 'ld1', 16)
            LW2 = b.op('g', lambda E: E.dma_start(W3[:, 2, :, :], wap(WRT)), 'ld2', 16)
            b.op('g', lambda E: E.dma_start(TRIB[:, :], TRID.ap()), 'gd', 16)
            b.op('g', lambda E: E.dma_start(IDB[:, :], IDD.ap()), 'gd', 16)
            b.op('g', lambda E: E.dma_start(ONEB[:, :], ONED.ap()), 'gd', 16)
            b.op('g', lambda E: E.dma_start(ONEF[:, :], ONED.ap()), 'gd', 16)

            def veccol(dst, src):
                return lambda E, d=dst, s=src: E.dma_start(
                    d[:, :], bass.AP(s, 0, [[1, 128], [128, 8]]),
                    allow_slow_non_contiguous=True)
            b.op('g', veccol(TSC, TSV), 'gd', 16)
            b.op('g', veccol(W1C, L1W), 'gd', 16)
            b.op('g', veccol(B1C, L1B), 'gd', 16)
            b.op('g', veccol(W2C, L2W), 'gd', 16)
            b.op('g', veccol(B2C, L2B), 'gd', 16)
            b.op('g', lambda E: E.dma_start(
                UROW, bass.AP(TFV, 0, [[C, 1], [1, C]])), 'gd', 16)
            LD_ALL = b.cnt['gd']
            b.op('g', lambda E: E.memset(CK[:, :], 1.0), 'gk')
            b.op('g', lambda E: E.memset(CA[:, :], 0.0), 'gk')
            b.op('g', lambda E: E.memset(SCL[:, :], 0.0), 'gk')
            G_init = b.op('g', lambda E: E.memset(CZ[:, :], 0.0), 'gk')

            b.w('s', 'gd', LD_ALL)
            b.op('s', lambda E: E.activation(OPTS[:, :], TSC[:, :], AF.Copy, bias=1.0), 'sk')
            b.op('s', lambda E: E.activation(NTS[:, :], TSC[:, :], AF.Copy, scale=-1.0), 'sk')
            b.op('s', lambda E: E.activation(OMTS[:, :], TSC[:, :], AF.Copy, scale=-1.0, bias=1.0), 'sk')
            b.op('s', lambda E: E.activation(TSB[:, :], TSC[:, :], AF.Copy), 'sk')
            S_eu = b.op('s', lambda E: E.activation(EROW, UROW, AF.Exp), 'sk')
            b.w('v', 'sk', S_eu)
            V_er = b.op('v', lambda E: E.tensor_scalar(
                EROW, EROW, -1.0, None, OP.add), 'vk')
            b.w('y', 'vk', V_er)
            Y_eu = b.op('y', lambda E: E.dma_start(eud.ap(), EROW), 'yd', 16)
            b.w('g', 'yd', Y_eu)
            G_eum = b.op('g', lambda E: E.dma_start(
                EUM1[:, :], bass.AP(eud, 0, [[0, 128], [1, C]])), 'gd', 16)
            pp_chunks = []
            for src, dst, nel in ((WCKR, WCK8D, NHC * 128 * NC8 * 128),
                                  (WCVT, WCV8D, H * C)):
                for q in range(8):
                    ck = nel // 8
                    pp_chunks.append(lambda src=src, dst=dst, off=q * ck,
                                     w=ck // 128: b.op(
                        'g', lambda E, src=src, dst=dst, off=off, w=w:
                        E.dma_start(
                            bass.AP(dst, off, [[w, 128], [1, w]]),
                            bass.AP(src, off, [[w, 128], [1, w]])),
                        'pp', 16))
            PP_K = 8 * 16
            PP_ALL = 16 * 16

            V_st = {}
            S_st = {}
            for j in range(nt):
                xs_ = 'xb%d' % (j % 2)
                Yx0 = b.op('y', lambda E, j=j: E.dma_start(
                    s2t(j), X[j * 128:(j + 1) * 128, :]), xs_, 16)
                b.w('v', xs_, Yx0)
                V_st[j] = b.op('v', lambda E, j=j: E.reduce_sum(
                    SUM[:, j:j + 1], s2t(j), AX), 'vk')
                b.w('s', xs_, Yx0)
                S_st[j] = b.op('s', lambda E, j=j: E.activation(
                    SCRA[:, 2048:3072],
                    s2t(j), AF.Square,
                    accum_out=SSQ[:, j:j + 1]), 'sk')

            def rs_chain(sumt, sqt, rs_outs, bias_outs, double_ln):
                M0c, T1c, M2c, V0c, L0c, LCc = (STA[:, i, :] for i in range(6))
                b.op('s', lambda E: E.activation(M0c, sumt[:, :], AF.Copy, scale=1.0 / C), 'sk')
                b.op('s', lambda E: E.activation(T1c, sqt[:, :], AF.Copy, scale=1.0 / C), 'sk')
                S_m = b.op('s', lambda E: E.activation(M2c, M0c, AF.Square), 'sk')
                b.w('v', 'sk', S_m)
                V_v0 = b.op('v', lambda E: E.tensor_sub(V0c, T1c, M2c), 'vk')
                b.w('s', 'vk', V_v0)
                b.op('s', lambda E: E.activation(L0c, V0c, AF.Ln, bias=EPS), 'sk')
                S_l = b.op('s', lambda E: E.activation(
                    rs_outs[0][:, :], L0c, AF.Exp, scale=-0.5), 'sk')
                if double_ln:
                    b.op('s', lambda E: E.activation(
                        LCc, V0c, AF.Ln, scale=1.0 + EPS, bias=EPS * EPS), 'sk')
                    S_l = b.op('s', lambda E: E.activation(
                        rs_outs[1][:, :], LCc, AF.Exp, scale=-0.5), 'sk')
                b.w('v', 'sk', S_l)
                tok = None
                for rs_t, bi_t in zip(rs_outs, bias_outs):
                    tok = b.op('v', lambda E, r=rs_t, o=bi_t: E.scalar_tensor_tensor(
                        o[:, :], M0c, -1.0, r[:, :], OP.mult, OP.mult), 'vk')
                return tok

            b.w('s', 'vk', V_st[nt - 1])
            b.w('s', 'sk', S_st[nt - 1])
            V_chain = rs_chain(SUM, SSQ, [RS0, RSC], [B0N, BCN], True)

            T_ttr = {}
            S_hev = {}
            V_xxx = {}
            V_done = {}
            S_done = {}
            hist = []
            T_r_lastslice = {}

            pend1 = [None]
            pend2 = [None]

            W = SL1 * 128

            def emit_xxt(s1, chunk):
                par = s1 % 2
                tiles = [s1 * SL1 + i for i in range(SL1)]
                if chunk in (0, 1):
                    j = tiles[chunk]
                    b.w('s', 'vk', V_chain)
                    b.w('s', 'tk', T_ttr.get(j - 2, 0))
                    S_done[j] = b.op('s', lambda E, j=j: E.activation(
                        SCRA[:, (j % 2) * 1024:(j % 2 + 1) * 1024],
                        s2t(j), AF.Identity,
                        scale=RSC[:, j:j + 1], bias=BCN[:, j:j + 1]), 'sk')
                if chunk in (1, 2):
                    j = tiles[chunk - 1]
                    b.w('t', 'sk', S_done[j])
                    b.w('t', 'sk', S_hev.get(j - 1, 0))
                    for i in range(NC8):
                        T_ttr[j] = b.op('t', lambda E, j=j, i=i: E.matmul(
                            PT[:, i * 128:(i + 1) * 128],
                            SCRA[:, (j % 2) * 1024 + i * 128:(j % 2) * 1024 + (i + 1) * 128],
                            IDB[:, :], start=True, stop=True),
                            'tk' if i == NC8 - 1 else None)
                    b.w('s', 'tk', T_ttr[j])
                    b.w('s', 'vk', V_xxx.get(s1 - 1, 0))
                    for i in range(NC8):
                        S_hev[j] = b.op('s', lambda E, j=j, i=i: E.activation(
                            HT[:, i, (j % SL1) * 128:(j % SL1 + 1) * 128],
                            PT[:, i * 128:(i + 1) * 128], AF.Identity,
                            scale=W1C[:, i:i + 1], bias=B1C[:, i:i + 1]),
                            'sk' if i == NC8 - 1 else None)
                if chunk != 3:
                    return
                S_t1 = None
                for i in range(NC8):
                    S_t1 = b.op('s', lambda E, i=i: E.activation(
                        T1F[:, i, :],
                        HT[:, i, :], AF.Copy, scale=OPTS[:, i:i + 1]),
                        'sk' if i == NC8 - 1 else None)
                b.w('v', 'sk', S_t1)
                b.w('v', 'tk', T_r_lastslice.get(s1 - 2, 0))
                pbsrc = TSB if s1 == 0 else PBB
                for i in range(NC8):
                    b.op('v', lambda E, i=i, par=par: E.scalar_tensor_tensor(
                        XXT[:, par, i, 1:W], HT[:, i, 0:W - 1], NTS[:, i:i + 1],
                        T1F[:, i, 1:W], OP.mult, OP.add), None)
                    V_xxx[s1] = b.op('v', lambda E, i=i, par=par, pbsrc=pbsrc:
                        E.scalar_tensor_tensor(
                            XXT[:, par, i, 0:1], pbsrc[:, i:i + 1], NTS[:, i:i + 1],
                            T1F[:, i, 0:1], OP.mult, OP.add),
                        'vk' if i == NC8 - 1 else None)
                b.w('s', 'vk', V_xxx[s1])
                for i in range(NC8):
                    b.op('s', lambda E, i=i: E.activation(
                        PBB[:, i:i + 1], HT[:, i, W - 1:W], AF.Copy),
                        'sk' if i == NC8 - 1 else None)

            for ch in range(4):
                emit_xxt(0, ch)
            for s1 in range(NS1):
                par = s1 % 2
                tiles = [s1 * SL1 + i for i in range(SL1)]
                for j in tiles:
                    for h in (0, 1):
                        hs = slice(h * 512, (h + 1) * 512)
                        xoff = (j % SL1) * 128
                        it = 2 * j + h
                        (F0, F1, F2, F3), (B_te, B_den, B_a, B_t2, B_dent,
                                           B_ig, B_z, B_g) = pools(it)
                        p1 = hist[-1] if len(hist) >= 1 else {}
                        p2 = hist[-2] if len(hist) >= 2 else {}
                        p3 = hist[-3] if len(hist) >= 3 else {}
                        cur = {}
                        b.w('t', 'sk', p1.get('e1', 0))
                        b.w('t', 'vk', V_xxx[s1])
                        b.w('t', 'gd', LD_ALL)
                        b.w('t', 'ld0', LW0)
                        b.w('t', 'ld1', LW1)
                        b.w('t', 'ld2', LW2)
                        b.w('t', 'gk', G_init)
                        for ci in range(NC8):
                            T_k = b.op('t', lambda E, ci=ci, par=par, xoff=xoff, h=h:
                                E.matmul(PKV[:, 0, :], XXT[:, par, ci, xoff:xoff + 128],
                                         W3[:, 0, ci, h * 512:(h + 1) * 512],
                                         start=(ci == 0), stop=(ci == NC8 - 1)),
                                'tk' if ci == NC8 - 1 else None)
                        b.w('t', 'vk', p1.get('ekv', 0))
                        for ci in range(NC8):
                            T_v = b.op('t', lambda E, ci=ci, par=par, xoff=xoff, h=h:
                                E.matmul(PKV[:, 1, :], XXT[:, par, ci, xoff:xoff + 128],
                                         W3[:, 1, ci, h * 512:(h + 1) * 512],
                                         start=(ci == 0), stop=(ci == NC8 - 1)),
                                'tk' if ci == NC8 - 1 else None)
                        b.w('s', 'tk', T_k)
                        b.w('s', 'gd', G_eum)
                        b.w('s', 'vk', p2.get('dent', 0))
                        b.w('s', 'gk', p2.get('gmax', 0))
                        S_ek = b.op('s', lambda E, hs=hs: E.activation(
                            EKB[:, hs], PKV[:, 0, :], AF.Exp), 'sk')
                        cur['ek'] = S_ek
                        b.w('t', 'sk', S_ek)
                        for ci in range(NC8):
                            T_r = b.op('t', lambda E, ci=ci, par=par, xoff=xoff, h=h:
                                E.matmul(PKV[:, 0, :], XXT[:, par, ci, xoff:xoff + 128],
                                         W3[:, 2, ci, h * 512:(h + 1) * 512],
                                         start=(ci == 0), stop=(ci == NC8 - 1)),
                                'tk' if ci == NC8 - 1 else None)
                        b.w('s', 'tk', T_r)
                        S_e1 = b.op('s', lambda E, hs=hs: E.activation(
                            E1B[:, hs], PKV[:, 0, :], AF.Exp, scale=-1.0), 'sk')
                        cur['e1'] = S_e1
                        b.w('t', 'vk', p1.get('dd', 0))
                        b.w('t', 'vk', p2.get('ck', 0))
                        b.op('t', lambda E, hs=hs: E.matmul(
                            PS[:, :], TRIB[:, :], EKB[:, hs],
                            start=True, stop=False), None)
                        T_s = b.op('t', lambda E, h=h: E.matmul(
                            PS[:, :], ONEF[:, :], CK[0:1, h * 512:(h + 1) * 512],
                            start=False, stop=True), 'tk')
                        b.w('t', 'vk', p1.get('ck', 0))
                        T_bk = b.op('t', lambda E, hs=hs: E.matmul(
                            PCS[0:1, :], TRIB[:, 127:128], EKB[:, hs],
                            start=True, stop=True), 'tk')
                        b.w('v', 'tk', T_s)
                        b.w('v', 'gk', p3.get('xin', 0))
                        S_dd = b.op('v', lambda E, F0=F0: E.tensor_scalar(
                            F0, PS[:, :], 1.0, None, OP.mult), 'vk')
                        cur['dd'] = S_dd
                        V_stt = b.op('v', lambda E, hs=hs, F0=F0, F1=F1: E.scalar_tensor_tensor(
                            F1, EKB[:, hs], -1.0, F0, OP.mult, OP.add), 'vk')
                        b.w('v', 'tk', T_v)
                        b.w('v', 'sk', S_ek)
                        V_ekv = b.op('v', lambda E, hs=hs: E.tensor_mul(
                            EKV[:, hs], EKB[:, hs], PKV[:, 1, :]), 'vk')
                        cur['ekv'] = V_ekv
                        b.w('g', 'sk', S_ek)
                        b.w('g', 'vk', p1.get('dent', 0))
                        G_te = b.op('g', lambda E, hs=hs, B_te=B_te: E.tensor_mul(
                            B_te, EKB[:, hs], EUM1[:, hs]), 'gk')
                        cur['gmax'] = G_te
                        b.w('g', 'vk', V_stt)
                        G_den = b.op('g', lambda E, F0=F0, B_te=B_te, B_den=B_den:
                                     E.tensor_add(B_den, F0, B_te), 'gk')
                        b.w('v', 'sk', p2.get('sg', 0))
                        V_max = b.op('v', lambda E, hs=hs, F1=F1, F2=F2: E.tensor_max(
                            F2, F1, EKB[:, hs]), 'vk')
                        b.w('v', 'tk', T_bk)
                        V_ck = b.op('v', lambda E, h=h: E.tensor_add(
                            CK[0:1, h * 512:(h + 1) * 512],
                            CK[0:1, h * 512:(h + 1) * 512], PCS[0:1, :]), 'vk')
                        cur['ck'] = V_ck
                        b.w('s', 'vk', V_max)
                        b.op('s', lambda E, F2=F2, F3=F3: E.activation(F3, F2, AF.Ln), None)
                        S_ln = b.op('s', lambda E, F0=F0, F1=F1: E.activation(F1, F0, AF.Ln), 'sk')
                        b.w('v', 'sk', S_ln)
                        b.w('v', 'sk', S_e1)
                        V_a = b.op('v', lambda E, F1=F1, F3=F3, B_a=B_a:
                                   E.tensor_sub(B_a, F1, F3), 'vk')
                        cur['a'] = V_a
                        b.w('v', 'gk', G_den)
                        b.op('v', lambda E, hs=hs, B_den=B_den, B_dent=B_dent, E1B=E1B:
                             E.scalar_tensor_tensor(
                                 B_dent, E1B[:, hs], 1.0, B_den, OP.add, OP.mult), None)
                        V_pre = b.op('v', lambda E, j=j, h=h, hs=hs, F3=F3: E.tensor_scalar(
                            F3, s2th(j, h),
                            RS0[:, j:j + 1], B0N[:, j:j + 1], OP.mult, OP.add), 'vk')
                        cur['dent'] = V_pre
                        if h == 1:
                            V_done[j] = V_pre

                        def stage_b1(j=j, h=h, hs=hs, F0=F0, F1=F1, F2=F2, F3=F3,
                                     B_te=B_te, B_den=B_den, B_a=B_a, B_t2=B_t2,
                                     B_dent=B_dent, B_ig=B_ig, B_z=B_z, B_g=B_g,
                                     cur=cur, p1=p1, p2=p2):
                            b.w('t', 'vk', cur['a'])
                            b.w('t', 'vk', p1.get('lng', 0))
                            b.w('t', 'vk', p2.get('ca', 0))
                            b.op('t', lambda E: E.matmul(
                                PA[:, :], TRIB[:, :], B_a, start=True, stop=False), None)
                            T_a = b.op('t', lambda E, h=h: E.matmul(
                                PA[:, :], ONEF[:, :], CA[0:1, h * 512:(h + 1) * 512],
                                start=False, stop=True), 'tk')
                            b.w('t', 'vk', p1.get('ca', 0))
                            T_ba = b.op('t', lambda E: E.matmul(
                                PCS[32:33, :], TRIB[:, 127:128], B_a,
                                start=True, stop=True), 'tk')
                            b.w('v', 'tk', T_a)
                            V_lng = b.op('v', lambda E: E.scalar_tensor_tensor(
                                F2, B_a, -1.0, PA[:, :], OP.mult, OP.add), 'vk')
                            cur['lng'] = V_lng
                            b.w('v', 'tk', T_ba)
                            V_ca = b.op('v', lambda E, h=h: E.tensor_add(
                                CA[0:1, h * 512:(h + 1) * 512],
                                CA[0:1, h * 512:(h + 1) * 512], PCS[32:33, :]), 'vk')
                            cur['ca'] = V_ca
                            b.w('s', 'vk', V_lng)
                            b.op('s', lambda E: E.activation(B_g, F2, AF.Exp), None)
                            b.op('s', lambda E: E.activation(B_ig, F2, AF.Exp, scale=-1.0), None)
                            b.op('s', lambda E: E.activation(F1, B_dent, AF.Ln), None)
                            S_g = b.op('s', lambda E: E.activation(
                                B_t2, F1, AF.Exp, scale=-1.0), 'sk')
                            cur['sg'] = S_g
                            b.w('v', 'sk', S_g)
                            V_tn = b.op('v', lambda E, hs=hs: E.tensor_mul(
                                B_te, EKV[:, hs], EUM1[:, hs]), 'vk')
                            b.w('v', 'tk', p2.get('tz', 0))
                            V_z = b.op('v', lambda E, hs=hs: E.tensor_mul(
                                B_z, EKV[:, hs], B_ig), 'vk')
                            cur['z'] = V_z

                            def stage_b2():
                                b.w('t', 'vk', V_z)
                                b.w('t', 'vk', p1.get('tg', 0))
                                b.w('t', 'vk', p2.get('cz', 0))
                                b.op('t', lambda E: E.matmul(
                                    PZ[:, :], TRIB[:, :], B_z, start=True, stop=False), None)
                                T_z = b.op('t', lambda E, h=h: E.matmul(
                                    PZ[:, :], ONEF[:, :], CZ[0:1, h * 512:(h + 1) * 512],
                                    start=False, stop=True), 'tk')
                                b.w('t', 'vk', p1.get('cz', 0))
                                T_bz = b.op('t', lambda E: E.matmul(
                                    PCS[64:65, :], TRIB[:, 127:128], B_z,
                                    start=True, stop=True), 'tk')
                                cur['tz'] = T_bz
                                b.w('v', 'tk', T_z)
                                V_tg = b.op('v', lambda E: E.scalar_tensor_tensor(
                                    F0, B_g, 0.0, PZ[:, :], OP.bypass, OP.mult), 'vk')
                                cur['tg'] = V_tg
                                b.w('v', 'tk', T_bz)
                                V_cz = b.op('v', lambda E, h=h: E.tensor_add(
                                    CZ[0:1, h * 512:(h + 1) * 512],
                                    CZ[0:1, h * 512:(h + 1) * 512], PCS[64:65, :]), 'vk')
                                cur['cz'] = V_cz
                                b.w('g', 'vk', V_tg)
                                b.op('g', lambda E: E.tensor_add(B_a, F0, B_te), None)
                                b.op('g', lambda E: E.tensor_mul(B_dent, B_a, B_t2), None)
                                G_xin = b.op('g', lambda E, j=j, h=h: E.tensor_add(
                                    s2th(j, h), B_dent, F3), 'gk')
                                cur['xin'] = G_xin
                                cur['num'] = G_xin
                            return stage_b2

                        hist.append(cur)
                        if s1 + 1 < NS1:
                            emit_xxt(s1 + 1, it % 4)
                        if pp_chunks and it % 2 == 0:
                            pp_chunks.pop(0)()
                        nb2 = pend1[0]() if pend1[0] is not None else None
                        if pend2[0] is not None:
                            pend2[0]()
                        pend1[0] = stage_b1
                        pend2[0] = nb2
                T_r_lastslice[s1] = b.cnt['tk']
            nb2 = pend1[0]()
            if pend2[0] is not None:
                pend2[0]()
            nb2()
            while pp_chunks:
                pp_chunks.pop(0)()
            XL = b.cnt['gk']
            b.w('v', 'gk', XL)
            b.w('s', 'gk', XL)
            for j in range(nt):
                V_last = b.op('v', lambda E, j=j: E.reduce_sum(
                    XS2[:, j:j + 1], s2t(j), AX), 'vk')
                b.op('s', lambda E, j=j: E.activation(
                    SCRA[:, 2048:3072], s2t(j), AF.Square,
                    accum_out=XSQ2[:, j:j + 1]), 'sk')
            iter_join = {'vk': b.cnt['vk'], 'sk': b.cnt['sk'],
                         'gk': b.cnt['gk'], 'tk': b.cnt['tk']}
            if dbg:
                for sem, val in iter_join.items():
                    b.w('y', sem, val)
                DBG1 = b.op('y', lambda E: E.dma_start(
                    DXI.ap().rearrange("(j p) c -> p j c", p=128),
                    S2[:, :].rearrange("p (j c) -> p j c", c=C)), 'yd', 16)
            b.emit_block()

        with ExitStack() as ph2:
            e2 = ph2.enter_context
            W2T = SL2 * 128
            H2T = e2(nc.sbuf_tensor("H2T", [128, NC8, W2T], BF16))
            XX2 = e2(nc.sbuf_tensor("XX2", [128, NC8, W2T], BF16))
            KC2 = e2(nc.sbuf_tensor("KC2", [128, NHC, W2T], BF16))
            WCR8 = e2(nc.sbuf_tensor("WCR8", [128, NC8, C], BF16))
            WCKB = e2(nc.sbuf_tensor("WCKB", [128, NC8, 5, 128], BF16))
            WCVB = e2(nc.sbuf_tensor("WCVB", [128, 5, C], BF16))
            RCB = e2(nc.sbuf_tensor("RCB", [128, SL2, C], BF16))
            RHOB = e2(nc.sbuf_tensor("RHOB", [128, W2T], BF16))
            T2F = e2(nc.sbuf_tensor("T2F", [128, NC8, W2T // 2], F32))
            T3B = e2(nc.sbuf_tensor("T3B", [128, 2, C], F32))
            SCR2 = e2(nc.sbuf_tensor("SCR2", [128, 2, C], BF16))
            PALL = e2(nc.psum_tensor("PALL", [128, 8, 512], F32))

            join2 = dict(iter_join)
            for eng in ('t', 'v', 's', 'g', 'y'):
                for sem, val in join2.items():
                    b.w(eng, sem, val)
            if dbg:
                b.w('v', 'yd', DBG1)

            G_wcr = b.op('g', lambda E: E.dma_start(
                WCR8[:, :, :],
                bass.AP(WCRT, 0, [[C, 128], [128 * C, NC8], [1, C]])), 'gd', 16)

            V_ch2 = rs_chain(XS2[:, 0:nt], XSQ2[:, 0:nt], [RS2], [B2N], False)

            jr = nt - 1
            b.w('s', 'vk', V_ch2)
            S_hr = b.op('s', lambda E, jr=jr: E.activation(
                SCRA[:, 0:1024], s2t(jr), AF.Identity,
                scale=RS2[:, jr:jr + 1], bias=B2N[:, jr:jr + 1]), 'sk')
            b.w('t', 'sk', S_hr)
            for i in range(NC8):
                T_tr = b.op('t', lambda E, i=i: E.matmul(
                    PALL[:, 0:2, :].rearrange("p a b -> p (a b)")[:, i * 128:(i + 1) * 128],
                    SCRA[:, i * 128:(i + 1) * 128], IDB[:, :],
                    start=True, stop=True),
                    'tk' if i == NC8 - 1 else None)
            b.w('s', 'tk', T_tr)
            S_roll = None
            for i in range(NC8):
                S_roll = b.op('s', lambda E, i=i: E.activation(
                    ROLL[:, i:i + 1],
                    PALL[:, 0:2, :].rearrange("p a b -> p (a b)")[:, i * 128 + 127:i * 128 + 128],
                    AF.Identity, scale=W2C[:, i:i + 1], bias=B2C[:, i:i + 1]),
                    'sk' if i == NC8 - 1 else None)

            slice_join = {'tk': b.cnt['tk'], 'vk': b.cnt['vk'],
                          'sk': b.cnt['sk'], 'gk': b.cnt['gk']}
            V_x1 = {}
            V_rcvv = {}
            V_x1j = {}
            G_t3 = {}
            Yx2 = {}
            V_mm = 0
            pend_ev = [None]
            A_tok = {}
            HW2 = W2T // 2
            NPK = NS2 * NHC
            wck_tok = {}
            wck_rd = {}
            wcv_tok = {}
            wcv_rd = {}

            def wck_load(g):
                b.w('s', 'pp', PP_K)
                b.w('s', 'tk', wck_rd.get(g - 5, 0))
                wck_tok[g] = b.op('s', lambda E, hh=g % NHC, g=g: E.dma_start(
                    WCKB[:, :, g % 5, :],
                    bass.AP(WCK8D, hh * 128 * NC8 * 128,
                            [[NC8 * 128, 128], [128, NC8], [1, 128]])),
                    'wk%d' % (g % 5), 16)

            def wcv_load(g):
                b.w('y', 'pp', PP_ALL)
                b.w('y', 'tk', wcv_rd.get(g - 5, 0))
                wcv_tok[g] = b.op('y', lambda E, hh=g % NHC, g=g: E.dma_start(
                    WCVB[:, g % 5, :],
                    bass.AP(WCV8D, hh * 128 * C, [[C, 128], [1, C]])),
                    'wv%d' % (g % 5), 16)

            for g in range(5):
                wck_load(g)
            for g in range(5):
                wcv_load(g)
            for jn in (0, 1):
                Yx2[jn] = b.op('g', lambda E, jn=jn: E.dma_start(
                    XB[:, jn % 2, :], X[jn * 128:(jn + 1) * 128, :]),
                    'xg%d' % (jn % 2), 16)

            def emit_A(s2i, part):
                par = s2i % 2
                tiles = [s2i * SL2 + i for i in range(SL2)]
                prev = A_tok.get(s2i - 1)
                if part == 0:
                    hh2 = {}
                    for j in tiles[:2]:
                        b.w('s', 'tk', prev['T_tt'] if prev else slice_join['tk'])
                        b.w('s', 'vk', V_ch2)
                        hh2[j] = b.op('s', lambda E, j=j: E.activation(
                            SCRA[:, (j % 2) * 1024:(j % 2 + 1) * 1024], s2t(j),
                            AF.Identity,
                            scale=RS2[:, j:j + 1], bias=B2N[:, j:j + 1]), 'sk')
                    A_tok[s2i] = dict(hh2=hh2, Trc=0)
                    return
                if part == 1:
                    cur = A_tok[s2i]
                    if s2i >= 2:
                        b.w('t', 'vk', V_rcvv.get((s2i - 2) * SL2, 0))
                    S_hev2 = S_roll if s2i == 0 else prev['S_hev2']
                    T_tt = 0
                    for j in tiles:
                        if j - 2 in tiles:
                            b.w('s', 'tk', T_tt)
                            b.w('s', 'vk', V_ch2)
                            cur['hh2'][j] = b.op('s', lambda E, j=j: E.activation(
                                SCRA[:, (j % 2) * 1024:(j % 2 + 1) * 1024],
                                s2t(j), AF.Identity,
                                scale=RS2[:, j:j + 1], bias=B2N[:, j:j + 1]), 'sk')
                        b.w('t', 'sk', cur['hh2'][j])
                        b.w('t', 'sk', S_hev2)
                        for i in range(NC8):
                            T_tt = b.op('t', lambda E, j=j, i=i: E.matmul(
                                PALL[:, 0:2, :].rearrange("p a b -> p (a b)")[:, i * 128:(i + 1) * 128],
                                SCRA[:, (j % 2) * 1024 + i * 128:(j % 2) * 1024 + (i + 1) * 128],
                                IDB[:, :], start=True, stop=True),
                                'tk' if i == NC8 - 1 else None)
                        b.w('s', 'tk', T_tt)
                        b.w('s', 'vk', slice_join['vk'] if s2i == 0
                            else prev['V_xx2'])
                        for i in range(NC8):
                            S_hev2 = b.op('s', lambda E, j=j, i=i: E.activation(
                                H2T[:, i, (j % SL2) * 128:(j % SL2 + 1) * 128],
                                PALL[:, 0:2, :].rearrange("p a b -> p (a b)")[:, i * 128:(i + 1) * 128],
                                AF.Identity, scale=W2C[:, i:i + 1], bias=B2C[:, i:i + 1]),
                                'sk' if i == NC8 - 1 else None)
                    cur['S_hev2'] = S_hev2
                    cur['T_tt'] = T_tt
                    return
                cur = A_tok[s2i]
                pbsrc = ROLL if s2i == 0 else PB2
                V_xx2 = None
                for hv in range(2):
                    if hv == 1:
                        b.w('s', 'vk', V_xx2)
                    S_t1 = None
                    for i in range(NC8):
                        S_t1 = b.op('s', lambda E, i=i, hv=hv: E.activation(
                            T2F[:, i, :],
                            H2T[:, i, hv * HW2:(hv + 1) * HW2], AF.Copy,
                            scale=OMTS[:, i:i + 1]),
                            'sk' if i == NC8 - 1 else None)
                    b.w('v', 'sk', S_t1)
                    if hv == 0 and s2i >= 1:
                        b.w('v', 'tk', prev['Trc'])
                    for i in range(NC8):
                        lo = hv * HW2
                        b.op('v', lambda E, i=i, lo=lo: E.scalar_tensor_tensor(
                            XX2[:, i, lo + 1:lo + HW2],
                            H2T[:, i, lo:lo + HW2 - 1], TSC[:, i:i + 1],
                            T2F[:, i, 1:HW2], OP.mult, OP.add), None)
                        shsrc = (pbsrc[:, i:i + 1] if hv == 0
                                 else H2T[:, i, HW2 - 1:HW2])
                        V_xx2 = b.op('v', lambda E, i=i, lo=lo, shsrc=shsrc:
                            E.scalar_tensor_tensor(
                                XX2[:, i, lo:lo + 1], shsrc, TSC[:, i:i + 1],
                                T2F[:, i, 0:1], OP.mult, OP.add),
                            'vk' if i == NC8 - 1 else None)
                b.w('s', 'vk', V_xx2)
                for i in range(NC8):
                    b.op('s', lambda E, i=i: E.activation(
                        PB2[:, i:i + 1], H2T[:, i, W2T - 1:W2T], AF.Copy),
                        'sk' if i == NC8 - 1 else None)
                cur['V_xx2'] = V_xx2

            for pp_ in range(3):
                emit_A(0, pp_)
            for s2i in range(NS2):
                tiles = [s2i * SL2 + i for i in range(SL2)]
                V_xx2 = A_tok[s2i]['V_xx2']
                V_kc2 = {}
                S_rho = {}
                Tkc = {}
                b.w('t', 'vk', V_rcvv.get((s2i - 1) * SL2 + 1, 0))
                for hh in range(NHC):
                    g = s2i * NHC + hh
                    b.w('t', 'wk%d' % (g % 5), wck_tok[g])
                    b.w('t', 'vk', V_xx2)
                    if hh >= 2:
                        b.w('t', 'sk', S_rho[hh - 2])
                    for ci in range(NC8):
                        Tkc[hh] = b.op('t', lambda E, hh=hh, g=g, ci=ci: E.matmul(
                            PALL[:, 2 + hh % 2, :], WCKB[:, ci, g % 5, :],
                            XX2[:, ci, :], start=(ci == 0), stop=(ci == NC8 - 1)),
                            'tk' if ci == NC8 - 1 else None)
                    wck_rd[g] = Tkc[hh]
                    if hh == 4 and pend_ev[0] is not None:
                        pend_ev[0]()
                        pend_ev[0] = None
                    if hh == 16 and s2i + 1 < NS2:
                        emit_A(s2i + 1, 0)
                    b.w('s', 'tk', Tkc[hh])
                    if hh >= 1:
                        b.w('s', 'vk', V_kc2[hh - 1])
                    S_rho[hh] = b.op('s', lambda E, hh=hh: E.activation(
                        RHOB[:, :], PALL[:, 2 + hh % 2, :], AF.Relu), 'sk')
                    if g + 5 < NPK:
                        wck_load(g + 5)
                    b.w('v', 'sk', S_rho[hh])
                    V_kc2[hh] = b.op('v', lambda E, hh=hh: E.tensor_mul(
                        KC2[:, hh, :], RHOB[:, :], RHOB[:, :]), 'vk')
                if s2i + 1 < NS2:
                    emit_A(s2i + 1, 1)
                S_rc = None
                Trc = {}
                Trc_s = {}
                b.w('s', 'vk', V_rcvv.get(s2i * SL2 - 1, 0))
                b.w('t', 'vk', V_rcvv.get((s2i - 1) * SL2 + 2, 0))
                for tq in range(SL2):
                    for hf2 in range(2):
                        idx = tq * 2 + hf2
                        if idx >= 2:
                            b.w('t', 'sk', Trc_s[idx - 2])
                        for ci in range(NC8):
                            Trc[idx] = b.op('t', lambda E, tq=tq, hf2=hf2, ci=ci:
                                E.matmul(PALL[:, 4 + (tq * 2 + hf2) % 2, :],
                                         XX2[:, ci, tq * 128:(tq + 1) * 128],
                                         WCR8[:, ci, hf2 * 512:(hf2 + 1) * 512],
                                         start=(ci == 0), stop=(ci == NC8 - 1)),
                                'tk' if ci == NC8 - 1 else None)
                        b.w('s', 'tk', Trc[idx])
                        Trc_s[idx] = b.op('s', lambda E, tq=tq, hf2=hf2: E.activation(
                            RCB[:, tq, hf2 * 512:(hf2 + 1) * 512],
                            PALL[:, 4 + (tq * 2 + hf2) % 2, :], AF.Sigmoid), 'sk')
                        S_rc = Trc_s[idx]
                A_tok[s2i]['Trc'] = Trc[2 * SL2 - 1]
                if s2i + 1 < NS2:
                    emit_A(s2i + 1, 2)
                b.w('t', 'vk', V_kc2[NHC - 1])
                b.w('t', 'vk', V_rcvv.get((s2i - 1) * SL2 + 3, 0))
                b.w('t', 'sk', S_rc)
                if s2i + 1 < NS2:
                    b.w('t', 'sk', A_tok[s2i + 1]['S_hev2'])
                Tvv = {}
                for hh in range(NHC):
                    g = s2i * NHC + hh
                    b.w('t', 'wv%d' % (g % 5), wcv_tok[g])
                    for tq in range(SL2):
                        for hf2 in range(2):
                            Tvv[hh] = b.op('t', lambda E, hh=hh, g=g, tq=tq, hf2=hf2:
                                E.matmul(PALL[:, tq * 2 + hf2, :],
                                         KC2[:, hh, tq * 128:(tq + 1) * 128],
                                         WCVB[:, g % 5, hf2 * 512:(hf2 + 1) * 512],
                                         start=(hh == 0), stop=(hh == NHC - 1)),
                                'tk' if (tq == SL2 - 1 and hf2 == 1) else None)
                    wcv_rd[g] = Tvv[hh]
                    if g + 5 < NPK:
                        wcv_load(g + 5)
                T_vv = Tvv[NHC - 1]
                for ii, j in enumerate(tiles[:2]):
                    b.w('v', 'tk', T_vv)
                    V_rcvv[j] = b.op('v', lambda E, tq=ii, j=j: E.tensor_mul(
                        SCR2[:, j % 2, :], RCB[:, tq, :],
                        PALL[:, tq * 2:tq * 2 + 2, :].rearrange("p a b -> p (a b)")), 'vk')
                for j in tiles[:2]:
                    b.w('g', 'xg%d' % (j % 2), Yx2[j])
                    b.w('g', 'vk', V_x1j.get(j - 2, 0))
                    G_t3[j] = b.op('g', lambda E, j=j: E.tensor_add(
                        T3B[:, j % 2, :], XB[:, j % 2, :], s2t(j)), 'gk')
                for j in tiles[:2]:
                    b.w('v', 'gk', G_t3[j])
                    V_x1j[j] = b.op('v', lambda E, j=j: E.scalar_tensor_tensor(
                        s2t(j), SCR2[:, j % 2, :], 0.0, T3B[:, j % 2, :],
                        OP.bypass, OP.add,
                        accum_out=LSUM[:, j:j + 1]), 'vk')
                for j in tiles[:2]:
                    b.w('s', 'vk', V_x1j[j])
                    b.op('s', lambda E, j=j: E.activation(
                        SCRA[:, 2048:3072], s2t(j), AF.Square,
                        accum_out=LSQ[:, j:j + 1]), 'sk')
                V_x1[s2i] = V_x1j[tiles[1]]

                def evac2(s2i=s2i, tiles=tiles):
                    for ii, j in enumerate(tiles[2:]):
                        tq = 2 + ii
                        b.w('g', 'vk', V_x1j[j - 2])
                        Yx2[j] = b.op('g', lambda E, j=j: E.dma_start(
                            XB[:, j % 2, :], X[j * 128:(j + 1) * 128, :]),
                            'xg%d' % (j % 2), 16)
                        b.w('v', 'tk', T_vv)
                        V_rcvv[j] = b.op('v', lambda E, tq=tq, j=j: E.tensor_mul(
                            SCR2[:, j % 2, :], RCB[:, tq, :],
                            PALL[:, tq * 2:tq * 2 + 2, :].rearrange("p a b -> p (a b)")), 'vk')
                    for j in tiles[2:]:
                        b.w('g', 'xg%d' % (j % 2), Yx2[j])
                        G_t3[j] = b.op('g', lambda E, j=j: E.tensor_add(
                            T3B[:, j % 2, :], XB[:, j % 2, :], s2t(j)), 'gk')
                    for j in tiles[2:]:
                        b.w('v', 'gk', G_t3[j])
                        V_x1j[j] = b.op('v', lambda E, j=j: E.scalar_tensor_tensor(
                            s2t(j), SCR2[:, j % 2, :], 0.0, T3B[:, j % 2, :],
                            OP.bypass, OP.add,
                            accum_out=LSUM[:, j:j + 1]), 'vk')
                    for j in tiles[2:]:
                        b.w('s', 'vk', V_x1j[j])
                        b.op('s', lambda E, j=j: E.activation(
                            SCRA[:, 2048:3072], s2t(j), AF.Square,
                            accum_out=LSQ[:, j:j + 1]), 'sk')
                    nonlocal V_mm
                    for j in tiles:
                        b.op('v', lambda E, j=j: E.reduce_max(
                            MXT[:, j:j + 1], s2t(j), AX), None)
                        V_mm = b.op('v', lambda E, j=j: E.tensor_reduce(
                            MNT[:, j:j + 1], s2t(j), AX, OP.min), 'vk')
                    if s2i + 1 < NS2:
                        for jn in [(s2i + 1) * SL2, (s2i + 1) * SL2 + 1]:
                            Yx2[jn] = b.op('g', lambda E, jn=jn: E.dma_start(
                                XB[:, jn % 2, :],
                                X[jn * 128:(jn + 1) * 128, :]),
                                'xg%d' % (jn % 2), 16)
                pend_ev[0] = evac2
                slice_join = {'tk': b.cnt['tk'], 'vk': b.cnt['vk'],
                              'sk': b.cnt['sk'], 'gk': b.cnt['gk']}

            if pend_ev[0] is not None:
                pend_ev[0]()
                pend_ev[0] = None
            slice_join = {'tk': b.cnt['tk'], 'vk': b.cnt['vk'],
                          'sk': b.cnt['sk'], 'gk': b.cnt['gk']}

            if dbg:
                for sem, val in slice_join.items():
                    b.w('y', sem, val)
                DBG2 = b.op('y', lambda E: E.dma_start(
                    DX1.ap().rearrange("(j p) c -> p j c", p=128),
                    S2[:, :].rearrange("p (j c) -> p j c", c=C)), 'yd', 16)
            b.emit_block()

        with ExitStack() as ph3:
            e3 = ph3.enter_context
            HFP = e3(nc.sbuf_tensor("HFP", [128, 2, C], F32))
            WFS = e3(nc.sbuf_tensor("WFS", [128, NC8, C], F32))
            WQ8 = e3(nc.sbuf_tensor("WQ8", [128, NC8, C], FP8))
            QT = e3(nc.sbuf_tensor("QT", [128, 2, NC8, 128], FP8))
            QTK = e3(nc.sbuf_tensor("QTK", [128, 2, C], FP8))
            IDF8 = e3(nc.sbuf_tensor("IDF8", [128, 128], FP8))
            FBB = e3(nc.sbuf_tensor("FBB", [128, C], F32))
            OF3 = e3(nc.sbuf_tensor("OF3", [128, 2, C], F32))
            UBB = e3(nc.sbuf_tensor("UBB", [128, 2, C], BF16))
            OB = e3(nc.sbuf_tensor("OB", [128, 2, C], F32))
            SWH = e3(nc.sbuf_tensor("SWH", [128, 1], F32))
            SXH = e3(nc.sbuf_tensor("SXH", [128, 1], F32))
            SCC = e3(nc.sbuf_tensor("SCC", [128, 1], F32))
            WAM = e3(nc.sbuf_tensor("WAM", [128, 1], F32))
            PT3 = e3(nc.psum_tensor("PT3", [128, 2, C], F32))
            PAC = e3(nc.psum_tensor("PAC", [128, 2, C], F32))

            for eng in ('t', 'v', 's', 'g', 'y'):
                for sem, val in slice_join.items():
                    b.w(eng, sem, val)
            G_id8 = b.op('g', lambda E: E.dma_start(IDF8[:, :], IDD.ap()), 'gd', 16)
            if dbg:
                b.w('y', 'yd', DBG2)
                b.w('s', 'yd', DBG2)

            V_ch3 = rs_chain(LSUM, LSQ, [RSL], [BLN], False)
            Y_wf = b.op('y', lambda E: E.dma_start(
                WFS[:, :, :],
                bass.AP(WFT, 0, [[C, 128], [128 * C, NC8], [1, C]])), 'yd', 16)
            G_fb = b.op('g', lambda E: E.dma_start(
                FBB[:, :], bass.AP(FBV, 0, [[0, 128], [1, C]])), 'gd', 16)
            b.w('v', 'yd', Y_wf)
            V_wa = b.op('v', lambda E: E.reduce_max(
                WAM[:, :], WFS[:, :, :].rearrange("p a b -> p (a b)"), AX,
                apply_absolute_value=True), 'vk')
            b.w('y', 'vk', V_wa)
            Y_wr = b.op('y', lambda E: E.dma_start(AROW[:, :], WAM[:, 0:1]), 'yd', 16)
            b.w('v', 'yd', Y_wr)
            V_wm = b.op('v', lambda E: E.reduce_max(SCL[0:1, 0:1], AROW[:, :], AX), 'vk')
            S_hf = {}

            def sw_step(i):
                if i == 0:
                    b.w('s', 'vk', V_wm)
                    b.op('s', lambda E: E.activation(SCL[0:1, 1:2], SCL[0:1, 0:1], AF.Copy), 'sk')
                elif i == 1:
                    b.w('v', 'sk', b.cnt['sk'])
                    b.op('v', lambda E: E.tensor_scalar(
                        SCL[0:1, 2:3], SCL[0:1, 1:2], 1e-12, None, OP.max), 'vk')
                elif i == 2:
                    b.w('s', 'vk', b.cnt['vk'])
                    b.op('s', lambda E: E.activation(SCL[0:1, 3:4], SCL[0:1, 2:3], AF.Copy), 'sk')
                elif i == 3:
                    b.w('v', 'sk', b.cnt['sk'])
                    b.op('v', lambda E: E.reciprocal(SCL[0:1, 4:5], SCL[0:1, 3:4]), 'vk')
                elif i == 4:
                    b.w('s', 'vk', b.cnt['vk'])
                    b.op('s', lambda E: E.activation(SCL[0:1, 5:6], SCL[0:1, 4:5], AF.Copy), 'sk')
                elif i == 5:
                    b.w('v', 'sk', b.cnt['sk'])
                    b.op('v', lambda E: E.tensor_scalar(
                        SCL[0:1, 6:7], SCL[0:1, 5:6], 448.0, 448.0, OP.mult, OP.min), 'vk')
                elif i == 6:
                    b.w('s', 'vk', b.cnt['vk'])
                    b.op('s', lambda E: E.activation(
                        SCL[0:1, 7:8], SCL[0:1, 6:7], AF.Copy, scale=0.5), 'sk')

            sw_step(0)
            b.w('v', 'vk', V_ch3)
            b.w('v', 'vk', V_mm)
            b.op('v', lambda E: E.tensor_mul(AMX[:, :], MXT[:, :], RSL[:, :]), None)
            sw_step(1)
            b.op('v', lambda E: E.tensor_mul(MNT[:, :], MNT[:, :], RSL[:, :]), None)
            b.op('v', lambda E: E.tensor_add(AMX[:, :], AMX[:, :], BLN[:, :]), None)
            sw_step(2)
            b.op('v', lambda E: E.tensor_add(MNT[:, :], MNT[:, :], BLN[:, :]), None)
            b.op('v', lambda E: E.reduce_max(
                MXT[:, 0:1], AMX[:, :], AX, apply_absolute_value=True), None)
            sw_step(3)
            b.op('v', lambda E: E.reduce_max(
                MNT[:, 0:1], MNT[:, :], AX, apply_absolute_value=True), None)
            sw_step(4)
            V_a1 = b.op('v', lambda E: E.tensor_max(
                WAM[:, 0:1], MXT[:, 0:1], MNT[:, 0:1]), 'vk')
            b.w('y', 'vk', V_a1)
            Y_ar = b.op('y', lambda E: E.dma_start(AROW[:, :], WAM[:, 0:1]), 'yd', 16)
            b.w('v', 'yd', Y_ar)
            V_ams = b.op('v', lambda E: E.reduce_max(SCL[0:1, 16:17], AROW[:, :], AX), 'vk')
            sw_step(5)
            b.w('s', 'vk', V_ams)
            S_amc = b.op('s', lambda E: E.activation(
                AMSB[0:1, 0:1], SCL[0:1, 16:17], AF.Copy), 'sk')
            b.op('g', lambda E: E.memset(AMSB[0:1, 1:8], 0.0), 'gk')
            b.w('g', 'sk', S_amc)
            G_ci = b.op('g', lambda E: E.dma_start(cc_in.ap(), AMSB[:, :]), 'gd', 16)
            b.w('g', 'gd', G_ci)
            G_cc = b.op('g', lambda E: E.collective_compute(
                "AllReduce", OP.max, replica_groups=[list(range(8))],
                ins=[cc_in.ap().opt()], outs=[cc_out.ap().opt()]), 'cc')
            sw_step(6)
            b.w('y', 'sk', b.cnt['sk'])
            Y_scA = b.op('y', lambda E: E.dma_start(
                bass.AP(scald, 0, [[8, 1], [1, 8]]), SCL[0:1, 0:8]), 'yd', 16)
            b.w('g', 'yd', Y_scA)
            LW_swh = b.op('g', lambda E: E.dma_start(
                SWH[:, :], bass.AP(scald, 7, [[0, 128], [1, 1]])), 'ld0', 16)
            b.w('v', 'ld0', LW_swh)
            V_wq = b.op('v', lambda E: E.tensor_scalar(
                WQ8[:, :, :].rearrange("p a b -> p (a b)"),
                WFS[:, :, :].rearrange("p a b -> p (a b)"),
                SWH[:, 0:1], None, OP.mult), 'vk')
            b.w('g', 'cc', G_cc)
            G_gm = b.op('g', lambda E: E.dma_start(GMX[:, :], cc_out.ap()), 'gd', 16)
            b.w('v', 'gd', G_gm)
            b.op('v', lambda E: E.tensor_scalar(
                SCL[0:1, 8:9], GMX[0:1, 0:1], 1e-12, None, OP.max), 'vk')
            b.w('s', 'vk', b.cnt['vk'])
            b.op('s', lambda E: E.activation(SCL[0:1, 9:10], SCL[0:1, 8:9], AF.Copy), 'sk')
            b.w('v', 'sk', b.cnt['sk'])
            b.op('v', lambda E: E.reciprocal(SCL[0:1, 10:11], SCL[0:1, 9:10]), 'vk')
            b.w('s', 'vk', b.cnt['vk'])
            b.op('s', lambda E: E.activation(SCL[0:1, 11:12], SCL[0:1, 10:11], AF.Copy), 'sk')
            b.w('v', 'sk', b.cnt['sk'])
            b.op('v', lambda E: E.tensor_scalar(
                SCL[0:1, 12:13], SCL[0:1, 11:12], 448.0, 448.0, OP.mult, OP.min), 'vk')
            b.w('s', 'vk', b.cnt['vk'])
            b.op('s', lambda E: E.activation(
                SCL[0:1, 13:14], SCL[0:1, 12:13], AF.Copy, scale=0.5), 'sk')
            b.w('v', 'sk', b.cnt['sk'])
            b.op('v', lambda E: E.tensor_mul(
                SCL[0:1, 14:15], SCL[0:1, 12:13], SCL[0:1, 6:7]), 'vk')
            b.w('s', 'vk', b.cnt['vk'])
            b.op('s', lambda E: E.activation(SCL[0:1, 15:16], SCL[0:1, 14:15], AF.Copy), 'sk')
            b.w('v', 'sk', b.cnt['sk'])
            V_sc = b.op('v', lambda E: E.reciprocal(SCL[0:1, 17:18], SCL[0:1, 15:16]), 'vk')
            b.w('s', 'vk', V_sc)
            S_sc = b.op('s', lambda E: E.activation(
                SCL[0:1, 18:19], SCL[0:1, 17:18], AF.Copy, scale=4.0), 'sk')
            b.w('y', 'sk', S_sc)
            Y_sc = b.op('y', lambda E: E.dma_start(
                bass.AP(scald, 8, [[16, 1], [1, 16]]), SCL[0:1, 8:24]), 'yd', 16)
            b.w('g', 'yd', Y_sc)
            LW_sxh = b.op('g', lambda E: E.dma_start(
                SXH[:, :], bass.AP(scald, 13, [[0, 128], [1, 1]])), 'ld1', 16)
            LW_scc = b.op('g', lambda E: E.dma_start(
                SCC[:, :], bass.AP(scald, 18, [[0, 128], [1, 1]])), 'ld2', 16)

            T_t3 = {}
            V_q8 = {}
            T_ff = {}
            S_so = {}
            G_u = {}
            V_o = {}
            S_hf2 = {}
            V_qtk = {}
            T_t3last = {}
            Y_out = {}
            pend3 = [None]
            for j in range(nt):
                b.w('s', 'vk', V_wq)
                b.w('s', 'vk', V_qtk.get(j - 2, 0))
                S_hf2[j] = b.op('s', lambda E, j=j: E.activation(
                    HFP[:, j % 2, :], s2t(j), AF.Identity,
                    scale=RSL[:, j:j + 1], bias=BLN[:, j:j + 1]), 'sk')
                if dbg:
                    b.w('y', 'sk', S_hf2[j])
                    Ydh = b.op('y', lambda E, j=j: E.dma_start(
                        DHF.ap()[j * 128:(j + 1) * 128, :], HFP[:, j % 2, :]),
                        'yd', 16)
                    b.w('s', 'yd', Ydh)
                b.w('v', 'sk', S_hf2[j])
                b.w('v', 'ld1', LW_sxh)
                b.w('v', 'tk', T_t3last.get(j - 2, 0))
                V_qtk[j] = b.op('v', lambda E, j=j: E.tensor_scalar(
                    QTK[:, j % 2, :], HFP[:, j % 2, :],
                    SXH[:, 0:1], None, OP.mult), 'vk')
                b.w('t', 'vk', V_qtk[j])
                b.w('t', 'gd', G_id8)
                b.w('t', 'vk', V_q8.get(j - 2, 0))
                for i in range(NC8):
                    T_t3[j] = b.op('t', lambda E, j=j, i=i: E.matmul(
                        PT3[:, j % 2, i * 128:(i + 1) * 128],
                        QTK[:, j % 2, i * 128:(i + 1) * 128], IDF8[:, :],
                        start=True, stop=True),
                        'tk' if i == NC8 - 1 else None)
                T_t3last[j] = T_t3[j]
                b.w('v', 'tk', T_t3[j])
                b.w('v', 'tk', T_ff.get(j - 2, 0))
                V_q8[j] = b.op('v', lambda E, j=j: E.tensor_copy(
                    QT[:, j % 2, :, :].rearrange("p a b -> p (a b)"),
                    PT3[:, j % 2, :]), 'vk')
                b.w('t', 'vk', V_q8[j])
                b.w('t', 'sk', S_so.get(j - 2, 0))
                for hf2 in range(2):
                    for ci in range(NC8):
                        T_ff[j] = b.op('t', lambda E, j=j, hf2=hf2, ci=ci: E.matmul(
                            PAC[:, j % 2, hf2 * 512:(hf2 + 1) * 512],
                            QT[:, j % 2, ci, :],
                            WQ8[:, ci, hf2 * 512:(hf2 + 1) * 512],
                            start=(ci == 0), stop=(ci == NC8 - 1)),
                            'tk' if (hf2 == 1 and ci == NC8 - 1) else None)

                def back(j=j, T_ffj=T_ff[j]):
                    b.w('s', 'tk', T_ffj)
                    b.w('s', 'ld2', LW_scc)
                    b.w('s', 'gk', G_u.get(j - 2, 0))
                    S_so[j] = b.op('s', lambda E, j=j: E.activation(
                        OF3[:, j % 2, :], PAC[:, j % 2, :], AF.Copy,
                        scale=SCC[:, 0:1]), 'sk')
                    b.w('g', 'sk', S_so[j])
                    b.w('g', 'gd', G_fb)
                    b.w('g', 'vk', V_o.get(j - 2, 0))
                    G_u[j] = b.op('g', lambda E, j=j: E.tensor_add(
                        UBB[:, j % 2, :], OF3[:, j % 2, :], FBB[:, :]), 'gk')
                    b.w('v', 'gk', G_u[j])
                    b.w('v', 'yo', Y_out.get(j - 2, 0))
                    V_o[j] = b.op('v', lambda E, j=j: E.tensor_add(
                        OB[:, j % 2, :], s2t(j), UBB[:, j % 2, :]), 'vk')
                    b.w('y', 'vk', V_o[j])
                    Y_out[j] = b.op('y', lambda E, j=j: E.dma_start(
                        OUT[j * 128:(j + 1) * 128, :], OB[:, j % 2, :]), 'yo', 16)

                if pend3[0] is not None:
                    pend3[0]()
                pend3[0] = back
            pend3[0]()
            b.w('y', 'yd', b.cnt['yd'])
            b.w('y', 'yo', b.cnt['yo'])
            b.emit_block()

    return nc


_BF = ml_dtypes.bfloat16


def _consts():
    tri = np.triu(np.ones((128, 128), dtype=np.float32))
    ident = np.eye(128, dtype=np.float32)
    ones = np.ones((1, 128), dtype=np.float32)
    return tri, ident, ones


def kernel(L0_attn_x, L0_attn_time_decay, L0_attn_time_first, L0_attn_time_shift,
           L0_attn_Wk, L0_attn_Wv, L0_attn_Wr, L0_attn_Wck, L0_attn_Wcv,
           L0_attn_Wcr, L0_attn_ln1_w, L0_attn_ln1_b, L0_attn_ln2_w,
           L0_attn_ln2_b, L0_ffn_weight, L0_ffn_bias):
    x = np.ascontiguousarray(np.asarray(L0_attn_x, dtype=np.float32))
    cA = np.ascontiguousarray
    tri, ident, ones = _consts()
    common = {
        "WKT": cA(np.asarray(L0_attn_Wk, np.float32).T),
        "WVT": cA(np.asarray(L0_attn_Wv, np.float32).T),
        "WRT": cA(np.asarray(L0_attn_Wr, np.float32).T),
        "WCKR": cA(np.asarray(L0_attn_Wck, np.float32).T.reshape(
            NC8, 128, NHC, 128).transpose(2, 1, 0, 3).reshape(NHC, 128, NC8 * 128)),
        "WCVT": cA(np.asarray(L0_attn_Wcv, np.float32).T),
        "WCRT": cA(np.asarray(L0_attn_Wcr, np.float32).T),
        "WFT": cA(np.asarray(L0_ffn_weight, np.float32).T),
        "TSV": cA(np.asarray(L0_attn_time_shift, np.float32).reshape(C)),
        "TFV": cA(np.asarray(L0_attn_time_first, np.float32).reshape(C)),
        "L1W": cA(np.asarray(L0_attn_ln1_w, np.float32)),
        "L1B": cA(np.asarray(L0_attn_ln1_b, np.float32)),
        "L2W": cA(np.asarray(L0_attn_ln2_w, np.float32)),
        "L2B": cA(np.asarray(L0_attn_ln2_b, np.float32)),
        "FBV": cA(np.asarray(L0_ffn_bias, np.float32)),
        "TRID": tri, "IDD": ident, "ONED": ones,
        "CONSTD": np.array(CONST_VALS, dtype=np.float32),
    }
    nc = build(nt=16, dbg=False)
    in_maps = [dict(common, X=cA(x[bb])) for bb in range(8)]
    res = run_bass_kernel_spmd(nc, in_maps, core_ids=list(range(8)))
    out = np.stack([np.asarray(r["OUT"]) for r in res.results], axis=0)
    return out.astype(np.float32)


# version 27 (final)
# speedup vs baseline: 1.0362x; 1.0066x over previous
import numpy as np
import ml_dtypes
from contextlib import ExitStack

import concourse.bass as bass
import concourse.mybir as mybir
from concourse.bass_utils import run_bass_kernel_spmd

F32 = mybir.dt.float32
BF16 = mybir.dt.bfloat16
FP8 = mybir.dt.float8e4
AX = mybir.AxisListType.X
OP = mybir.AluOpType
AF = mybir.ActivationFunctionType
EPS = 1e-5
C = 1024
CONST_VALS = (EPS, EPS * EPS, 1.0 + EPS, 1.0, -1.0, 0.5, -0.5, 1.0 / 1024.0,
              448.0, 4.0, 1e-12, 0.0, 2.0, -2.0, 128.0, 1024.0)
NC8 = 8
H = 4096
NHC = 32


class PB:

    def __init__(self, nc):
        self.nc = nc
        self.plan = []
        self.cnt = {}
        self.lastw = {}
        self.sems = {}
        self.dmatok = {}

    def mksem(self, ctx, *names):
        for n in names:
            self.sems[n] = ctx.enter_context(self.nc.semaphore(n))
            self.cnt[n] = 0

    def w(self, eng, sem, val):
        if not val or val <= 0:
            return
        key = (eng, sem)
        if self.lastw.get(key, 0) >= val:
            return
        self.lastw[key] = val
        h = self.sems[sem]
        self.plan.append((eng, lambda E, h=h, v=val: E.wait_ge(h, v)))

    def op(self, eng, fn, inc=None, n=1):
        if inc is None:
            self.plan.append((eng, fn))
            return None
        self.cnt[inc] += n
        h = self.sems[inc]
        self.plan.append((eng, lambda E, f=fn, h=h, m=n: f(E).then_inc(h, m)))
        val = self.cnt[inc]
        if inc in ("yd", "gd"):
            toks = self.dmatok.setdefault((eng, inc), [])
            toks.append(val)
            tgt = toks[-1]
            self.plan.append(
                (eng, lambda E, h=h, v=tgt: E.wait_ge(h, v)))
            key = (eng, inc)
            if self.lastw.get(key, 0) < tgt:
                self.lastw[key] = tgt
        return val

    def emit_block(self):
        plan = self.plan
        self.plan = []
        with self.nc.Block() as blk:
            blk.tensor(lambda E: [f(E) for e, f in plan if e == 't'])
            blk.vector(lambda E: [f(E) for e, f in plan if e == 'v'])
            blk.scalar(lambda E: [f(E) for e, f in plan if e == 's'])
            blk.gpsimd(lambda E: [f(E) for e, f in plan if e == 'g'])
            blk.sync(lambda E: [f(E) for e, f in plan if e == 'y'])


def build(nt=16, dbg=False):
    nc = bass.Bass(target_bir_lowering=False, detect_race_conditions=False)
    T = nt * 128
    SL1 = 2
    SL2 = min(4, nt)
    NS1 = nt // SL1
    NS2 = nt // SL2

    X = nc.dram_tensor("X", [T, C], F32, kind="ExternalInput")
    WKT = nc.dram_tensor("WKT", [C, C], F32, kind="ExternalInput")
    WVT = nc.dram_tensor("WVT", [C, C], F32, kind="ExternalInput")
    WRT = nc.dram_tensor("WRT", [C, C], F32, kind="ExternalInput")
    WCKR = nc.dram_tensor("WCKR", [NHC, 128, NC8 * 128], F32, kind="ExternalInput")
    WCVT = nc.dram_tensor("WCVT", [H, C], F32, kind="ExternalInput")
    WCRT = nc.dram_tensor("WCRT", [C, C], F32, kind="ExternalInput")
    WFT = nc.dram_tensor("WFT", [C, C], F32, kind="ExternalInput")
    TSV = nc.dram_tensor("TSV", [C], F32, kind="ExternalInput")
    TFV = nc.dram_tensor("TFV", [C], F32, kind="ExternalInput")
    L1W = nc.dram_tensor("L1W", [C], F32, kind="ExternalInput")
    L1B = nc.dram_tensor("L1B", [C], F32, kind="ExternalInput")
    L2W = nc.dram_tensor("L2W", [C], F32, kind="ExternalInput")
    L2B = nc.dram_tensor("L2B", [C], F32, kind="ExternalInput")
    FBV = nc.dram_tensor("FBV", [C], F32, kind="ExternalInput")
    CONSTD = nc.dram_tensor("CONSTD", [16], F32, kind="ExternalInput")
    TRID = nc.dram_tensor("TRID", [128, 128], F32, kind="ExternalInput")
    IDD = nc.dram_tensor("IDD", [128, 128], F32, kind="ExternalInput")
    ONED = nc.dram_tensor("ONED", [1, 128], F32, kind="ExternalInput")
    OUT = nc.dram_tensor("OUT", [T, C], F32, kind="ExternalOutput")
    eud = nc.dram_tensor("eud", [1, C], F32)
    WCK8D = nc.dram_tensor("WCK8D", [NHC, 128, NC8 * 128], BF16)
    WCV8D = nc.dram_tensor("WCV8D", [H, C], BF16)
    scald = nc.dram_tensor("scald", [1, 24], F32)
    cc_in = nc.dram_tensor("cc_in", [1, 8], F32)
    cc_out = nc.dram_tensor("cc_out", [1, 8], F32)
    if dbg:
        DXI = nc.dram_tensor("DXI", [T, C], F32, kind="ExternalOutput")
        DX1 = nc.dram_tensor("DX1", [T, C], F32, kind="ExternalOutput")
        DHF = nc.dram_tensor("DHF", [T, C], F32, kind="ExternalOutput")

    b = PB(nc)
    with ExitStack() as outer:
        e = outer.enter_context
        b.mksem(outer, "tk", "vk", "sk", "gk", "yd", "gd", "cc", "yo", "pp",
                "wk0", "wk1", "wk2", "wk3", "wk4", "wk5", "wv0", "wv1", "wv2", "wv3", "wv4", "wv5", "xb0", "xb1", "xg0", "xg1",
                "ld0", "ld1", "ld2")

        S2 = e(nc.sbuf_tensor("S2", [128, nt * C], F32))
        XB = e(nc.sbuf_tensor("XB", [128, 2, C], F32))
        SCRA = e(nc.sbuf_tensor("SCRA", [128, 3072], BF16))
        SUM = e(nc.sbuf_tensor("SUM", [128, nt], F32))
        SSQ = e(nc.sbuf_tensor("SSQ", [128, nt], F32))
        MXT = SUM
        MNT = SSQ
        XS2 = e(nc.sbuf_tensor("XS2", [128, 2 * nt], F32))
        XSQ2 = e(nc.sbuf_tensor("XSQ2", [128, 2 * nt], F32))
        LSUM = e(nc.sbuf_tensor("LSUM", [128, nt], F32))
        LSQ = e(nc.sbuf_tensor("LSQ", [128, nt], F32))
        STA = e(nc.sbuf_tensor("STA", [128, 6, nt], F32))
        RS0 = e(nc.sbuf_tensor("RS0", [128, nt], F32))
        B0N = e(nc.sbuf_tensor("B0N", [128, nt], F32))
        RSC = e(nc.sbuf_tensor("RSC", [128, nt], F32))
        BCN = e(nc.sbuf_tensor("BCN", [128, nt], F32))
        RS2 = e(nc.sbuf_tensor("RS2", [128, nt], F32))
        B2N = e(nc.sbuf_tensor("B2N", [128, nt], F32))
        RSL = e(nc.sbuf_tensor("RSL", [128, nt], F32))
        BLN = e(nc.sbuf_tensor("BLN", [128, nt], F32))
        TRIB = e(nc.sbuf_tensor("TRIB", [128, 128], BF16))
        IDB = e(nc.sbuf_tensor("IDB", [128, 128], BF16))
        ONEB = e(nc.sbuf_tensor("ONEB", [1, 128], BF16))
        ONEF = e(nc.sbuf_tensor("ONEF", [1, 128], F32))
        TSC = e(nc.sbuf_tensor("TSC", [128, 8], F32))
        OPTS = e(nc.sbuf_tensor("OPTS", [128, 8], F32))
        NTS = e(nc.sbuf_tensor("NTS", [128, 8], F32))
        OMTS = e(nc.sbuf_tensor("OMTS", [128, 8], F32))
        TSB = e(nc.sbuf_tensor("TSB", [128, 8], BF16))
        W1C = e(nc.sbuf_tensor("W1C", [128, 8], F32))
        B1C = e(nc.sbuf_tensor("B1C", [128, 8], F32))
        W2C = e(nc.sbuf_tensor("W2C", [128, 8], F32))
        B2C = e(nc.sbuf_tensor("B2C", [128, 8], F32))
        CK = e(nc.sbuf_tensor("CK", [1, C], F32))
        CA = e(nc.sbuf_tensor("CA", [1, C], F32))
        CZ = e(nc.sbuf_tensor("CZ", [1, C], F32))
        PBB = e(nc.sbuf_tensor("PBB", [128, 8], BF16))
        PB2 = e(nc.sbuf_tensor("PB2", [128, 8], BF16))
        ROLL = e(nc.sbuf_tensor("ROLL", [128, 8], BF16))
        AMX = e(nc.sbuf_tensor("AMX", [128, nt], F32))
        AROW = e(nc.sbuf_tensor("AROW", [1, 128], F32))
        SCL = e(nc.sbuf_tensor("SCL", [1, 24], F32))
        AMSB = e(nc.sbuf_tensor("AMSB", [1, 8], F32))
        GMX = e(nc.sbuf_tensor("GMX", [1, 8], F32))
        CB = e(nc.sbuf_tensor("CB", [128, 16], F32))
        for _i, _v in enumerate(CONST_VALS):
            nc.const_aps.register(F32, float(_v), CB[:, _i:_i + 1])

        def s2t(j):
            return S2[:, j * C:(j + 1) * C]

        def s2th(j, h):
            return S2[:, j * C + h * 512: j * C + (h + 1) * 512]

        with ExitStack() as ph1:
            e1 = ph1.enter_context
            W3 = e1(nc.sbuf_tensor("W3", [128, 3, NC8, C], BF16))
            HT = e1(nc.sbuf_tensor("HT", [128, NC8, SL1 * 128], BF16))
            XXT = e1(nc.sbuf_tensor("XXT", [128, 2, NC8, SL1 * 128], BF16))
            EKB = e1(nc.sbuf_tensor("EKB", [128, C], BF16))
            EKV = e1(nc.sbuf_tensor("EKV", [128, C], BF16))
            E1B = e1(nc.sbuf_tensor("E1B", [128, C], BF16))
            EUM1 = e1(nc.sbuf_tensor("EUM1", [128, C], BF16))
            T1F = e1(nc.sbuf_tensor("T1F", [128, NC8, SL1 * 128], F32))
            FP2 = e1(nc.sbuf_tensor("FP2", [128, 2, 2, 512], F32))
            BP2 = e1(nc.sbuf_tensor("BP2", [128, 2, 2, 512], BF16))
            BDEN = e1(nc.sbuf_tensor("BDEN", [128, 512], BF16))
            BIG = e1(nc.sbuf_tensor("BIG", [128, 512], BF16))
            F0P = e1(nc.sbuf_tensor("F0P", [128, 3, 512], F32))
            F3P = e1(nc.sbuf_tensor("F3P", [128, 3, 512], F32))
            B3P = e1(nc.sbuf_tensor("B3P", [128, 3, 4, 512], BF16))
            PKV = e1(nc.psum_tensor("PKV", [128, 2, 512], F32))
            PCS = e1(nc.psum_tensor("PCS", [65, 512], F32))
            PS = e1(nc.psum_tensor("PS", [128, 512], F32))
            PA = e1(nc.psum_tensor("PA", [128, 512], F32))
            PZ = e1(nc.psum_tensor("PZ", [128, 512], F32))
            PT = e1(nc.psum_tensor("PT", [128, C], F32))

            def pools(it):
                par = it % 2
                t3 = it % 3
                F0 = F0P[:, t3, :]
                F1 = FP2[:, par, 0, :]
                F2 = FP2[:, par, 1, :]
                F3 = F3P[:, t3, :]
                B_te = B3P[:, t3, 0, :]
                B_a = B3P[:, t3, 1, :]
                B_dent = B3P[:, t3, 2, :]
                B_t2 = B3P[:, t3, 3, :]
                B_den = BDEN[:, :]
                B_ig = BIG[:, :]
                B_z = BP2[:, par, 0, :]
                B_g = BP2[:, par, 1, :]
                return (F0, F1, F2, F3), (B_te, B_den, B_a, B_t2,
                                          B_dent, B_ig, B_z, B_g)
            UROW = FP2[0:1, 0, 0:2, :].rearrange("p a b -> p (a b)")
            EROW = FP2[0:1, 1, 0:2, :].rearrange("p a b -> p (a b)")

            b.op('g', lambda E: E.dma_start(
                CB[:, :], bass.AP(CONSTD, 0, [[0, 128], [1, 16]])), 'gd', 16)
            wap = lambda t: bass.AP(t, 0, [[C, 128], [128 * C, NC8], [1, C]])
            LW0 = b.op('g', lambda E: E.dma_start(W3[:, 0, :, :], wap(WKT)), 'ld0', 16)
            LW1 = b.op('g', lambda E: E.dma_start(W3[:, 1, :, :], wap(WVT)), 'ld1', 16)
            LW2 = b.op('g', lambda E: E.dma_start(W3[:, 2, :, :], wap(WRT)), 'ld2', 16)
            b.op('g', lambda E: E.dma_start(TRIB[:, :], TRID.ap()), 'gd', 16)
            b.op('g', lambda E: E.dma_start(IDB[:, :], IDD.ap()), 'gd', 16)
            b.op('g', lambda E: E.dma_start(ONEB[:, :], ONED.ap()), 'gd', 16)
            b.op('g', lambda E: E.dma_start(ONEF[:, :], ONED.ap()), 'gd', 16)

            def veccol(dst, src):
                return lambda E, d=dst, s=src: E.dma_start(
                    d[:, :], bass.AP(s, 0, [[1, 128], [128, 8]]),
                    allow_slow_non_contiguous=True)
            b.op('g', veccol(TSC, TSV), 'gd', 16)
            b.op('g', veccol(W1C, L1W), 'gd', 16)
            b.op('g', veccol(B1C, L1B), 'gd', 16)
            b.op('g', veccol(W2C, L2W), 'gd', 16)
            b.op('g', veccol(B2C, L2B), 'gd', 16)
            b.op('g', lambda E: E.dma_start(
                UROW, bass.AP(TFV, 0, [[C, 1], [1, C]])), 'gd', 16)
            LD_ALL = b.cnt['gd']
            b.op('g', lambda E: E.memset(CK[:, :], 1.0), 'gk')
            b.op('g', lambda E: E.memset(CA[:, :], 0.0), 'gk')
            b.op('g', lambda E: E.memset(SCL[:, :], 0.0), 'gk')
            G_init = b.op('g', lambda E: E.memset(CZ[:, :], 0.0), 'gk')

            b.w('s', 'gd', LD_ALL)
            b.op('s', lambda E: E.activation(OPTS[:, :], TSC[:, :], AF.Copy, bias=1.0), 'sk')
            b.op('s', lambda E: E.activation(NTS[:, :], TSC[:, :], AF.Copy, scale=-1.0), 'sk')
            b.op('s', lambda E: E.activation(OMTS[:, :], TSC[:, :], AF.Copy, scale=-1.0, bias=1.0), 'sk')
            b.op('s', lambda E: E.activation(TSB[:, :], TSC[:, :], AF.Copy), 'sk')
            S_eu = b.op('s', lambda E: E.activation(EROW, UROW, AF.Exp), 'sk')
            b.w('v', 'sk', S_eu)
            V_er = b.op('v', lambda E: E.tensor_scalar(
                EROW, EROW, -1.0, None, OP.add), 'vk')
            b.w('y', 'vk', V_er)
            Y_eu = b.op('y', lambda E: E.dma_start(eud.ap(), EROW), 'yd', 16)
            b.w('g', 'yd', Y_eu)
            G_eum = b.op('g', lambda E: E.dma_start(
                EUM1[:, :], bass.AP(eud, 0, [[0, 128], [1, C]])), 'gd', 16)
            pp_chunks = []
            for src, dst, nel in ((WCKR, WCK8D, NHC * 128 * NC8 * 128),
                                  (WCVT, WCV8D, H * C)):
                for q in range(8):
                    ck = nel // 8
                    pp_chunks.append(lambda src=src, dst=dst, off=q * ck,
                                     w=ck // 128: b.op(
                        'g', lambda E, src=src, dst=dst, off=off, w=w:
                        E.dma_start(
                            bass.AP(dst, off, [[w, 128], [1, w]]),
                            bass.AP(src, off, [[w, 128], [1, w]])),
                        'pp', 16))
            PP_K = 8 * 16
            PP_ALL = 16 * 16

            V_st = {}
            S_st = {}
            for j in range(nt):
                xs_ = 'xb%d' % (j % 2)
                Yx0 = b.op('y', lambda E, j=j: E.dma_start(
                    s2t(j), X[j * 128:(j + 1) * 128, :]), xs_, 16)
                b.w('v', xs_, Yx0)
                V_st[j] = b.op('v', lambda E, j=j: E.reduce_sum(
                    SUM[:, j:j + 1], s2t(j), AX), 'vk')
                b.w('s', xs_, Yx0)
                S_st[j] = b.op('s', lambda E, j=j: E.activation(
                    SCRA[:, 2048:3072],
                    s2t(j), AF.Square,
                    accum_out=SSQ[:, j:j + 1]), 'sk')

            def rs_chain(sumt, sqt, rs_outs, bias_outs, double_ln):
                M0c, T1c, M2c, V0c, L0c, LCc = (STA[:, i, :] for i in range(6))
                b.op('s', lambda E: E.activation(M0c, sumt[:, :], AF.Copy, scale=1.0 / C), 'sk')
                b.op('s', lambda E: E.activation(T1c, sqt[:, :], AF.Copy, scale=1.0 / C), 'sk')
                S_m = b.op('s', lambda E: E.activation(M2c, M0c, AF.Square), 'sk')
                b.w('v', 'sk', S_m)
                V_v0 = b.op('v', lambda E: E.tensor_sub(V0c, T1c, M2c), 'vk')
                b.w('s', 'vk', V_v0)
                b.op('s', lambda E: E.activation(L0c, V0c, AF.Ln, bias=EPS), 'sk')
                S_l = b.op('s', lambda E: E.activation(
                    rs_outs[0][:, :], L0c, AF.Exp, scale=-0.5), 'sk')
                if double_ln:
                    b.op('s', lambda E: E.activation(
                        LCc, V0c, AF.Ln, scale=1.0 + EPS, bias=EPS * EPS), 'sk')
                    S_l = b.op('s', lambda E: E.activation(
                        rs_outs[1][:, :], LCc, AF.Exp, scale=-0.5), 'sk')
                b.w('v', 'sk', S_l)
                tok = None
                for rs_t, bi_t in zip(rs_outs, bias_outs):
                    tok = b.op('v', lambda E, r=rs_t, o=bi_t: E.scalar_tensor_tensor(
                        o[:, :], M0c, -1.0, r[:, :], OP.mult, OP.mult), 'vk')
                return tok

            b.w('s', 'vk', V_st[nt - 1])
            b.w('s', 'sk', S_st[nt - 1])
            V_chain = rs_chain(SUM, SSQ, [RS0, RSC], [B0N, BCN], True)

            T_ttr = {}
            S_hev = {}
            V_xxx = {}
            V_done = {}
            S_done = {}
            hist = []
            T_r_lastslice = {}

            pend1 = [None]
            pend2 = [None]

            W = SL1 * 128

            def emit_xxt(s1, chunk):
                par = s1 % 2
                tiles = [s1 * SL1 + i for i in range(SL1)]
                if chunk in (0, 1):
                    j = tiles[chunk]
                    b.w('s', 'vk', V_chain)
                    b.w('s', 'tk', T_ttr.get(j - 2, 0))
                    S_done[j] = b.op('s', lambda E, j=j: E.activation(
                        SCRA[:, (j % 2) * 1024:(j % 2 + 1) * 1024],
                        s2t(j), AF.Identity,
                        scale=RSC[:, j:j + 1], bias=BCN[:, j:j + 1]), 'sk')
                if chunk in (1, 2):
                    j = tiles[chunk - 1]
                    b.w('t', 'sk', S_done[j])
                    b.w('t', 'sk', S_hev.get(j - 1, 0))
                    for i in range(NC8):
                        T_ttr[j] = b.op('t', lambda E, j=j, i=i: E.matmul(
                            PT[:, i * 128:(i + 1) * 128],
                            SCRA[:, (j % 2) * 1024 + i * 128:(j % 2) * 1024 + (i + 1) * 128],
                            IDB[:, :], start=True, stop=True),
                            'tk' if i == NC8 - 1 else None)
                    b.w('s', 'tk', T_ttr[j])
                    b.w('s', 'vk', V_xxx.get(s1 - 1, 0))
                    for i in range(NC8):
                        S_hev[j] = b.op('s', lambda E, j=j, i=i: E.activation(
                            HT[:, i, (j % SL1) * 128:(j % SL1 + 1) * 128],
                            PT[:, i * 128:(i + 1) * 128], AF.Identity,
                            scale=W1C[:, i:i + 1], bias=B1C[:, i:i + 1]),
                            'sk' if i == NC8 - 1 else None)
                if chunk != 3:
                    return
                S_t1 = None
                for i in range(NC8):
                    S_t1 = b.op('s', lambda E, i=i: E.activation(
                        T1F[:, i, :],
                        HT[:, i, :], AF.Copy, scale=OPTS[:, i:i + 1]),
                        'sk' if i == NC8 - 1 else None)
                b.w('v', 'sk', S_t1)
                b.w('v', 'tk', T_r_lastslice.get(s1 - 2, 0))
                pbsrc = TSB if s1 == 0 else PBB
                for i in range(NC8):
                    b.op('v', lambda E, i=i, par=par: E.scalar_tensor_tensor(
                        XXT[:, par, i, 1:W], HT[:, i, 0:W - 1], NTS[:, i:i + 1],
                        T1F[:, i, 1:W], OP.mult, OP.add), None)
                    V_xxx[s1] = b.op('v', lambda E, i=i, par=par, pbsrc=pbsrc:
                        E.scalar_tensor_tensor(
                            XXT[:, par, i, 0:1], pbsrc[:, i:i + 1], NTS[:, i:i + 1],
                            T1F[:, i, 0:1], OP.mult, OP.add),
                        'vk' if i == NC8 - 1 else None)
                b.w('s', 'vk', V_xxx[s1])
                for i in range(NC8):
                    b.op('s', lambda E, i=i: E.activation(
                        PBB[:, i:i + 1], HT[:, i, W - 1:W], AF.Copy),
                        'sk' if i == NC8 - 1 else None)

            for ch in range(4):
                emit_xxt(0, ch)
            for s1 in range(NS1):
                par = s1 % 2
                tiles = [s1 * SL1 + i for i in range(SL1)]
                for j in tiles:
                    for h in (0, 1):
                        hs = slice(h * 512, (h + 1) * 512)
                        xoff = (j % SL1) * 128
                        it = 2 * j + h
                        (F0, F1, F2, F3), (B_te, B_den, B_a, B_t2, B_dent,
                                           B_ig, B_z, B_g) = pools(it)
                        p1 = hist[-1] if len(hist) >= 1 else {}
                        p2 = hist[-2] if len(hist) >= 2 else {}
                        p3 = hist[-3] if len(hist) >= 3 else {}
                        cur = {}
                        b.w('t', 'sk', p1.get('e1', 0))
                        b.w('t', 'vk', V_xxx[s1])
                        b.w('t', 'gd', LD_ALL)
                        b.w('t', 'ld0', LW0)
                        b.w('t', 'ld1', LW1)
                        b.w('t', 'ld2', LW2)
                        b.w('t', 'gk', G_init)
                        for ci in range(NC8):
                            T_k = b.op('t', lambda E, ci=ci, par=par, xoff=xoff, h=h:
                                E.matmul(PKV[:, 0, :], XXT[:, par, ci, xoff:xoff + 128],
                                         W3[:, 0, ci, h * 512:(h + 1) * 512],
                                         start=(ci == 0), stop=(ci == NC8 - 1)),
                                'tk' if ci == NC8 - 1 else None)
                        b.w('t', 'vk', p1.get('ekv', 0))
                        for ci in range(NC8):
                            T_v = b.op('t', lambda E, ci=ci, par=par, xoff=xoff, h=h:
                                E.matmul(PKV[:, 1, :], XXT[:, par, ci, xoff:xoff + 128],
                                         W3[:, 1, ci, h * 512:(h + 1) * 512],
                                         start=(ci == 0), stop=(ci == NC8 - 1)),
                                'tk' if ci == NC8 - 1 else None)
                        b.w('s', 'tk', T_k)
                        b.w('s', 'gd', G_eum)
                        b.w('s', 'vk', p2.get('dent', 0))
                        b.w('s', 'gk', p2.get('gmax', 0))
                        S_ek = b.op('s', lambda E, hs=hs: E.activation(
                            EKB[:, hs], PKV[:, 0, :], AF.Exp), 'sk')
                        cur['ek'] = S_ek
                        b.w('t', 'sk', S_ek)
                        for ci in range(NC8):
                            T_r = b.op('t', lambda E, ci=ci, par=par, xoff=xoff, h=h:
                                E.matmul(PKV[:, 0, :], XXT[:, par, ci, xoff:xoff + 128],
                                         W3[:, 2, ci, h * 512:(h + 1) * 512],
                                         start=(ci == 0), stop=(ci == NC8 - 1)),
                                'tk' if ci == NC8 - 1 else None)
                        b.w('s', 'tk', T_r)
                        S_e1 = b.op('s', lambda E, hs=hs: E.activation(
                            E1B[:, hs], PKV[:, 0, :], AF.Exp, scale=-1.0), 'sk')
                        cur['e1'] = S_e1
                        b.w('t', 'vk', p1.get('dd', 0))
                        b.w('t', 'vk', p2.get('ck', 0))
                        b.op('t', lambda E, hs=hs: E.matmul(
                            PS[:, :], TRIB[:, :], EKB[:, hs],
                            start=True, stop=False), None)
                        T_s = b.op('t', lambda E, h=h: E.matmul(
                            PS[:, :], ONEF[:, :], CK[0:1, h * 512:(h + 1) * 512],
                            start=False, stop=True), 'tk')
                        b.w('t', 'vk', p1.get('ck', 0))
                        T_bk = b.op('t', lambda E, hs=hs: E.matmul(
                            PCS[0:1, :], TRIB[:, 127:128], EKB[:, hs],
                            start=True, stop=True), 'tk')
                        b.w('v', 'tk', T_s)
                        b.w('v', 'gk', p3.get('xin', 0))
                        S_dd = b.op('v', lambda E, F0=F0: E.tensor_scalar(
                            F0, PS[:, :], 1.0, None, OP.mult), 'vk')
                        cur['dd'] = S_dd
                        V_stt = b.op('v', lambda E, hs=hs, F0=F0, F1=F1: E.scalar_tensor_tensor(
                            F1, EKB[:, hs], -1.0, F0, OP.mult, OP.add), 'vk')
                        b.w('v', 'tk', T_v)
                        b.w('v', 'sk', S_ek)
                        b.w('v', 'gk', p2.get('z', 0))
                        V_ekv = b.op('v', lambda E, hs=hs: E.tensor_mul(
                            EKV[:, hs], EKB[:, hs], PKV[:, 1, :]), 'vk')
                        cur['ekv'] = V_ekv
                        b.w('g', 'sk', S_ek)
                        b.w('g', 'vk', p1.get('dent', 0))
                        G_te = b.op('g', lambda E, hs=hs, B_te=B_te: E.tensor_mul(
                            B_te, EKB[:, hs], EUM1[:, hs]), 'gk')
                        cur['gmax'] = G_te
                        b.w('g', 'vk', V_stt)
                        G_den = b.op('g', lambda E, F0=F0, B_te=B_te, B_den=B_den:
                                     E.tensor_add(B_den, F0, B_te), 'gk')
                        b.w('v', 'sk', p2.get('sg', 0))
                        V_max = b.op('v', lambda E, hs=hs, F1=F1, F2=F2: E.tensor_max(
                            F2, F1, EKB[:, hs]), 'vk')
                        b.w('v', 'tk', T_bk)
                        V_ck = b.op('v', lambda E, h=h: E.tensor_add(
                            CK[0:1, h * 512:(h + 1) * 512],
                            CK[0:1, h * 512:(h + 1) * 512], PCS[0:1, :]), 'vk')
                        cur['ck'] = V_ck
                        b.w('s', 'vk', V_max)
                        b.op('s', lambda E, F2=F2, F3=F3: E.activation(F3, F2, AF.Ln), None)
                        S_ln = b.op('s', lambda E, F0=F0, F1=F1: E.activation(F1, F0, AF.Ln), 'sk')
                        b.w('v', 'sk', S_ln)
                        b.w('v', 'sk', S_e1)
                        V_a = b.op('v', lambda E, F1=F1, F3=F3, B_a=B_a:
                                   E.tensor_sub(B_a, F1, F3), 'vk')
                        cur['a'] = V_a
                        b.w('v', 'gk', G_den)
                        b.op('v', lambda E, hs=hs, B_den=B_den, B_dent=B_dent, E1B=E1B:
                             E.scalar_tensor_tensor(
                                 B_dent, E1B[:, hs], 1.0, B_den, OP.add, OP.mult), None)
                        V_pre = b.op('v', lambda E, j=j, h=h, hs=hs, F3=F3: E.tensor_scalar(
                            F3, s2th(j, h),
                            RS0[:, j:j + 1], B0N[:, j:j + 1], OP.mult, OP.add), 'vk')
                        cur['dent'] = V_pre
                        if h == 1:
                            V_done[j] = V_pre

                        def stage_b1(j=j, h=h, hs=hs, F0=F0, F1=F1, F2=F2, F3=F3,
                                     B_te=B_te, B_den=B_den, B_a=B_a, B_t2=B_t2,
                                     B_dent=B_dent, B_ig=B_ig, B_z=B_z, B_g=B_g,
                                     cur=cur, p1=p1, p2=p2):
                            b.w('t', 'vk', cur['a'])
                            b.w('t', 'vk', p1.get('lng', 0))
                            b.w('t', 'vk', p2.get('ca', 0))
                            b.op('t', lambda E: E.matmul(
                                PA[:, :], TRIB[:, :], B_a, start=True, stop=False), None)
                            T_a = b.op('t', lambda E, h=h: E.matmul(
                                PA[:, :], ONEF[:, :], CA[0:1, h * 512:(h + 1) * 512],
                                start=False, stop=True), 'tk')
                            b.w('t', 'vk', p1.get('ca', 0))
                            T_ba = b.op('t', lambda E: E.matmul(
                                PCS[32:33, :], TRIB[:, 127:128], B_a,
                                start=True, stop=True), 'tk')
                            b.w('v', 'tk', T_a)
                            V_lng = b.op('v', lambda E: E.scalar_tensor_tensor(
                                F2, B_a, -1.0, PA[:, :], OP.mult, OP.add), 'vk')
                            cur['lng'] = V_lng
                            b.w('v', 'tk', T_ba)
                            V_ca = b.op('v', lambda E, h=h: E.tensor_add(
                                CA[0:1, h * 512:(h + 1) * 512],
                                CA[0:1, h * 512:(h + 1) * 512], PCS[32:33, :]), 'vk')
                            cur['ca'] = V_ca
                            b.w('s', 'vk', V_lng)
                            b.w('s', 'gk', p1.get('z', 0))
                            b.op('s', lambda E: E.activation(B_g, F2, AF.Exp), None)
                            b.op('s', lambda E: E.activation(B_ig, F2, AF.Exp, scale=-1.0), None)
                            b.op('s', lambda E: E.activation(F1, B_dent, AF.Ln), None)
                            S_g = b.op('s', lambda E: E.activation(
                                B_t2, F1, AF.Exp, scale=-1.0), 'sk')
                            cur['sg'] = S_g
                            b.w('v', 'sk', S_g)
                            b.w('g', 'vk', cur['ekv'])
                            G_tn = b.op('g', lambda E, hs=hs: E.tensor_mul(
                                B_te, EKV[:, hs], EUM1[:, hs]), 'gk')
                            b.w('g', 'sk', S_g)
                            b.w('g', 'tk', p2.get('tz', 0))
                            V_z = b.op('g', lambda E, hs=hs: E.tensor_mul(
                                B_z, EKV[:, hs], B_ig), 'gk')
                            cur['z'] = V_z

                            def stage_b2():
                                b.w('t', 'gk', V_z)
                                b.w('t', 'vk', p1.get('tg', 0))
                                b.w('t', 'vk', p2.get('cz', 0))
                                b.op('t', lambda E: E.matmul(
                                    PZ[:, :], TRIB[:, :], B_z, start=True, stop=False), None)
                                T_z = b.op('t', lambda E, h=h: E.matmul(
                                    PZ[:, :], ONEF[:, :], CZ[0:1, h * 512:(h + 1) * 512],
                                    start=False, stop=True), 'tk')
                                b.w('t', 'vk', p1.get('cz', 0))
                                T_bz = b.op('t', lambda E: E.matmul(
                                    PCS[64:65, :], TRIB[:, 127:128], B_z,
                                    start=True, stop=True), 'tk')
                                cur['tz'] = T_bz
                                b.w('v', 'tk', T_z)
                                V_tg = b.op('v', lambda E: E.scalar_tensor_tensor(
                                    F0, B_g, 0.0, PZ[:, :], OP.bypass, OP.mult), 'vk')
                                cur['tg'] = V_tg
                                b.w('v', 'tk', T_bz)
                                V_cz = b.op('v', lambda E, h=h: E.tensor_add(
                                    CZ[0:1, h * 512:(h + 1) * 512],
                                    CZ[0:1, h * 512:(h + 1) * 512], PCS[64:65, :]), 'vk')
                                cur['cz'] = V_cz
                                b.w('g', 'vk', V_tg)
                                b.op('g', lambda E: E.tensor_add(B_a, F0, B_te), None)
                                b.op('g', lambda E: E.tensor_mul(B_dent, B_a, B_t2), None)
                                G_xin = b.op('g', lambda E, j=j, h=h: E.tensor_add(
                                    s2th(j, h), B_dent, F3), 'gk')
                                cur['xin'] = G_xin
                                cur['num'] = G_xin
                            return stage_b2

                        hist.append(cur)
                        if s1 + 1 < NS1:
                            emit_xxt(s1 + 1, it % 4)
                        if pp_chunks and it % 2 == 0:
                            pp_chunks.pop(0)()
                        nb2 = pend1[0]() if pend1[0] is not None else None
                        if pend2[0] is not None:
                            pend2[0]()
                        pend1[0] = stage_b1
                        pend2[0] = nb2
                T_r_lastslice[s1] = b.cnt['tk']
            nb2 = pend1[0]()
            if pend2[0] is not None:
                pend2[0]()
            nb2()
            while pp_chunks:
                pp_chunks.pop(0)()
            XL = b.cnt['gk']
            b.w('v', 'gk', XL)
            b.w('s', 'gk', XL)
            for j in range(nt):
                V_last = b.op('v', lambda E, j=j: E.reduce_sum(
                    XS2[:, j:j + 1], s2t(j), AX), 'vk')
                b.op('s', lambda E, j=j: E.activation(
                    SCRA[:, 2048:3072], s2t(j), AF.Square,
                    accum_out=XSQ2[:, j:j + 1]), 'sk')
            iter_join = {'vk': b.cnt['vk'], 'sk': b.cnt['sk'],
                         'gk': b.cnt['gk'], 'tk': b.cnt['tk']}
            if dbg:
                for sem, val in iter_join.items():
                    b.w('y', sem, val)
                DBG1 = b.op('y', lambda E: E.dma_start(
                    DXI.ap().rearrange("(j p) c -> p j c", p=128),
                    S2[:, :].rearrange("p (j c) -> p j c", c=C)), 'yd', 16)
            b.emit_block()

        with ExitStack() as ph2:
            e2 = ph2.enter_context
            W2T = SL2 * 128
            H2T = e2(nc.sbuf_tensor("H2T", [128, NC8, W2T], BF16))
            XX2 = e2(nc.sbuf_tensor("XX2", [128, NC8, W2T], BF16))
            KC2 = e2(nc.sbuf_tensor("KC2", [128, NHC, W2T], BF16))
            WCR8 = e2(nc.sbuf_tensor("WCR8", [128, NC8, C], BF16))
            WCKB = e2(nc.sbuf_tensor("WCKB", [128, NC8, 5, 128], BF16))
            WCVB = e2(nc.sbuf_tensor("WCVB", [128, 5, C], BF16))
            RCB = e2(nc.sbuf_tensor("RCB", [128, SL2, C], BF16))
            RHOB = e2(nc.sbuf_tensor("RHOB", [128, W2T], BF16))
            T2F = e2(nc.sbuf_tensor("T2F", [128, NC8, W2T // 2], F32))
            T3B = e2(nc.sbuf_tensor("T3B", [128, 2, C], F32))
            SCR2 = e2(nc.sbuf_tensor("SCR2", [128, 2, C], BF16))
            PALL = e2(nc.psum_tensor("PALL", [128, 8, 512], F32))

            join2 = dict(iter_join)
            for eng in ('t', 'v', 's', 'g', 'y'):
                for sem, val in join2.items():
                    b.w(eng, sem, val)
            if dbg:
                b.w('v', 'yd', DBG1)

            G_wcr = b.op('g', lambda E: E.dma_start(
                WCR8[:, :, :],
                bass.AP(WCRT, 0, [[C, 128], [128 * C, NC8], [1, C]])), 'gd', 16)

            V_ch2 = rs_chain(XS2[:, 0:nt], XSQ2[:, 0:nt], [RS2], [B2N], False)

            jr = nt - 1
            b.w('s', 'vk', V_ch2)
            S_hr = b.op('s', lambda E, jr=jr: E.activation(
                SCRA[:, 0:1024], s2t(jr), AF.Identity,
                scale=RS2[:, jr:jr + 1], bias=B2N[:, jr:jr + 1]), 'sk')
            b.w('t', 'sk', S_hr)
            for i in range(NC8):
                T_tr = b.op('t', lambda E, i=i: E.matmul(
                    PALL[:, 0:2, :].rearrange("p a b -> p (a b)")[:, i * 128:(i + 1) * 128],
                    SCRA[:, i * 128:(i + 1) * 128], IDB[:, :],
                    start=True, stop=True),
                    'tk' if i == NC8 - 1 else None)
            b.w('s', 'tk', T_tr)
            S_roll = None
            for i in range(NC8):
                S_roll = b.op('s', lambda E, i=i: E.activation(
                    ROLL[:, i:i + 1],
                    PALL[:, 0:2, :].rearrange("p a b -> p (a b)")[:, i * 128 + 127:i * 128 + 128],
                    AF.Identity, scale=W2C[:, i:i + 1], bias=B2C[:, i:i + 1]),
                    'sk' if i == NC8 - 1 else None)

            slice_join = {'tk': b.cnt['tk'], 'vk': b.cnt['vk'],
                          'sk': b.cnt['sk'], 'gk': b.cnt['gk']}
            V_x1 = {}
            V_rcvv = {}
            V_x1j = {}
            G_t3 = {}
            Yx2 = {}
            V_mm = 0
            pend_ev = [None]
            A_tok = {}
            HW2 = W2T // 2
            NPK = NS2 * NHC
            wck_tok = {}
            wck_rd = {}
            wcv_tok = {}
            wcv_rd = {}

            def wck_load(g):
                b.w('s', 'pp', PP_K)
                b.w('s', 'tk', wck_rd.get(g - 5, 0))
                wck_tok[g] = b.op('s', lambda E, hh=g % NHC, g=g: E.dma_start(
                    WCKB[:, :, g % 5, :],
                    bass.AP(WCK8D, hh * 128 * NC8 * 128,
                            [[NC8 * 128, 128], [128, NC8], [1, 128]])),
                    'wk%d' % (g % 5), 16)

            def wcv_load(g):
                b.w('y', 'pp', PP_ALL)
                b.w('y', 'tk', wcv_rd.get(g - 5, 0))
                wcv_tok[g] = b.op('y', lambda E, hh=g % NHC, g=g: E.dma_start(
                    WCVB[:, g % 5, :],
                    bass.AP(WCV8D, hh * 128 * C, [[C, 128], [1, C]])),
                    'wv%d' % (g % 5), 16)

            for g in range(5):
                wck_load(g)
            for g in range(5):
                wcv_load(g)
            for jn in (0, 1):
                Yx2[jn] = b.op('g', lambda E, jn=jn: E.dma_start(
                    XB[:, jn % 2, :], X[jn * 128:(jn + 1) * 128, :]),
                    'xg%d' % (jn % 2), 16)

            def emit_A(s2i, part):
                par = s2i % 2
                tiles = [s2i * SL2 + i for i in range(SL2)]
                prev = A_tok.get(s2i - 1)
                if part == 0:
                    hh2 = {}
                    for j in tiles[:2]:
                        b.w('s', 'tk', prev['T_tt'] if prev else slice_join['tk'])
                        b.w('s', 'vk', V_ch2)
                        hh2[j] = b.op('s', lambda E, j=j: E.activation(
                            SCRA[:, (j % 2) * 1024:(j % 2 + 1) * 1024], s2t(j),
                            AF.Identity,
                            scale=RS2[:, j:j + 1], bias=B2N[:, j:j + 1]), 'sk')
                    A_tok[s2i] = dict(hh2=hh2, Trc=0)
                    return
                if part == 1:
                    cur = A_tok[s2i]
                    if s2i >= 2:
                        b.w('t', 'vk', V_rcvv.get((s2i - 2) * SL2, 0))
                    S_hev2 = S_roll if s2i == 0 else prev['S_hev2']
                    T_tt = 0
                    for j in tiles:
                        if j - 2 in tiles:
                            b.w('s', 'tk', T_tt)
                            b.w('s', 'vk', V_ch2)
                            cur['hh2'][j] = b.op('s', lambda E, j=j: E.activation(
                                SCRA[:, (j % 2) * 1024:(j % 2 + 1) * 1024],
                                s2t(j), AF.Identity,
                                scale=RS2[:, j:j + 1], bias=B2N[:, j:j + 1]), 'sk')
                        b.w('t', 'sk', cur['hh2'][j])
                        b.w('t', 'sk', S_hev2)
                        for i in range(NC8):
                            T_tt = b.op('t', lambda E, j=j, i=i: E.matmul(
                                PALL[:, 0:2, :].rearrange("p a b -> p (a b)")[:, i * 128:(i + 1) * 128],
                                SCRA[:, (j % 2) * 1024 + i * 128:(j % 2) * 1024 + (i + 1) * 128],
                                IDB[:, :], start=True, stop=True),
                                'tk' if i == NC8 - 1 else None)
                        b.w('s', 'tk', T_tt)
                        b.w('s', 'vk', slice_join['vk'] if s2i == 0
                            else prev['V_xx2'])
                        for i in range(NC8):
                            S_hev2 = b.op('s', lambda E, j=j, i=i: E.activation(
                                H2T[:, i, (j % SL2) * 128:(j % SL2 + 1) * 128],
                                PALL[:, 0:2, :].rearrange("p a b -> p (a b)")[:, i * 128:(i + 1) * 128],
                                AF.Identity, scale=W2C[:, i:i + 1], bias=B2C[:, i:i + 1]),
                                'sk' if i == NC8 - 1 else None)
                    cur['S_hev2'] = S_hev2
                    cur['T_tt'] = T_tt
                    return
                cur = A_tok[s2i]
                pbsrc = ROLL if s2i == 0 else PB2
                V_xx2 = None
                for hv in range(2):
                    if hv == 1:
                        b.w('s', 'vk', V_xx2)
                    S_t1 = None
                    for i in range(NC8):
                        S_t1 = b.op('s', lambda E, i=i, hv=hv: E.activation(
                            T2F[:, i, :],
                            H2T[:, i, hv * HW2:(hv + 1) * HW2], AF.Copy,
                            scale=OMTS[:, i:i + 1]),
                            'sk' if i == NC8 - 1 else None)
                    b.w('v', 'sk', S_t1)
                    if hv == 0 and s2i >= 1:
                        b.w('v', 'tk', prev['Trc'])
                    for i in range(NC8):
                        lo = hv * HW2
                        b.op('v', lambda E, i=i, lo=lo: E.scalar_tensor_tensor(
                            XX2[:, i, lo + 1:lo + HW2],
                            H2T[:, i, lo:lo + HW2 - 1], TSC[:, i:i + 1],
                            T2F[:, i, 1:HW2], OP.mult, OP.add), None)
                        shsrc = (pbsrc[:, i:i + 1] if hv == 0
                                 else H2T[:, i, HW2 - 1:HW2])
                        V_xx2 = b.op('v', lambda E, i=i, lo=lo, shsrc=shsrc:
                            E.scalar_tensor_tensor(
                                XX2[:, i, lo:lo + 1], shsrc, TSC[:, i:i + 1],
                                T2F[:, i, 0:1], OP.mult, OP.add),
                            'vk' if i == NC8 - 1 else None)
                b.w('s', 'vk', V_xx2)
                for i in range(NC8):
                    b.op('s', lambda E, i=i: E.activation(
                        PB2[:, i:i + 1], H2T[:, i, W2T - 1:W2T], AF.Copy),
                        'sk' if i == NC8 - 1 else None)
                cur['V_xx2'] = V_xx2

            for pp_ in range(3):
                emit_A(0, pp_)
            for s2i in range(NS2):
                tiles = [s2i * SL2 + i for i in range(SL2)]
                V_xx2 = A_tok[s2i]['V_xx2']
                V_kc2 = {}
                S_rho = {}
                Tkc = {}
                b.w('t', 'vk', V_rcvv.get((s2i - 1) * SL2 + 1, 0))
                for hh in range(NHC):
                    g = s2i * NHC + hh
                    b.w('t', 'wk%d' % (g % 5), wck_tok[g])
                    b.w('t', 'vk', V_xx2)
                    if hh >= 2:
                        b.w('t', 'sk', S_rho[hh - 2])
                    for ci in range(NC8):
                        Tkc[hh] = b.op('t', lambda E, hh=hh, g=g, ci=ci: E.matmul(
                            PALL[:, 2 + hh % 2, :], WCKB[:, ci, g % 5, :],
                            XX2[:, ci, :], start=(ci == 0), stop=(ci == NC8 - 1)),
                            'tk' if ci == NC8 - 1 else None)
                    wck_rd[g] = Tkc[hh]
                    if hh == 4 and pend_ev[0] is not None:
                        pend_ev[0]()
                        pend_ev[0] = None
                    if hh == 16 and s2i + 1 < NS2:
                        emit_A(s2i + 1, 0)
                    b.w('s', 'tk', Tkc[hh])
                    if hh >= 1:
                        b.w('s', 'vk', V_kc2[hh - 1])
                    S_rho[hh] = b.op('s', lambda E, hh=hh: E.activation(
                        RHOB[:, :], PALL[:, 2 + hh % 2, :], AF.Relu), 'sk')
                    if g + 5 < NPK:
                        wck_load(g + 5)
                    b.w('v', 'sk', S_rho[hh])
                    V_kc2[hh] = b.op('v', lambda E, hh=hh: E.tensor_mul(
                        KC2[:, hh, :], RHOB[:, :], RHOB[:, :]), 'vk')
                if s2i + 1 < NS2:
                    emit_A(s2i + 1, 1)
                S_rc = None
                Trc = {}
                Trc_s = {}
                b.w('s', 'vk', V_rcvv.get(s2i * SL2 - 1, 0))
                b.w('t', 'vk', V_rcvv.get((s2i - 1) * SL2 + 2, 0))
                for tq in range(SL2):
                    for hf2 in range(2):
                        idx = tq * 2 + hf2
                        if idx >= 2:
                            b.w('t', 'sk', Trc_s[idx - 2])
                        for ci in range(NC8):
                            Trc[idx] = b.op('t', lambda E, tq=tq, hf2=hf2, ci=ci:
                                E.matmul(PALL[:, 4 + (tq * 2 + hf2) % 2, :],
                                         XX2[:, ci, tq * 128:(tq + 1) * 128],
                                         WCR8[:, ci, hf2 * 512:(hf2 + 1) * 512],
                                         start=(ci == 0), stop=(ci == NC8 - 1)),
                                'tk' if ci == NC8 - 1 else None)
                        b.w('s', 'tk', Trc[idx])
                        Trc_s[idx] = b.op('s', lambda E, tq=tq, hf2=hf2: E.activation(
                            RCB[:, tq, hf2 * 512:(hf2 + 1) * 512],
                            PALL[:, 4 + (tq * 2 + hf2) % 2, :], AF.Sigmoid), 'sk')
                        S_rc = Trc_s[idx]
                A_tok[s2i]['Trc'] = Trc[2 * SL2 - 1]
                if s2i + 1 < NS2:
                    emit_A(s2i + 1, 2)
                b.w('t', 'vk', V_kc2[NHC - 1])
                b.w('t', 'vk', V_rcvv.get((s2i - 1) * SL2 + 3, 0))
                b.w('t', 'sk', S_rc)
                if s2i + 1 < NS2:
                    b.w('t', 'sk', A_tok[s2i + 1]['S_hev2'])
                Tvv = {}
                for hh in range(NHC):
                    g = s2i * NHC + hh
                    b.w('t', 'wv%d' % (g % 5), wcv_tok[g])
                    for tq in range(SL2):
                        for hf2 in range(2):
                            Tvv[hh] = b.op('t', lambda E, hh=hh, g=g, tq=tq, hf2=hf2:
                                E.matmul(PALL[:, tq * 2 + hf2, :],
                                         KC2[:, hh, tq * 128:(tq + 1) * 128],
                                         WCVB[:, g % 5, hf2 * 512:(hf2 + 1) * 512],
                                         start=(hh == 0), stop=(hh == NHC - 1)),
                                'tk' if (tq == SL2 - 1 and hf2 == 1) else None)
                    wcv_rd[g] = Tvv[hh]
                    if g + 5 < NPK:
                        wcv_load(g + 5)
                T_vv = Tvv[NHC - 1]
                for ii, j in enumerate(tiles[:2]):
                    b.w('v', 'tk', T_vv)
                    V_rcvv[j] = b.op('v', lambda E, tq=ii, j=j: E.tensor_mul(
                        SCR2[:, j % 2, :], RCB[:, tq, :],
                        PALL[:, tq * 2:tq * 2 + 2, :].rearrange("p a b -> p (a b)")), 'vk')
                for j in tiles[:2]:
                    b.w('g', 'xg%d' % (j % 2), Yx2[j])
                    b.w('g', 'vk', V_x1j.get(j - 2, 0))
                    G_t3[j] = b.op('g', lambda E, j=j: E.tensor_add(
                        T3B[:, j % 2, :], XB[:, j % 2, :], s2t(j)), 'gk')
                for j in tiles[:2]:
                    b.w('v', 'gk', G_t3[j])
                    V_x1j[j] = b.op('v', lambda E, j=j: E.scalar_tensor_tensor(
                        s2t(j), SCR2[:, j % 2, :], 0.0, T3B[:, j % 2, :],
                        OP.bypass, OP.add,
                        accum_out=LSUM[:, j:j + 1]), 'vk')
                for j in tiles[:2]:
                    b.w('s', 'vk', V_x1j[j])
                    b.op('s', lambda E, j=j: E.activation(
                        SCRA[:, 2048:3072], s2t(j), AF.Square,
                        accum_out=LSQ[:, j:j + 1]), 'sk')
                V_x1[s2i] = V_x1j[tiles[1]]

                def evac2(s2i=s2i, tiles=tiles):
                    for ii, j in enumerate(tiles[2:]):
                        tq = 2 + ii
                        b.w('g', 'vk', V_x1j[j - 2])
                        Yx2[j] = b.op('g', lambda E, j=j: E.dma_start(
                            XB[:, j % 2, :], X[j * 128:(j + 1) * 128, :]),
                            'xg%d' % (j % 2), 16)
                        b.w('v', 'tk', T_vv)
                        V_rcvv[j] = b.op('v', lambda E, tq=tq, j=j: E.tensor_mul(
                            SCR2[:, j % 2, :], RCB[:, tq, :],
                            PALL[:, tq * 2:tq * 2 + 2, :].rearrange("p a b -> p (a b)")), 'vk')
                    for j in tiles[2:]:
                        b.w('g', 'xg%d' % (j % 2), Yx2[j])
                        G_t3[j] = b.op('g', lambda E, j=j: E.tensor_add(
                            T3B[:, j % 2, :], XB[:, j % 2, :], s2t(j)), 'gk')
                    for j in tiles[2:]:
                        b.w('v', 'gk', G_t3[j])
                        V_x1j[j] = b.op('v', lambda E, j=j: E.scalar_tensor_tensor(
                            s2t(j), SCR2[:, j % 2, :], 0.0, T3B[:, j % 2, :],
                            OP.bypass, OP.add,
                            accum_out=LSUM[:, j:j + 1]), 'vk')
                    for j in tiles[2:]:
                        b.w('s', 'vk', V_x1j[j])
                        b.op('s', lambda E, j=j: E.activation(
                            SCRA[:, 2048:3072], s2t(j), AF.Square,
                            accum_out=LSQ[:, j:j + 1]), 'sk')
                    nonlocal V_mm
                    for j in tiles:
                        b.op('v', lambda E, j=j: E.reduce_max(
                            MXT[:, j:j + 1], s2t(j), AX), None)
                        V_mm = b.op('v', lambda E, j=j: E.tensor_reduce(
                            MNT[:, j:j + 1], s2t(j), AX, OP.min), 'vk')
                    if s2i + 1 < NS2:
                        for jn in [(s2i + 1) * SL2, (s2i + 1) * SL2 + 1]:
                            Yx2[jn] = b.op('g', lambda E, jn=jn: E.dma_start(
                                XB[:, jn % 2, :],
                                X[jn * 128:(jn + 1) * 128, :]),
                                'xg%d' % (jn % 2), 16)
                pend_ev[0] = evac2
                slice_join = {'tk': b.cnt['tk'], 'vk': b.cnt['vk'],
                              'sk': b.cnt['sk'], 'gk': b.cnt['gk']}

            if pend_ev[0] is not None:
                pend_ev[0]()
                pend_ev[0] = None
            slice_join = {'tk': b.cnt['tk'], 'vk': b.cnt['vk'],
                          'sk': b.cnt['sk'], 'gk': b.cnt['gk']}

            if dbg:
                for sem, val in slice_join.items():
                    b.w('y', sem, val)
                DBG2 = b.op('y', lambda E: E.dma_start(
                    DX1.ap().rearrange("(j p) c -> p j c", p=128),
                    S2[:, :].rearrange("p (j c) -> p j c", c=C)), 'yd', 16)
            b.emit_block()

        with ExitStack() as ph3:
            e3 = ph3.enter_context
            HFP = e3(nc.sbuf_tensor("HFP", [128, 2, C], F32))
            WFS = e3(nc.sbuf_tensor("WFS", [128, NC8, C], F32))
            WQ8 = e3(nc.sbuf_tensor("WQ8", [128, NC8, C], FP8))
            QT = e3(nc.sbuf_tensor("QT", [128, 2, NC8, 128], FP8))
            QTK = e3(nc.sbuf_tensor("QTK", [128, 2, C], FP8))
            IDF8 = e3(nc.sbuf_tensor("IDF8", [128, 128], FP8))
            FBB = e3(nc.sbuf_tensor("FBB", [128, C], F32))
            OF3 = e3(nc.sbuf_tensor("OF3", [128, 2, C], F32))
            UBB = e3(nc.sbuf_tensor("UBB", [128, 2, C], BF16))
            OB = e3(nc.sbuf_tensor("OB", [128, 2, C], F32))
            SWH = e3(nc.sbuf_tensor("SWH", [128, 1], F32))
            SXH = e3(nc.sbuf_tensor("SXH", [128, 1], F32))
            SCC = e3(nc.sbuf_tensor("SCC", [128, 1], F32))
            WAM = e3(nc.sbuf_tensor("WAM", [128, 1], F32))
            PT3 = e3(nc.psum_tensor("PT3", [128, 2, C], F32))
            PAC = e3(nc.psum_tensor("PAC", [128, 2, C], F32))

            for eng in ('t', 'v', 's', 'g', 'y'):
                for sem, val in slice_join.items():
                    b.w(eng, sem, val)
            G_id8 = b.op('g', lambda E: E.dma_start(IDF8[:, :], IDD.ap()), 'gd', 16)
            if dbg:
                b.w('y', 'yd', DBG2)
                b.w('s', 'yd', DBG2)

            V_ch3 = rs_chain(LSUM, LSQ, [RSL], [BLN], False)
            Y_wf = b.op('y', lambda E: E.dma_start(
                WFS[:, :, :],
                bass.AP(WFT, 0, [[C, 128], [128 * C, NC8], [1, C]])), 'yd', 16)
            G_fb = b.op('g', lambda E: E.dma_start(
                FBB[:, :], bass.AP(FBV, 0, [[0, 128], [1, C]])), 'gd', 16)
            b.w('v', 'yd', Y_wf)
            V_wa = b.op('v', lambda E: E.reduce_max(
                WAM[:, :], WFS[:, :, :].rearrange("p a b -> p (a b)"), AX,
                apply_absolute_value=True), 'vk')
            b.w('y', 'vk', V_wa)
            Y_wr = b.op('y', lambda E: E.dma_start(AROW[:, :], WAM[:, 0:1]), 'yd', 16)
            b.w('v', 'yd', Y_wr)
            V_wm = b.op('v', lambda E: E.reduce_max(SCL[0:1, 0:1], AROW[:, :], AX), 'vk')
            S_hf = {}

            def sw_step(i):
                if i == 0:
                    b.w('s', 'vk', V_wm)
                    b.op('s', lambda E: E.activation(SCL[0:1, 1:2], SCL[0:1, 0:1], AF.Copy), 'sk')
                elif i == 1:
                    b.w('v', 'sk', b.cnt['sk'])
                    b.op('v', lambda E: E.tensor_scalar(
                        SCL[0:1, 2:3], SCL[0:1, 1:2], 1e-12, None, OP.max), 'vk')
                elif i == 2:
                    b.w('s', 'vk', b.cnt['vk'])
                    b.op('s', lambda E: E.activation(SCL[0:1, 3:4], SCL[0:1, 2:3], AF.Copy), 'sk')
                elif i == 3:
                    b.w('v', 'sk', b.cnt['sk'])
                    b.op('v', lambda E: E.reciprocal(SCL[0:1, 4:5], SCL[0:1, 3:4]), 'vk')
                elif i == 4:
                    b.w('s', 'vk', b.cnt['vk'])
                    b.op('s', lambda E: E.activation(SCL[0:1, 5:6], SCL[0:1, 4:5], AF.Copy), 'sk')
                elif i == 5:
                    b.w('v', 'sk', b.cnt['sk'])
                    b.op('v', lambda E: E.tensor_scalar(
                        SCL[0:1, 6:7], SCL[0:1, 5:6], 448.0, 448.0, OP.mult, OP.min), 'vk')
                elif i == 6:
                    b.w('s', 'vk', b.cnt['vk'])
                    b.op('s', lambda E: E.activation(
                        SCL[0:1, 7:8], SCL[0:1, 6:7], AF.Copy, scale=0.5), 'sk')

            sw_step(0)
            b.w('v', 'vk', V_ch3)
            b.w('v', 'vk', V_mm)
            b.op('v', lambda E: E.tensor_mul(AMX[:, :], MXT[:, :], RSL[:, :]), None)
            sw_step(1)
            b.op('v', lambda E: E.tensor_mul(MNT[:, :], MNT[:, :], RSL[:, :]), None)
            b.op('v', lambda E: E.tensor_add(AMX[:, :], AMX[:, :], BLN[:, :]), None)
            sw_step(2)
            b.op('v', lambda E: E.tensor_add(MNT[:, :], MNT[:, :], BLN[:, :]), None)
            b.op('v', lambda E: E.reduce_max(
                MXT[:, 0:1], AMX[:, :], AX, apply_absolute_value=True), None)
            sw_step(3)
            b.op('v', lambda E: E.reduce_max(
                MNT[:, 0:1], MNT[:, :], AX, apply_absolute_value=True), None)
            sw_step(4)
            V_a1 = b.op('v', lambda E: E.tensor_max(
                WAM[:, 0:1], MXT[:, 0:1], MNT[:, 0:1]), 'vk')
            b.w('y', 'vk', V_a1)
            Y_ar = b.op('y', lambda E: E.dma_start(AROW[:, :], WAM[:, 0:1]), 'yd', 16)
            b.w('v', 'yd', Y_ar)
            V_ams = b.op('v', lambda E: E.reduce_max(SCL[0:1, 16:17], AROW[:, :], AX), 'vk')
            sw_step(5)
            b.w('s', 'vk', V_ams)
            S_amc = b.op('s', lambda E: E.activation(
                AMSB[0:1, 0:1], SCL[0:1, 16:17], AF.Copy), 'sk')
            b.op('g', lambda E: E.memset(AMSB[0:1, 1:8], 0.0), 'gk')
            b.w('g', 'sk', S_amc)
            G_ci = b.op('g', lambda E: E.dma_start(cc_in.ap(), AMSB[:, :]), 'gd', 16)
            b.w('g', 'gd', G_ci)
            G_cc = b.op('g', lambda E: E.collective_compute(
                "AllReduce", OP.max, replica_groups=[list(range(8))],
                ins=[cc_in.ap().opt()], outs=[cc_out.ap().opt()]), 'cc')
            sw_step(6)
            b.w('y', 'sk', b.cnt['sk'])
            Y_scA = b.op('y', lambda E: E.dma_start(
                bass.AP(scald, 0, [[8, 1], [1, 8]]), SCL[0:1, 0:8]), 'yd', 16)
            b.w('g', 'yd', Y_scA)
            LW_swh = b.op('g', lambda E: E.dma_start(
                SWH[:, :], bass.AP(scald, 7, [[0, 128], [1, 1]])), 'ld0', 16)
            b.w('v', 'ld0', LW_swh)
            V_wq = b.op('v', lambda E: E.tensor_scalar(
                WQ8[:, :, :].rearrange("p a b -> p (a b)"),
                WFS[:, :, :].rearrange("p a b -> p (a b)"),
                SWH[:, 0:1], None, OP.mult), 'vk')
            b.w('g', 'cc', G_cc)
            G_gm = b.op('g', lambda E: E.dma_start(GMX[:, :], cc_out.ap()), 'gd', 16)
            b.w('v', 'gd', G_gm)
            b.op('v', lambda E: E.tensor_scalar(
                SCL[0:1, 8:9], GMX[0:1, 0:1], 1e-12, None, OP.max), 'vk')
            b.w('s', 'vk', b.cnt['vk'])
            b.op('s', lambda E: E.activation(SCL[0:1, 9:10], SCL[0:1, 8:9], AF.Copy), 'sk')
            b.w('v', 'sk', b.cnt['sk'])
            b.op('v', lambda E: E.reciprocal(SCL[0:1, 10:11], SCL[0:1, 9:10]), 'vk')
            b.w('s', 'vk', b.cnt['vk'])
            b.op('s', lambda E: E.activation(SCL[0:1, 11:12], SCL[0:1, 10:11], AF.Copy), 'sk')
            b.w('v', 'sk', b.cnt['sk'])
            b.op('v', lambda E: E.tensor_scalar(
                SCL[0:1, 12:13], SCL[0:1, 11:12], 448.0, 448.0, OP.mult, OP.min), 'vk')
            b.w('s', 'vk', b.cnt['vk'])
            b.op('s', lambda E: E.activation(
                SCL[0:1, 13:14], SCL[0:1, 12:13], AF.Copy, scale=0.5), 'sk')
            b.w('v', 'sk', b.cnt['sk'])
            b.op('v', lambda E: E.tensor_mul(
                SCL[0:1, 14:15], SCL[0:1, 12:13], SCL[0:1, 6:7]), 'vk')
            b.w('s', 'vk', b.cnt['vk'])
            b.op('s', lambda E: E.activation(SCL[0:1, 15:16], SCL[0:1, 14:15], AF.Copy), 'sk')
            b.w('v', 'sk', b.cnt['sk'])
            V_sc = b.op('v', lambda E: E.reciprocal(SCL[0:1, 17:18], SCL[0:1, 15:16]), 'vk')
            b.w('s', 'vk', V_sc)
            S_sc = b.op('s', lambda E: E.activation(
                SCL[0:1, 18:19], SCL[0:1, 17:18], AF.Copy, scale=4.0), 'sk')
            b.w('y', 'sk', S_sc)
            Y_sc = b.op('y', lambda E: E.dma_start(
                bass.AP(scald, 8, [[16, 1], [1, 16]]), SCL[0:1, 8:24]), 'yd', 16)
            b.w('g', 'yd', Y_sc)
            LW_sxh = b.op('g', lambda E: E.dma_start(
                SXH[:, :], bass.AP(scald, 13, [[0, 128], [1, 1]])), 'ld1', 16)
            LW_scc = b.op('g', lambda E: E.dma_start(
                SCC[:, :], bass.AP(scald, 18, [[0, 128], [1, 1]])), 'ld2', 16)

            T_t3 = {}
            V_q8 = {}
            T_ff = {}
            S_so = {}
            G_u = {}
            V_o = {}
            S_hf2 = {}
            V_qtk = {}
            T_t3last = {}
            Y_out = {}
            pend3 = [None]
            for j in range(nt):
                b.w('s', 'vk', V_wq)
                b.w('s', 'vk', V_qtk.get(j - 2, 0))
                S_hf2[j] = b.op('s', lambda E, j=j: E.activation(
                    HFP[:, j % 2, :], s2t(j), AF.Identity,
                    scale=RSL[:, j:j + 1], bias=BLN[:, j:j + 1]), 'sk')
                if dbg:
                    b.w('y', 'sk', S_hf2[j])
                    Ydh = b.op('y', lambda E, j=j: E.dma_start(
                        DHF.ap()[j * 128:(j + 1) * 128, :], HFP[:, j % 2, :]),
                        'yd', 16)
                    b.w('s', 'yd', Ydh)
                b.w('v', 'sk', S_hf2[j])
                b.w('v', 'ld1', LW_sxh)
                b.w('v', 'tk', T_t3last.get(j - 2, 0))
                V_qtk[j] = b.op('v', lambda E, j=j: E.tensor_scalar(
                    QTK[:, j % 2, :], HFP[:, j % 2, :],
                    SXH[:, 0:1], None, OP.mult), 'vk')
                b.w('t', 'vk', V_qtk[j])
                b.w('t', 'gd', G_id8)
                b.w('t', 'vk', V_q8.get(j - 2, 0))
                for i in range(NC8):
                    T_t3[j] = b.op('t', lambda E, j=j, i=i: E.matmul(
                        PT3[:, j % 2, i * 128:(i + 1) * 128],
                        QTK[:, j % 2, i * 128:(i + 1) * 128], IDF8[:, :],
                        start=True, stop=True),
                        'tk' if i == NC8 - 1 else None)
                T_t3last[j] = T_t3[j]
                b.w('v', 'tk', T_t3[j])
                b.w('v', 'tk', T_ff.get(j - 2, 0))
                V_q8[j] = b.op('v', lambda E, j=j: E.tensor_copy(
                    QT[:, j % 2, :, :].rearrange("p a b -> p (a b)"),
                    PT3[:, j % 2, :]), 'vk')
                b.w('t', 'vk', V_q8[j])
                b.w('t', 'sk', S_so.get(j - 2, 0))
                for hf2 in range(2):
                    for ci in range(NC8):
                        T_ff[j] = b.op('t', lambda E, j=j, hf2=hf2, ci=ci: E.matmul(
                            PAC[:, j % 2, hf2 * 512:(hf2 + 1) * 512],
                            QT[:, j % 2, ci, :],
                            WQ8[:, ci, hf2 * 512:(hf2 + 1) * 512],
                            start=(ci == 0), stop=(ci == NC8 - 1)),
                            'tk' if (hf2 == 1 and ci == NC8 - 1) else None)

                def back(j=j, T_ffj=T_ff[j]):
                    b.w('s', 'tk', T_ffj)
                    b.w('s', 'ld2', LW_scc)
                    b.w('s', 'gk', G_u.get(j - 2, 0))
                    S_so[j] = b.op('s', lambda E, j=j: E.activation(
                        OF3[:, j % 2, :], PAC[:, j % 2, :], AF.Copy,
                        scale=SCC[:, 0:1]), 'sk')
                    b.w('g', 'sk', S_so[j])
                    b.w('g', 'gd', G_fb)
                    b.w('g', 'vk', V_o.get(j - 2, 0))
                    G_u[j] = b.op('g', lambda E, j=j: E.tensor_add(
                        UBB[:, j % 2, :], OF3[:, j % 2, :], FBB[:, :]), 'gk')
                    b.w('v', 'gk', G_u[j])
                    b.w('v', 'yo', Y_out.get(j - 2, 0))
                    V_o[j] = b.op('v', lambda E, j=j: E.tensor_add(
                        OB[:, j % 2, :], s2t(j), UBB[:, j % 2, :]), 'vk')
                    b.w('y', 'vk', V_o[j])
                    Y_out[j] = b.op('y', lambda E, j=j: E.dma_start(
                        OUT[j * 128:(j + 1) * 128, :], OB[:, j % 2, :]), 'yo', 16)

                if pend3[0] is not None:
                    pend3[0]()
                pend3[0] = back
            pend3[0]()
            b.w('y', 'yd', b.cnt['yd'])
            b.w('y', 'yo', b.cnt['yo'])
            b.emit_block()

    return nc


_BF = ml_dtypes.bfloat16


def _consts():
    tri = np.triu(np.ones((128, 128), dtype=np.float32))
    ident = np.eye(128, dtype=np.float32)
    ones = np.ones((1, 128), dtype=np.float32)
    return tri, ident, ones


def kernel(L0_attn_x, L0_attn_time_decay, L0_attn_time_first, L0_attn_time_shift,
           L0_attn_Wk, L0_attn_Wv, L0_attn_Wr, L0_attn_Wck, L0_attn_Wcv,
           L0_attn_Wcr, L0_attn_ln1_w, L0_attn_ln1_b, L0_attn_ln2_w,
           L0_attn_ln2_b, L0_ffn_weight, L0_ffn_bias):
    x = np.ascontiguousarray(np.asarray(L0_attn_x, dtype=np.float32))
    cA = np.ascontiguousarray
    tri, ident, ones = _consts()
    common = {
        "WKT": cA(np.asarray(L0_attn_Wk, np.float32).T),
        "WVT": cA(np.asarray(L0_attn_Wv, np.float32).T),
        "WRT": cA(np.asarray(L0_attn_Wr, np.float32).T),
        "WCKR": cA(np.asarray(L0_attn_Wck, np.float32).T.reshape(
            NC8, 128, NHC, 128).transpose(2, 1, 0, 3).reshape(NHC, 128, NC8 * 128)),
        "WCVT": cA(np.asarray(L0_attn_Wcv, np.float32).T),
        "WCRT": cA(np.asarray(L0_attn_Wcr, np.float32).T),
        "WFT": cA(np.asarray(L0_ffn_weight, np.float32).T),
        "TSV": cA(np.asarray(L0_attn_time_shift, np.float32).reshape(C)),
        "TFV": cA(np.asarray(L0_attn_time_first, np.float32).reshape(C)),
        "L1W": cA(np.asarray(L0_attn_ln1_w, np.float32)),
        "L1B": cA(np.asarray(L0_attn_ln1_b, np.float32)),
        "L2W": cA(np.asarray(L0_attn_ln2_w, np.float32)),
        "L2B": cA(np.asarray(L0_attn_ln2_b, np.float32)),
        "FBV": cA(np.asarray(L0_ffn_bias, np.float32)),
        "TRID": tri, "IDD": ident, "ONED": ones,
        "CONSTD": np.array(CONST_VALS, dtype=np.float32),
    }
    nc = build(nt=16, dbg=False)
    in_maps = [dict(common, X=cA(x[bb])) for bb in range(8)]
    res = run_bass_kernel_spmd(nc, in_maps, core_ids=list(range(8)))
    out = np.stack([np.asarray(r["OUT"]) for r in res.results], axis=0)
    return out.astype(np.float32)
